# version 13
# baseline (speedup 1.0000x reference)
import contextlib
import numpy as np
import concourse.bass as bass
import concourse.mybir as mybir
from concourse.bass_utils import run_bass_kernel_spmd

F32 = mybir.dt.float32
BF16 = mybir.dt.bfloat16
ALU = mybir.AluOpType
AF = mybir.ActivationFunctionType
AX = mybir.AxisListType

S = 2048
D = 2048
NSC = 128
KST = 16
H = 8192
TS = 512
NTS = S // TS
DC = D // 128
HC = H // 128
EPS = 1e-5
QB = 127.0
STAGES = ("A0", "A1", "A2", "A3", "A", "P", "F15", "Q2", "F2")


class Sem:
    def __init__(self, stack, nc, name):
        self.h = stack.enter_context(nc.semaphore(name))
        self.n = 0

    def inc(self, instr, amt=1):
        instr.then_inc(self.h, amt)
        self.n += amt
        return self.n

    def tok(self):
        return (self.h, self.n)


class DRing:

    def __init__(self, stack, nc, name, slots):
        self.sems = [Sem(stack, nc, f"{name}{i}") for i in range(slots)]
        self.slots = slots

    def inc(self, idx, instr, amt=16):
        s = self.sems[idx % self.slots]
        s.inc(instr, amt)
        return (s.h, s.n)


def emit_peel(v, absbuf, tmpbuf, levels=7):
    for lvl in range(levels):
        hh = float(2 ** (levels - 1 - lvl))
        v.tensor_scalar(out=tmpbuf, in0=absbuf, scalar1=hh - 0.5, scalar2=hh,
                        op0=ALU.is_ge, op1=ALU.mult)
        v.tensor_tensor(out=absbuf, in0=absbuf, in1=tmpbuf, op=ALU.subtract)


def rep_bcast(t, c, w):
    return t.ap().rearrange("p (c w) -> p c w", c=1).broadcast_to((128, c, w))


class SyncEng:
    _PASS = ("wait_ge", "dma_start", "dma_start_transpose", "nop")

    def __init__(self, eng, sem):
        object.__setattr__(self, "_e", eng)
        object.__setattr__(self, "_s", sem)

    def __getattr__(self, name):
        f = getattr(self._e, name)
        if name in SyncEng._PASS or not callable(f):
            return f
        s = self._s
        e = self._e

        def call(*a, **kw):
            r = f(*a, **kw)
            s.inc(r)
            e.wait_ge(s.h, s.n)
            return r
        return call


def build_program(upto="F2", dbg=()):
    stage_n = STAGES.index(upto) - 4
    nc = bass.Bass("TRN2", target_bir_lowering=False, debug=False)
    st = contextlib.ExitStack()
    dbg = set(dbg)
    dbg_tensors = {}

    xT = nc.dram_tensor("xT", [D, S], F32, kind="ExternalInput").ap()
    wiT = nc.dram_tensor("wiT", [D, NSC], F32, kind="ExternalInput").ap()
    wgT = nc.dram_tensor("wgT", [D, NSC], F32, kind="ExternalInput").ap()
    wdtT = nc.dram_tensor("wdtT", [NSC, NSC], F32, kind="ExternalInput").ap()
    wbrep = nc.dram_tensor("wbrep", [KST, NSC, 128], F32, kind="ExternalInput").ap()
    wcrep = nc.dram_tensor("wcrep", [KST, NSC, 128], F32, kind="ExternalInput").ap()
    woT = nc.dram_tensor("woT", [NSC, D], F32, kind="ExternalInput").ap()
    alog = nc.dram_tensor("alog", [NSC, KST], F32, kind="ExternalInput").ap()
    bdt = nc.dram_tensor("bdt", [NSC, 1], F32, kind="ExternalInput").ap()
    wnorm = nc.dram_tensor("wnorm", [128, DC], F32, kind="ExternalInput").ap()
    wgvt = nc.dram_tensor("wgvt", [HC, 2, 128, DC * 128], F32, kind="ExternalInput").ap()
    woutT = nc.dram_tensor("woutT", [2, HC, 128, 8 * 128], F32, kind="ExternalInput").ap()
    outT = nc.dram_tensor("outT", [DC, 128, S], F32, kind="ExternalOutput").ap()

    u_sp = nc.dram_tensor("u_sp", [HC, 128, S], F32, kind="Internal").ap()
    ln1_sp = nc.dram_tensor("ln1_sp", [DC, 128, S], F32, kind="Internal").ap()
    qo_sp = nc.dram_tensor("qo_sp", [2, HC // 4, 128, 4 * 1024], BF16, kind="Internal").ap()
    q2_sp = nc.dram_tensor("q2_sp", [NTS, HC // 4, 128, 4 * TS], BF16, kind="Internal").ap()

    sLG = Sem(st, nc, "sLG")
    sSm = Sem(st, nc, "sSm")
    sMM = Sem(st, nc, "sMM")
    sV = Sem(st, nc, "sV")
    sA = Sem(st, nc, "sA")
    rW0 = DRing(st, nc, "rW0", 6)
    rPX = DRing(st, nc, "rPX", 2)
    rWLD = DRing(st, nc, "rWLD", 3)
    rWO = DRing(st, nc, "rWO", 2)
    rUB = DRing(st, nc, "rUB", 2)
    rF2 = DRing(st, nc, "rF2", 2)
    rQ2R = DRing(st, nc, "rQ2R", 16)
    rQ2S = DRing(st, nc, "rQ2S", 2)
    rUSP = DRing(st, nc, "rUSP", 2)
    sLn1 = Sem(st, nc, "sLn1")
    sLb = Sem(st, nc, "sLb")
    rQoS = DRing(st, nc, "rQoS", 2)
    sOut = Sem(st, nc, "sOut")

    sb = st.enter_context
    ones1 = sb(nc.sbuf_tensor("ones1", [1, 128], F32))
    onesc = sb(nc.sbuf_tensor("onesc", [128, 1], F32))
    onescb = sb(nc.sbuf_tensor("onescb", [128, 1], BF16))
    bdt_s = sb(nc.sbuf_tensor("bdt_s", [NSC, 1], F32))
    nega = sb(nc.sbuf_tensor("nega", [NSC, KST], F32))
    wn_s = sb(nc.sbuf_tensor("wn_s", [128, DC], F32))
    sgv128 = sb(nc.sbuf_tensor("sgv128", [128, 1], F32))
    sout128 = sb(nc.sbuf_tensor("sout128", [128, 1], F32))
    mwc1 = sb(nc.sbuf_tensor("mwc1", [1, 1], F32))
    mwc2 = sb(nc.sbuf_tensor("mwc2", [1, 1], F32))
    accW = sb(nc.sbuf_tensor("accW", [128, 256], F32))
    scr1 = sb(nc.sbuf_tensor("scr1", [1, TS], F32))
    scr2 = sb(nc.sbuf_tensor("scr2", [1, TS], F32))
    scr4 = sb(nc.sbuf_tensor("scr4", [1, TS], F32))
    epsc = sb(nc.sbuf_tensor("epsc", [1, 1], F32))

    ybf = sb(nc.sbuf_tensor("ybf", [NSC, S], BF16))
    c2b = sb(nc.sbuf_tensor("c2b", [128, S], F32))
    s2b = sb(nc.sbuf_tensor("s2b", [128, S], F32))
    pbh = sb(nc.sbuf_tensor("pbh", [128, 1], F32))
    pbm = sb(nc.sbuf_tensor("pbm", [128, 1], F32))
    stPF = contextlib.ExitStack()

    blk = st.enter_context(nc.Block())

    def VEC(f):
        blk.vector(lambda v: f(SyncEng(v, sV)))

    def SCA(f):
        blk.scalar(lambda sc: f(SyncEng(sc, sA)))

    def emit_taps(names):
        taps = []
        for nm in names:
            if nm not in dbg:
                continue
            hnd, shape, dt = dbg_tensors[nm]
            ap = nc.dram_tensor("dbg_" + nm, shape, F32, kind="ExternalOutput").ap()
            taps.append((hnd, ap))
        vv, aa, mm = sV.n, sA.n, sMM.n

        @blk.gpsimd
        def _(g):
            g.wait_ge(sV.h, vv)
            g.wait_ge(sA.h, aa)
            g.wait_ge(sMM.h, mm)
            for hnd, ap in taps:
                sLG.inc(g.dma_start(out=ap, in_=hnd[:]), 16)
            g.wait_ge(sLG.h, sLG.n)

    ast = contextlib.ExitStack()
    asb = ast.enter_context
    wdt_s = asb(nc.sbuf_tensor("wdt_s", [NSC, NSC], BF16))
    wb_s = asb(nc.sbuf_tensor("wb_s", [NSC, KST * 128], BF16))
    wc_s = asb(nc.sbuf_tensor("wc_s", [NSC, KST * 128], BF16))
    wo_s = asb(nc.sbuf_tensor("wo_s", [NSC, D], BF16))
    alog_s = asb(nc.sbuf_tensor("alog_s", [NSC, KST], F32))
    zf = asb(nc.sbuf_tensor("zf", [NSC, S], F32))
    zbf = asb(nc.sbuf_tensor("zbf", [NSC, S], BF16))
    gsil = asb(nc.sbuf_tensor("gsil", [NSC, S], F32))
    dt_s = asb(nc.sbuf_tensor("dt_s", [NSC, S], F32))
    w_s = asb(nc.sbuf_tensor("w_s", [NSC, S], F32))
    ak = asb(nc.sbuf_tensor("ak", [NSC, S], F32))
    uk = asb(nc.sbuf_tensor("uk", [NSC, S], F32))
    stk = asb(nc.sbuf_tensor("stk", [NSC, S], F32))
    yacc = asb(nc.sbuf_tensor("yacc", [NSC, S], F32))
    ptmp = asb(nc.sbuf_tensor("ptmp", [NSC, S], F32))
    sp1 = asb(nc.sbuf_tensor("sp1", [NSC, S], F32))
    sp2 = asb(nc.sbuf_tensor("sp2", [NSC, S], F32))
    ast0 = contextlib.ExitStack()
    xbf = ast0.enter_context(nc.sbuf_tensor("xbf", [128, DC * S], BF16))
    wi_s = ast0.enter_context(nc.sbuf_tensor("wi_s", [128, DC * NSC], BF16))
    wg_s = ast0.enter_context(nc.sbuf_tensor("wg_s", [128, DC * NSC], BF16))

    a1 = contextlib.ExitStack()
    pz = a1.enter_context(nc.psum_tensor("pz", [NSC, S], F32))
    pg = a1.enter_context(nc.psum_tensor("pg", [NSC, S], F32))

    @blk.gpsimd
    def _(g):
        for k in range(DC):
            sLG.inc(g.dma_start(out=xbf[:, k * S:(k + 1) * S],
                                in_=xT[k * 128:(k + 1) * 128, :]), 16)
        for k in range(DC):
            sLG.inc(g.dma_start(out=wi_s[:, k * NSC:(k + 1) * NSC],
                                in_=wiT[k * 128:(k + 1) * 128, :]), 16)
        for k in range(DC):
            sLG.inc(g.dma_start(out=wg_s[:, k * NSC:(k + 1) * NSC],
                                in_=wgT[k * 128:(k + 1) * 128, :]), 16)
        sLG.inc(g.dma_start(out=wdt_s[:], in_=wdtT[:]), 16)
        for k in range(KST):
            sLG.inc(g.dma_start(out=wb_s[:, k * 128:(k + 1) * 128], in_=wbrep[k]), 16)
        for k in range(KST):
            sLG.inc(g.dma_start(out=wc_s[:, k * 128:(k + 1) * 128], in_=wcrep[k]), 16)
        sLG.inc(g.dma_start(out=wo_s[:], in_=woT[:]), 16)
    tok_gload = sLG.n

    @blk.sync
    def _(s):
        sSm.inc(s.dma_start(out=alog_s[:], in_=alog[:]), 16)
        sSm.inc(s.dma_start(out=bdt_s[:], in_=bdt[:]), 16)
        sSm.inc(s.dma_start(out=wn_s[:], in_=wnorm[:]), 16)
    tok_sload = sSm.n

    @VEC
    def _(v):
        v.memset(ones1[:], 1.0)
        v.memset(onesc[:], 1.0)
        v.memset(onescb[:], 1.0)
        v.memset(epsc[:], EPS)
        v.memset(yacc[:], 0.0)
        v.memset(pbh[:], 0.5)
        v.memset(pbm[:], -0.5)
    tok_init = sV.n

    if stage_n == -4:
        emit_taps([])
        ast0.close(); a1.close(); ast.close(); stPF.close(); st.close()
        return nc, []

    @blk.tensor
    def _(t):
        t.wait_ge(sLG.h, tok_gload)
        for ts in range(NTS):
            for k in range(DC):
                last = t.matmul(pz[:, ts * TS:(ts + 1) * TS],
                                wi_s[:, k * NSC:(k + 1) * NSC],
                                xbf[:, k * S + ts * TS:k * S + (ts + 1) * TS],
                                start=(k == 0), stop=(k == DC - 1))
            sMM.inc(last)
        for ts in range(NTS):
            for k in range(DC):
                last = t.matmul(pg[:, ts * TS:(ts + 1) * TS],
                                wg_s[:, k * NSC:(k + 1) * NSC],
                                xbf[:, k * S + ts * TS:k * S + (ts + 1) * TS],
                                start=(k == 0), stop=(k == DC - 1))
            sMM.inc(last)
    tok_zg = sMM.n

    vbase_zf = sV.n

    @VEC
    def _(v):
        for ts in range(NTS):
            v.wait_ge(sMM.h, tok_zg - 8 + ts + 1)
            v.tensor_copy(out=zf[:, ts * TS:(ts + 1) * TS],
                          in_=pz[:, ts * TS:(ts + 1) * TS])
    tok_zf = sV.n

    @SCA
    def _(sc):
        for ts in range(NTS):
            sc.wait_ge(sV.h, vbase_zf + ts + 1)
            sc.copy(out=zbf[:, ts * TS:(ts + 1) * TS],
                    in_=pz[:, ts * TS:(ts + 1) * TS])
        for ts in range(NTS):
            sc.wait_ge(sMM.h, tok_zg - 4 + ts + 1)
            sc.activation(out=gsil[:, ts * TS:(ts + 1) * TS],
                                 in_=pg[:, ts * TS:(ts + 1) * TS], func=AF.Silu)
        sc.wait_ge(sSm.h, tok_sload)
        sc.activation(out=nega[:], in_=alog_s[:], func=AF.Exp)
    tok_aevac = sA.n

    @VEC
    def _(v):
        v.wait_ge(sA.h, tok_aevac)
        v.tensor_scalar_mul(out=nega[:], in0=nega[:], scalar1=-1.0)
    tok_nega = sV.n

    if stage_n == -3:
        emit_taps(["zf", "gsil"]) if False else None
        dbg_tensors.update(dict(zf=(zf, [NSC, S], F32), gsil=(gsil, [NSC, S], F32)))
        emit_taps(["zf", "gsil"])
        ast0.close(); a1.close(); ast.close(); stPF.close(); st.close()
        return nc, sorted(dbg_tensors.keys() & dbg)

    a1.close()
    a2 = contextlib.ExitStack()
    pdt = a2.enter_context(nc.psum_tensor("pdt", [NSC, S], F32))

    @blk.tensor
    def _(t):
        t.wait_ge(sA.h, tok_aevac)
        t.wait_ge(sV.h, tok_zf)
        for ts in range(NTS):
            last = t.matmul(pdt[:, ts * TS:(ts + 1) * TS], wdt_s[:],
                            zbf[:, ts * TS:(ts + 1) * TS], start=True, stop=True)
        sMM.inc(last)
    tok_dtmm = sMM.n

    @SCA
    def _(sc):
        sc.wait_ge(sMM.h, tok_dtmm)
        sc.activation(out=sp1[:], in_=pdt[:], func=AF.Relu, bias=bdt_s[:])
        sc.activation(out=sp2[:], in_=pdt[:], func=AF.Abs, bias=bdt_s[:])
        sc.wait_ge(sA.h, sA.n)
        sc.activation(out=sp2[:], in_=sp2[:], func=AF.Exp, scale=-1.0)
        sc.wait_ge(sA.h, sA.n)
        sc.activation(out=sp2[:], in_=sp2[:], func=AF.Ln, bias=1.0)
    tok_sp = sA.n

    @VEC
    def _(v):
        v.wait_ge(sA.h, tok_sp)
        v.tensor_add(out=dt_s[:], in0=sp1[:], in1=sp2[:])
        v.tensor_mul(out=w_s[:], in0=dt_s[:], in1=zf[:])
    tok_dt = sV.n

    if stage_n == -2:
        dbg_tensors.update(dict(dt_s=(dt_s, [NSC, S], F32)))
        emit_taps(["dt_s"])
        a2.close(); ast0.close(); ast.close(); stPF.close(); st.close()
        return nc, sorted(dbg_tensors.keys() & dbg)

    a2.close()
    a3 = contextlib.ExitStack()
    pb = a3.enter_context(nc.psum_tensor("pb", [NSC, S], F32))
    pc = a3.enter_context(nc.psum_tensor("pc", [NSC, S], F32))

    vbase_k = sV.n

    def _kcum(k):
        return 0 if k == 0 else 4 + 5 * (k - 1)

    def tok_uk(k):
        return vbase_k + _kcum(k) + 2

    def tok_pk(k):
        return vbase_k + _kcum(k) + (4 if k == 0 else 5)

    mbase_k = sMM.n

    def tok_bmm(k):
        return mbase_k + 2 * k + 1

    def tok_cmm(k):
        return mbase_k + 2 * k + 2

    @blk.tensor
    def _(t):
        t.wait_ge(sV.h, tok_dt)
        for k in range(KST):
            if k > 0:
                t.wait_ge(sV.h, tok_uk(k - 1))
            for ts in range(NTS):
                last = t.matmul(pb[:, ts * TS:(ts + 1) * TS],
                                wb_s[:, k * 128:(k + 1) * 128],
                                zbf[:, ts * TS:(ts + 1) * TS], start=True, stop=True)
            sMM.inc(last)
            if k > 0:
                t.wait_ge(sV.h, tok_pk(k - 1))
            for ts in range(NTS):
                last = t.matmul(pc[:, ts * TS:(ts + 1) * TS],
                                wc_s[:, k * 128:(k + 1) * 128],
                                zbf[:, ts * TS:(ts + 1) * TS], start=True, stop=True)
            sMM.inc(last)

    @VEC
    def _(v):
        v.wait_ge(sV.h, tok_nega)
        for k in range(KST):
            v.wait_ge(sMM.h, tok_bmm(k))
            v.tensor_scalar(out=ak[:], in0=dt_s[:], scalar1=nega[:, k:k + 1],
                            scalar2=1.0, op0=ALU.mult, op1=ALU.add)
            v.tensor_mul(out=uk[:], in0=w_s[:], in1=pb[:])
            v.tensor_tensor_scan(out=stk[:], data0=ak[:], data1=uk[:],
                                 initial=0.0, op0=ALU.mult, op1=ALU.add)
            v.wait_ge(sMM.h, tok_cmm(k))
            if k == 0:
                v.tensor_mul(out=yacc[:], in0=stk[:], in1=pc[:])
            else:
                v.tensor_mul(out=ptmp[:], in0=stk[:], in1=pc[:])
                v.tensor_add(out=yacc[:], in0=yacc[:], in1=ptmp[:])
        v.tensor_mul(out=ybf[:], in0=yacc[:], in1=gsil[:])
    tok_ybf = sV.n
    assert sV.n == vbase_k + _kcum(KST) + 1, (sV.n, vbase_k, _kcum(KST))

    if stage_n == -1:
        dbg_tensors.update(dict(ybf=(ybf, [NSC, S], BF16)))
        emit_taps(["ybf"])
        a3.close(); ast0.close(); ast.close(); stPF.close(); st.close()
        return nc, sorted(dbg_tensors.keys() & dbg)

    ast0.close()
    aw = contextlib.ExitStack()
    wmean = aw.enter_context(nc.sbuf_tensor("wmean", [128, 6 * 2048], F32))
    wabs = aw.enter_context(nc.sbuf_tensor("wabs", [128, 2048], F32))

    w0_tiles = []
    for j in range(HC):
        for vg in range(2):
            w0_tiles.append((wgvt[j, vg], 2048))
    for dg in range(2):
        for j in range(HC):
            w0_tiles.append((woutT[dg, j], 1024))

    abase_w0 = sA.n
    vbase_w0 = sV.n

    @SCA
    def _(sc):
        for i, (_src, width) in enumerate(w0_tiles):
            if i % 2:
                continue
            e = i // 2
            sl = 2 * (e % 3)
            sc.wait_ge(rW0.sems[sl].h, 16 * (e // 3 + 1))
            sc.activation(out=wabs[:, 0:width],
                          in_=wmean[:, sl * 2048:sl * 2048 + width],
                          func=AF.Abs, accum_out=accW[:, i:i + 1])
    tok_w0acc = sA.n

    @VEC
    def _(v):
        for i, (_src, width) in enumerate(w0_tiles):
            if i % 2 == 0:
                continue
            o = i // 2
            sl = 2 * (o % 3) + 1
            v.wait_ge(rW0.sems[sl].h, 16 * (o // 3 + 1))
            v.tensor_reduce(out=accW[:, i:i + 1],
                            in_=wmean[:, sl * 2048:sl * 2048 + width],
                            axis=AX.X, op=ALU.add, apply_absolute_value=True)
    tok_w0accv = sV.n

    @blk.sync
    def _(s):
        s.wait_ge(sMM.h, tok_zg)
        for i, (src, width) in enumerate(w0_tiles):
            if i % 2:
                continue
            e = i // 2
            sl = 2 * (e % 3)
            if e >= 3:
                s.wait_ge(sA.h, abase_w0 + (e - 3) + 1)
            rW0.inc(sl, s.dma_start(out=wmean[:, sl * 2048:sl * 2048 + width],
                                    in_=src))

    @blk.gpsimd
    def _(g):
        g.wait_ge(sMM.h, tok_zg)
        for i, (src, width) in enumerate(w0_tiles):
            if i % 2 == 0:
                continue
            o = i // 2
            sl = 2 * (o % 3) + 1
            if o >= 3:
                g.wait_ge(sV.h, vbase_w0 + (o - 3) + 1)
            rW0.inc(sl, g.dma_start(out=wmean[:, sl * 2048:sl * 2048 + width],
                                    in_=src))

    a3.close()
    a4 = contextlib.ExitStack()
    pw0 = a4.enter_context(nc.psum_tensor("pw0", [1, 2], F32))
    pw1 = a4.enter_context(nc.psum_tensor("pw1", [128, 2], F32))
    redW = a4.enter_context(nc.sbuf_tensor("redW", [128, 2], F32))
    rowW = a4.enter_context(nc.sbuf_tensor("rowW", [1, 2], F32))

    @VEC
    def _(v):
        v.wait_ge(sA.h, tok_w0acc)
        v.tensor_reduce(out=redW[:, 0:1], in_=accW[:, 0:128], axis=AX.X, op=ALU.add)
        v.tensor_reduce(out=redW[:, 1:2], in_=accW[:, 128:256], axis=AX.X,
                               op=ALU.add)
    tok_redW = sV.n

    @blk.tensor
    def _(t):
        t.wait_ge(sV.h, tok_redW)
        t.matmul(pw0[:, 0:1], onesc[:], redW[:, 0:1], start=True, stop=True)
        sMM.inc(t.matmul(pw0[:, 1:2], onesc[:], redW[:, 1:2], start=True, stop=True))
    tok_w0mm = sMM.n

    @VEC
    def _(v):
        v.wait_ge(sMM.h, tok_w0mm)
        v.tensor_scalar(out=rowW[:, 0:1], in0=pw0[:, 0:1], scalar1=1.0 / (2 * H * D),
                        scalar2=EPS, op0=ALU.mult, op1=ALU.max)
        v.tensor_scalar(out=rowW[:, 1:2], in0=pw0[:, 1:2], scalar1=1.0 / (D * H),
                        scalar2=EPS, op0=ALU.mult, op1=ALU.max)
        v.tensor_scalar_mul(out=mwc1[:], in0=rowW[:, 0:1], scalar1=1.0 / QB)
        v.tensor_scalar_mul(out=mwc2[:], in0=rowW[:, 1:2], scalar1=1.0 / QB)
        v.reciprocal(out=rowW[:], in_=rowW[:])
    tok_w0row = sV.n

    @blk.tensor
    def _(t):
        t.wait_ge(sV.h, tok_w0row)
        sMM.inc(t.matmul(pw1[:], ones1[:], rowW[:], start=True, stop=True))
    tok_w0bc = sMM.n

    @SCA
    def _(sc):
        sc.wait_ge(sMM.h, tok_w0bc)
        sc.copy(out=sgv128[:], in_=pw1[:, 0:1])
        sc.copy(out=sout128[:], in_=pw1[:, 1:2])
    tok_w0done = sA.n
    a4.close()

    dbg_tensors.update(dict(
        zf=(zf, [NSC, S], F32), gsil=(gsil, [NSC, S], F32),
        dt_s=(dt_s, [NSC, S], F32), yacc=(yacc, [NSC, S], F32),
        ybf=(ybf, [NSC, S], BF16), sgv128=(sgv128, [128, 1], F32),
        mwc1=(mwc1, [1, 1], F32), mwc2=(mwc2, [1, 1], F32),
    ))
    if stage_n == 0:
        emit_taps(["zf", "gsil", "dt_s", "yacc", "ybf", "sgv128", "mwc1", "mwc2"])
        aw.close(); ast0.close() if False else None; ast.close(); stPF.close()
        st.close()
        return nc, sorted(dbg_tensors.keys() & dbg)

    aw.close()
    ast.close()

    q1 = stPF.enter_context(nc.sbuf_tensor("q1", [128, NTS * DC * TS], BF16))
    c1b = stPF.enter_context(nc.sbuf_tensor("c1b", [128, S], F32))
    pst = contextlib.ExitStack()
    psb = pst.enter_context
    wo2_s = psb(nc.sbuf_tensor("wo2_s", [NSC, D], BF16))
    xr = psb(nc.sbuf_tensor("xr", [128, 2 * TS], F32))
    res1 = psb(nc.sbuf_tensor("res1", [128, DC * TS], F32))
    hbuf = psb(nc.sbuf_tensor("hbuf", [128, DC * TS], F32))
    paby = psb(nc.sbuf_tensor("paby", [128, DC * TS // 2], F32))
    sgn_p = psb(nc.sbuf_tensor("sgn_p", [128, DC * TS // 2], BF16))
    sqb = psb(nc.sbuf_tensor("sqb", [128, 2 * TS], BF16))
    m1t = psb(nc.sbuf_tensor("m1t", [128, TS], F32))
    trt = psb(nc.sbuf_tensor("trt", [64, TS], F32))
    mu_b = psb(nc.sbuf_tensor("mu_b", [128, TS], F32))
    rstd_b = psb(nc.sbuf_tensor("rstd_b", [128, TS], F32))
    ppst = contextlib.ExitStack()
    pwo = ppst.enter_context(nc.psum_tensor("pwo", [128, 2 * TS], F32))
    pstat = ppst.enter_context(nc.psum_tensor("pstat", [1, 2 * TS], F32))
    pbc = ppst.enter_context(nc.psum_tensor("pbc", [128, TS], F32))

    @blk.gpsimd
    def _(g):
        g.wait_ge(sA.h, tok_w0done)
        sLG.inc(g.dma_start(out=wo2_s[:], in_=woT[:]), 16)
    tok_wo2 = sLG.n

    pA_v, pA_a, pA_m = sV.n, sA.n, sMM.n
    prev_q1ts = None
    prev_xadd_end = None
    for ts in range(NTS):
        abase_evac = sA.n
        @blk.tensor
        def _(t, ts=ts, abase_evac=abase_evac):
            t.wait_ge(sLG.h, tok_wo2)
            t.wait_ge(sV.h, tok_ybf)
            for d in range(DC):
                if d >= 2:
                    t.wait_ge(sA.h, abase_evac + (d - 1))
                sMM.inc(t.matmul(pwo[:, (d % 2) * TS:(d % 2 + 1) * TS],
                                 wo2_s[:, d * 128:(d + 1) * 128],
                                 ybf[:, ts * TS:(ts + 1) * TS], start=True, stop=True))
        tok_womm = sMM.n

        @SCA
        def _(sc, tok_womm=tok_womm, prev_q1ts=prev_q1ts):
            if prev_q1ts is not None:
                sc.wait_ge(sV.h, prev_q1ts)
            for d in range(DC):
                sc.wait_ge(sMM.h, tok_womm - DC + d + 1)
                sc.copy(out=res1[:, d * TS:(d + 1) * TS],
                        in_=pwo[:, (d % 2) * TS:(d % 2 + 1) * TS])
        tok_evac = sA.n

        vbase_x = sV.n

        xr_toks = {}

        @blk.sync
        def _(s, ts=ts, vbase_x=vbase_x, prev_xadd_end=prev_xadd_end):
            if ts == 0:
                s.wait_ge(sV.h, pA_v)
                s.wait_ge(sA.h, pA_a)
                s.wait_ge(sMM.h, pA_m)
            for d in range(DC):
                if d >= 2:
                    s.wait_ge(sV.h, vbase_x + (d - 1))
                elif prev_xadd_end is not None:
                    s.wait_ge(sV.h, prev_xadd_end)
                xr_toks[d] = rPX.inc(d, s.dma_start(
                    out=xr[:, (d % 2) * TS:(d % 2 + 1) * TS],
                    in_=xT[d * 128:(d + 1) * 128, ts * TS:(ts + 1) * TS]))

        @VEC
        def _(v, tok_evac=tok_evac):
            for d in range(DC):
                v.wait_ge(*xr_toks[d])
                v.wait_ge(sA.h, tok_evac - DC + d + 1)
                v.tensor_add(out=res1[:, d * TS:(d + 1) * TS],
                                    in0=res1[:, d * TS:(d + 1) * TS],
                                    in1=xr[:, (d % 2) * TS:(d % 2 + 1) * TS])
        tok_res1 = sV.n
        prev_xadd_end = tok_res1

        abase_sq = sA.n
        mbase_sq = sMM.n
        @SCA
        def _(sc, tok_res1=tok_res1, mbase_sq=mbase_sq):
            sc.wait_ge(sV.h, tok_res1)
            for d in range(DC):
                if d >= 2:
                    sc.wait_ge(sMM.h, mbase_sq + DC + d - 1)
                sc.activation(out=sqb[:, (d % 2) * TS:(d % 2 + 1) * TS],
                                     in_=res1[:, d * TS:(d + 1) * TS], func=AF.Square)
        tok_sq = sA.n

        @blk.tensor
        def _(t, tok_res1=tok_res1, abase_sq=abase_sq):
            t.wait_ge(sV.h, tok_res1)
            for d in range(DC):
                sMM.inc(t.matmul(pstat[:, 0:TS], onesc[:], res1[:, d * TS:(d + 1) * TS],
                                 start=(d == 0), stop=(d == DC - 1)))
            for d in range(DC):
                t.wait_ge(sA.h, abase_sq + d + 1)
                sMM.inc(t.matmul(pstat[:, TS:2 * TS], onescb[:],
                                 sqb[:, (d % 2) * TS:(d % 2 + 1) * TS],
                                 start=(d == 0), stop=(d == DC - 1)))
        tok_st12 = sMM.n

        @VEC
        def _(v, tok_st12=tok_st12):
            v.wait_ge(sMM.h, tok_st12)
            v.tensor_scalar_mul(out=scr1[:], in0=pstat[:, 0:TS], scalar1=1.0 / D)
            v.tensor_scalar_mul(out=scr2[:], in0=pstat[:, TS:2 * TS], scalar1=1.0 / D)
            v.tensor_mul(out=scr4[:], in0=scr1[:], in1=scr1[:])
            v.tensor_sub(out=scr2[:], in0=scr2[:], in1=scr4[:])
        tok_var = sV.n

        @SCA
        def _(sc, tok_var=tok_var):
            sc.wait_ge(sV.h, tok_var)
            sc.activation(out=scr2[:], in_=scr2[:], func=AF.Sqrt, bias=epsc[:])
        tok_sqv = sA.n

        @VEC
        def _(v, tok_sqv=tok_sqv):
            v.wait_ge(sA.h, tok_sqv)
            v.reciprocal(out=scr2[:], in_=scr2[:])
        tok_rstd = sV.n

        @blk.tensor
        def _(t, tok_rstd=tok_rstd):
            t.wait_ge(sV.h, tok_rstd)
            sMM.inc(t.matmul(pbc[:], ones1[:], scr1[:], start=True, stop=True))
        tok_bc1 = sMM.n

        @SCA
        def _(sc, tok_bc1=tok_bc1):
            sc.wait_ge(sMM.h, tok_bc1)
            sc.copy(out=mu_b[:], in_=pbc[:])
        tok_mub = sA.n

        @blk.tensor
        def _(t, tok_mub=tok_mub):
            t.wait_ge(sA.h, tok_mub)
            sMM.inc(t.matmul(pbc[:], ones1[:], scr2[:], start=True, stop=True))
        tok_bc2 = sMM.n

        @SCA
        def _(sc, tok_bc2=tok_bc2):
            sc.wait_ge(sMM.h, tok_bc2)
            sc.copy(out=rstd_b[:], in_=pbc[:])
        tok_rstdb = sA.n

        res1_3 = res1.ap().rearrange("p (c w) -> p c w", c=DC)
        hbuf_3 = hbuf.ap().rearrange("p (c w) -> p c w", c=DC)

        @VEC
        def _(v, tok_rstdb=tok_rstdb):
            v.wait_ge(sA.h, tok_rstdb)
            v.tensor_tensor(out=res1_3, in0=res1_3, in1=rep_bcast(mu_b, DC, TS),
                            op=ALU.subtract)
            v.tensor_tensor(out=res1_3, in0=res1_3,
                                   in1=rep_bcast(rstd_b, DC, TS), op=ALU.mult)
        tok_ln1 = sV.n

        @SCA
        def _(sc, ts=ts, tok_ln1=tok_ln1):
            sc.wait_ge(sV.h, tok_ln1)
            for d in range(DC):
                sLn1.inc(sc.dma_start(out=ln1_sp[d, :, ts * TS:(ts + 1) * TS],
                                      in_=res1[:, d * TS:(d + 1) * TS]), 16)
        tok_ln1sp = sLn1.n

        abase_sq2 = sA.n
        mbase_sq2 = sMM.n
        @SCA
        def _(sc, tok_ln1=tok_ln1, mbase_sq2=mbase_sq2):
            for d in range(DC):
                if d >= 2:
                    sc.wait_ge(sMM.h, mbase_sq2 + d - 1)
                sc.activation(out=sqb[:, (d % 2) * TS:(d % 2 + 1) * TS],
                                     in_=res1[:, d * TS:(d + 1) * TS], func=AF.Square)
        tok_sq2 = sA.n

        @blk.tensor
        def _(t, abase_sq2=abase_sq2):
            for d in range(DC):
                t.wait_ge(sA.h, abase_sq2 + d + 1)
                sMM.inc(t.matmul(pstat[:, TS:2 * TS], onescb[:],
                                 sqb[:, (d % 2) * TS:(d % 2 + 1) * TS],
                                 start=(d == 0), stop=(d == DC - 1)))
        tok_ms = sMM.n

        @VEC
        def _(v, tok_ms=tok_ms):
            v.wait_ge(sMM.h, tok_ms)
            v.tensor_scalar_mul(out=scr2[:], in0=pstat[:, TS:2 * TS],
                                       scalar1=1.0 / D)
        tok_msr = sV.n

        @SCA
        def _(sc, tok_msr=tok_msr):
            sc.wait_ge(sV.h, tok_msr)
            sc.activation(out=scr2[:], in_=scr2[:], func=AF.Sqrt, bias=epsc[:])
        tok_sqr2 = sA.n

        @VEC
        def _(v, tok_sqr2=tok_sqr2):
            v.wait_ge(sA.h, tok_sqr2)
            v.reciprocal(out=scr2[:], in_=scr2[:])
        tok_rms = sV.n

        @blk.tensor
        def _(t, tok_rms=tok_rms):
            t.wait_ge(sV.h, tok_rms)
            sMM.inc(t.matmul(pbc[:], ones1[:], scr2[:], start=True, stop=True))
        tok_bc3 = sMM.n

        @SCA
        def _(sc, tok_bc3=tok_bc3):
            sc.wait_ge(sMM.h, tok_bc3)
            sc.copy(out=rstd_b[:], in_=pbc[:])
        tok_rmsb = sA.n

        @VEC
        def _(v, tok_rmsb=tok_rmsb):
            v.wait_ge(sA.h, tok_rmsb)
            for d in range(DC):
                v.tensor_scalar(out=hbuf[:, d * TS:(d + 1) * TS],
                                in0=res1[:, d * TS:(d + 1) * TS],
                                scalar1=wn_s[:, d:d + 1], scalar2=None, op0=ALU.mult)
            v.tensor_tensor(out=hbuf_3, in0=hbuf_3,
                                   in1=rep_bcast(rstd_b, DC, TS), op=ALU.mult)
        tok_h = sV.n

        abase_habs = sA.n
        vbase_m1 = sV.n

        @SCA
        def _(sc, tok_h=tok_h, vbase_m1=vbase_m1):
            sc.wait_ge(sV.h, tok_h)
            for d in range(DC):
                if d >= 2:
                    sc.wait_ge(sV.h, vbase_m1 + d - 1)
                sc.activation(out=paby[:, (d % 2) * TS:(d % 2 + 1) * TS],
                                     in_=hbuf[:, d * TS:(d + 1) * TS], func=AF.Abs)
        tok_habs = sA.n

        @VEC
        def _(v, abase_habs=abase_habs):
            for d in range(DC):
                v.wait_ge(sA.h, abase_habs + d + 1)
                if d == 0:
                    v.tensor_copy(out=m1t[:], in_=paby[:, 0:TS])
                else:
                    v.tensor_max(out=m1t[:], in0=m1t[:],
                                        in1=paby[:, (d % 2) * TS:(d % 2 + 1) * TS])
            for width in (64, 32):
                v.tensor_copy(out=trt[0:width, :], in_=m1t[width:2 * width, :])
                v.tensor_max(out=m1t[0:width, :], in0=m1t[0:width, :],
                             in1=trt[0:width, :])
            for width in (16, 8, 4, 2, 1):
                mask = [width + i if i < width else 0 for i in range(32)]
                v.stream_shuffle(out=trt[0:32, :], in_=m1t[0:32, :], mask=mask)
                v.tensor_max(out=m1t[0:width, :], in0=m1t[0:width, :],
                             in1=trt[0:width, :])
            v.tensor_scalar(out=scr1[:], in0=m1t[0:1, :], scalar1=EPS, scalar2=None,
                            op0=ALU.max)
            v.reciprocal(out=scr2[:], in_=scr1[:])
            v.tensor_scalar_mul(out=scr2[:], in0=scr2[:], scalar1=QB)
            v.wait_ge(sA.h, tok_w0done)
            v.tensor_scalar(out=scr1[:], in0=scr1[:], scalar1=mwc1[0:1, 0:1],
                                   scalar2=None, op0=ALU.mult)
        tok_s1 = sV.n

        @blk.tensor
        def _(t, tok_s1=tok_s1):
            t.wait_ge(sV.h, tok_s1)
            sMM.inc(t.matmul(pbc[:], ones1[:], scr2[:], start=True, stop=True))
        tok_bs1 = sMM.n

        @SCA
        def _(sc, tok_bs1=tok_bs1):
            sc.wait_ge(sMM.h, tok_bs1)
            sc.copy(out=mu_b[:], in_=pbc[:])
        tok_s1b = sA.n

        @blk.tensor
        def _(t, tok_s1b=tok_s1b):
            t.wait_ge(sA.h, tok_s1b)
            sMM.inc(t.matmul(pbc[:], ones1[:], scr1[:], start=True, stop=True))
        tok_bc1r = sMM.n

        @SCA
        def _(sc, ts=ts, tok_bc1r=tok_bc1r):
            sc.wait_ge(sMM.h, tok_bc1r)
            sc.copy(out=c1b[:, ts * TS:(ts + 1) * TS], in_=pbc[:])
        tok_c1b = sA.n

        @VEC
        def _(v, tok_s1b=tok_s1b):
            v.wait_ge(sA.h, tok_s1b)
            v.tensor_tensor(out=hbuf_3, in0=hbuf_3,
                                   in1=rep_bcast(mu_b, DC, TS), op=ALU.mult)
        tok_y1 = sV.n

        HW = DC * TS // 2
        for hh in range(2):
            @SCA
            def _(sc, hh=hh, tok_y1=tok_y1):
                sc.wait_ge(sV.h, tok_y1 if hh == 0 else sV.n)
                sc.activation(out=sgn_p[:], in_=hbuf[:, hh * HW:(hh + 1) * HW],
                              func=AF.Sign)
                sc.activation(out=paby[:], in_=hbuf[:, hh * HW:(hh + 1) * HW],
                                     func=AF.Abs)
            tok_absgn = sA.n

            @VEC
            def _(v, ts=ts, hh=hh, tok_absgn=tok_absgn, tok_ln1sp=tok_ln1sp):
                v.wait_ge(sA.h, tok_absgn)
                if hh == 0:
                    v.wait_ge(sLn1.h, tok_ln1sp)
                emit_peel(v, paby[:], res1[:, 0:HW])
                v.tensor_mul(out=res1[:, 0:HW], in0=paby[:], in1=sgn_p[:])
                v.tensor_tensor(
                    out=q1[:, ts * DC * TS + hh * HW:ts * DC * TS + (hh + 1) * HW],
                    in0=hbuf[:, hh * HW:(hh + 1) * HW], in1=res1[:, 0:HW],
                    op=ALU.subtract)
            tok_q1ts = sV.n
        prev_q1ts = tok_q1ts

    tok_q1 = sV.n
    tok_ln1_all = sLn1.n
    dbg_tensors.update(dict(q1=(q1, [128, NTS * DC * TS], BF16), c1b=(c1b, [128, S], F32)))
    if stage_n == 1:
        emit_taps(["q1", "c1b", "ybf", "sgv128", "mwc1", "mwc2"])
        ppst.close(); pst.close(); stPF.close()
        st.close()
        return nc, sorted(dbg_tensors.keys() & dbg)

    ppst.close()
    pst.close()

    f1L = contextlib.ExitStack()
    Mx = f1L.enter_context(nc.sbuf_tensor("Mx", [128, S], F32))
    scr3 = f1L.enter_context(nc.sbuf_tensor("scr3", [1, S], F32))
    f1 = contextlib.ExitStack()
    fsb = f1.enter_context
    wld = fsb(nc.sbuf_tensor("wld", [128, 3 * 2048], F32))
    qwv = fsb(nc.sbuf_tensor("qwv", [128, 2 * 2048], BF16))
    qwg = fsb(nc.sbuf_tensor("qwg", [128, 2 * 2048], BF16))
    wqt = fsb(nc.sbuf_tensor("wqt", [128, 2048], F32))
    wqs = fsb(nc.sbuf_tensor("wqs", [128, 2048], BF16))
    u_r = fsb(nc.sbuf_tensor("u_r", [128, 2 * S], F32))
    gc_t = fsb(nc.sbuf_tensor("gc_t", [128, TS], F32))
    sg_t = fsb(nc.sbuf_tensor("sg_t", [128, TS], F32))
    wo_r = fsb(nc.sbuf_tensor("wo_r", [128, 2 * 1024], F32))
    qo_t = fsb(nc.sbuf_tensor("qo_t", [128, 2 * 1024], BF16))
    woqt = fsb(nc.sbuf_tensor("woqt", [128, 1024], F32))
    woqs = fsb(nc.sbuf_tensor("woqs", [128, 1024], BF16))
    f1p = contextlib.ExitStack()
    pvg = f1p.enter_context(nc.psum_tensor("pvg", [128, 8 * TS], F32))

    @VEC
    def _(v):
        v.memset(Mx[:], 0.0)
    tok_mx0 = sV.n

    a0, v0, m0 = sA.n, sV.n, sMM.n
    TOK = {}
    a, v_, m = a0, v0, m0
    _wld_cnt = [0, 0, 0]
    for j in range(HC):
        for h, key in ((0, ("ldv", j)), (1, ("ldg", j))):
            slot = (2 * j + h) % 3
            _wld_cnt[slot] += 16
            TOK[key] = (rWLD.sems[slot].h, _wld_cnt[slot])
    _usp_cnt = [0, 0]
    for j in range(HC + 1):
        if j < HC:
            a += 2; TOK[("qv", j)] = a
            a += 2; TOK[("qg", j)] = a
        if j >= 1:
            jj = j - 1
            for ts in range(NTS):
                a += 1; TOK[("silu", jj, ts)] = a
                a += 1; TOK[("au", jj, ts)] = a
            _usp_cnt[jj % 2] += 16
            TOK[("usp", jj)] = (rUSP.sems[jj % 2].h, _usp_cnt[jj % 2])
        if j < HC:
            for h in (0, 1):
                a += 2; TOK[("woq", 2 * j + h)] = a
    for j in range(HC + 1):
        if j < HC:
            v_ += 1; TOK[("ternv", j)] = v_
            v_ += 1; TOK[("terng", j)] = v_
        if j >= 1:
            jj = j - 1
            for ts in range(NTS):
                v_ += 1; TOK[("gc", jj, ts)] = v_
                v_ += 1; TOK[("sg2", jj, ts)] = v_
                v_ += 1; TOK[("u", jj, ts)] = v_
                v_ += 1; TOK[("mx", jj, ts)] = v_
        if j < HC:
            v_ += 1; TOK[("wot", 2 * j)] = v_
            v_ += 1; TOK[("wot", 2 * j + 1)] = v_
    for j in range(HC):
        for ts in range(NTS):
            m += 1
            TOK[("mm", j, ts)] = m
    end_a, end_v, end_m = a, v_, m

    pP_v, pP_a = tok_q1, sA.n

    @blk.sync
    def _(s):
        for j in range(HC):
            if j < 2:
                s.wait_ge(sV.h, pP_v)
                s.wait_ge(sA.h, pP_a)
                s.wait_ge(sLn1.h, tok_ln1_all)
            else:
                s.wait_ge(sV.h, TOK[("terng", j - 2)])
            rWLD.inc(2 * j, s.dma_start(
                out=wld[:, (2 * j % 3) * 2048:(2 * j % 3 + 1) * 2048], in_=wgvt[j, 0]))
            if j >= 1:
                s.wait_ge(sV.h, TOK[("ternv", j - 1)])
            rWLD.inc(2 * j + 1, s.dma_start(
                out=wld[:, ((2 * j + 1) % 3) * 2048:((2 * j + 1) % 3 + 1) * 2048],
                in_=wgvt[j, 1]))

    @SCA
    def _(sc):
        sc.wait_ge(sA.h, tok_w0done)
        for j in range(HC + 1):
            if j < HC:
                sc.wait_ge(*TOK[("ldv", j)])
                if j >= 1:
                    sc.wait_ge(sV.h, TOK[("terng", j - 1)])
                sc.activation(out=wqt[:], in_=wld[:, (2 * j % 3) * 2048:(2 * j % 3 + 1) * 2048],
                              func=AF.Abs, scale=sgv128[:])
                sc.activation(out=wqs[:], in_=wld[:, (2 * j % 3) * 2048:(2 * j % 3 + 1) * 2048],
                                     func=AF.Sign, scale=sgv128[:])
                sc.wait_ge(*TOK[("ldg", j)])
                sc.wait_ge(sV.h, TOK[("ternv", j)])
                sc.activation(out=wqt[:], in_=wld[:, ((2 * j + 1) % 3) * 2048:((2 * j + 1) % 3 + 1) * 2048],
                              func=AF.Abs, scale=sgv128[:])
                sc.activation(out=wqs[:], in_=wld[:, ((2 * j + 1) % 3) * 2048:((2 * j + 1) % 3 + 1) * 2048],
                                     func=AF.Sign, scale=sgv128[:])
            if j >= 1:
                jj = j - 1
                r = jj % 2
                for ts in range(NTS):
                    sc.wait_ge(sV.h, TOK[("gc", jj, ts)])
                    sc.activation(out=sg_t[:], in_=gc_t[:], func=AF.Silu)
                    sc.wait_ge(sV.h, TOK[("u", jj, ts)])
                    sc.activation(out=gc_t[:],
                                         in_=u_r[:, r * S + ts * TS:r * S + (ts + 1) * TS],
                                         func=AF.Abs)
            for h in ((0, 1) if j < HC else ()):
                i = 2 * j + h
                wl = i % 2
                if i >= 2:
                    sc.wait_ge(sV.h, TOK[("wot", i - 2)])
                    dgp, jwp = divmod(i - 2, HC)
                    rQoS.inc(i - 2, sc.dma_start(
                        out=qo_sp[dgp, jwp // 4, :, (jwp % 4) * 1024:(jwp % 4 + 1) * 1024],
                        in_=qo_t[:, wl * 1024:(wl + 1) * 1024]))
                sc.wait_ge(rWO.sems[wl].h, 16 * (i // 2 + 1))
                if i >= 1:
                    sc.wait_ge(sV.h, TOK[("wot", i - 1)])
                sc.activation(out=woqt[:],
                              in_=wo_r[:, wl * 1024:(wl + 1) * 1024],
                              func=AF.Abs, scale=sout128[:])
                sc.activation(out=woqs[:],
                              in_=wo_r[:, wl * 1024:(wl + 1) * 1024],
                              func=AF.Sign, scale=sout128[:])
            if j == HC:
                for i in (2 * HC - 2, 2 * HC - 1):
                    wl = i % 2
                    sc.wait_ge(sV.h, TOK[("wot", i)])
                    dgp, jwp = divmod(i, HC)
                    rQoS.inc(i, sc.dma_start(
                        out=qo_sp[dgp, jwp // 4, :, (jwp % 4) * 1024:(jwp % 4 + 1) * 1024],
                        in_=qo_t[:, wl * 1024:(wl + 1) * 1024]))
    assert sA.n == end_a

    @VEC
    def _(v):
        for j in range(HC + 1):
            if j < HC:
                v.wait_ge(sA.h, TOK[("qv", j)])
                if j >= 2:
                    v.wait_ge(sMM.h, TOK[("mm", j - 2, NTS - 1)])
                v.scalar_tensor_tensor(
                    out=qwv[:, (j % 2) * 2048:(j % 2 + 1) * 2048], in0=wqt[:],
                    scalar=0.5, in1=wqs[:], op0=ALU.is_ge, op1=ALU.mult)
                v.wait_ge(sA.h, TOK[("qg", j)])
                v.scalar_tensor_tensor(
                    out=qwg[:, (j % 2) * 2048:(j % 2 + 1) * 2048], in0=wqt[:],
                    scalar=0.5, in1=wqs[:], op0=ALU.is_ge, op1=ALU.mult)
            if j >= 1:
                jj = j - 1
                r = jj % 2
                for ts in range(NTS):
                    q = (jj * NTS + ts) % 4
                    v.wait_ge(sMM.h, TOK[("mm", jj, ts)])
                    v.tensor_mul(out=gc_t[:],
                                 in0=pvg[:, (2 * q + 1) * TS:(2 * q + 2) * TS],
                                 in1=c1b[:, ts * TS:(ts + 1) * TS])
                    v.wait_ge(sA.h, TOK[("silu", jj, ts)])
                    v.tensor_mul(out=sg_t[:], in0=sg_t[:],
                                 in1=c1b[:, ts * TS:(ts + 1) * TS])
                    if jj >= 2:
                        v.wait_ge(*TOK[("usp", jj - 2)])
                        v.wait_ge(sA.h, TOK[("au", jj - 2, ts)])
                    v.tensor_mul(out=u_r[:, r * S + ts * TS:r * S + (ts + 1) * TS],
                                 in0=pvg[:, 2 * q * TS:(2 * q + 1) * TS],
                                 in1=sg_t[:])
                    v.wait_ge(sA.h, TOK[("au", jj, ts)])
                    v.tensor_max(out=Mx[:, ts * TS:(ts + 1) * TS],
                                 in0=Mx[:, ts * TS:(ts + 1) * TS], in1=gc_t[:])
            for h in ((0, 1) if j < HC else ()):
                i = 2 * j + h
                wl = i % 2
                v.wait_ge(sA.h, TOK[("woq", i)])
                if i >= 2:
                    v.wait_ge(rQoS.sems[wl].h, 16 * (i // 2))
                v.scalar_tensor_tensor(out=qo_t[:, wl * 1024:(wl + 1) * 1024],
                                       in0=woqt[:], scalar=0.5, in1=woqs[:],
                                       op0=ALU.is_ge, op1=ALU.mult)
    assert sV.n == end_v

    @blk.tensor
    def _(t):
        for j in range(HC):
            r = j % 2
            t.wait_ge(sV.h, TOK[("terng", j)])
            for ts in range(NTS):
                q = (j * NTS + ts) % 4
                if j >= 1:
                    t.wait_ge(sV.h, TOK[("u", j - 1, ts)])
                for k in range(DC):
                    t.matmul(pvg[:, 2 * q * TS:(2 * q + 1) * TS],
                             qwv[:, r * 2048 + k * 128:r * 2048 + (k + 1) * 128],
                             q1[:, (ts * DC + k) * TS:(ts * DC + k + 1) * TS],
                             start=(k == 0), stop=(k == DC - 1))
                for k in range(DC):
                    last = t.matmul(pvg[:, (2 * q + 1) * TS:(2 * q + 2) * TS],
                                    qwg[:, r * 2048 + k * 128:r * 2048 + (k + 1) * 128],
                                    q1[:, (ts * DC + k) * TS:(ts * DC + k + 1) * TS],
                                    start=(k == 0), stop=(k == DC - 1))
                sMM.inc(last)
    assert sMM.n == end_m

    @blk.gpsimd
    def _(g):
        for j in range(HC + 1):
            for h in ((0, 1) if j < HC else ()):
                i = 2 * j + h
                if i < 2:
                    g.wait_ge(sV.h, pP_v)
                    g.wait_ge(sA.h, pP_a)
                    g.wait_ge(sLn1.h, tok_ln1_all)
                else:
                    g.wait_ge(sA.h, TOK[("woq", i - 2)])
                dg, jw = divmod(i, HC)
                rWO.inc(i, g.dma_start(
                    out=wo_r[:, (i % 2) * 1024:(i % 2 + 1) * 1024],
                    in_=woutT[dg, jw]))
            if j >= 1:
                jj = j - 1
                g.wait_ge(sV.h, TOK[("mx", jj, NTS - 1)])
                rUSP.inc(jj, g.dma_start(out=u_sp[jj],
                                         in_=u_r[:, (jj % 2) * S:(jj % 2 + 1) * S]))

    tok_woq = end_a
    tok_qodone = end_v
    tok_qosp = [s.tok() for s in rQoS.sems]

    f1p.close()
    f1.close()
    f15 = contextlib.ExitStack()
    trt2 = f15.enter_context(nc.sbuf_tensor("trt2", [64, S], F32))
    f1p2 = contextlib.ExitStack()
    pbc2 = f1p2.enter_context(nc.psum_tensor("pbc2", [128, S], F32))

    @VEC
    def _(v):
        v.wait_ge(sV.h, end_v)
        for width in (64, 32):
            v.tensor_copy(out=trt2[0:width, :], in_=Mx[width:2 * width, :])
            v.tensor_max(out=Mx[0:width, :], in0=Mx[0:width, :], in1=trt2[0:width, :])
        for width in (16, 8, 4, 2, 1):
            mask = [width + i if i < width else 0 for i in range(32)]
            v.stream_shuffle(out=trt2[0:32, :], in_=Mx[0:32, :], mask=mask)
            v.tensor_max(out=Mx[0:width, :], in0=Mx[0:width, :], in1=trt2[0:width, :])
        v.tensor_scalar(out=scr3[:], in0=Mx[0:1, :], scalar1=EPS, scalar2=None,
                        op0=ALU.max)
        v.reciprocal(out=Mx[0:1, :], in_=scr3[:])
        v.tensor_scalar_mul(out=Mx[0:1, :], in0=Mx[0:1, :], scalar1=QB)
        v.wait_ge(sA.h, tok_w0done)
        v.tensor_scalar(out=scr3[:], in0=scr3[:], scalar1=mwc2[0:1, 0:1],
                               scalar2=None, op0=ALU.mult)
    tok_s2row = sV.n

    @blk.tensor
    def _(t):
        t.wait_ge(sV.h, tok_s2row)
        for ts in range(NTS):
            last = t.matmul(pbc2[:, ts * TS:(ts + 1) * TS], ones1[:],
                            Mx[0:1, ts * TS:(ts + 1) * TS], start=True, stop=True)
        sMM.inc(last)
    tok_bs2 = sMM.n

    @SCA
    def _(sc):
        sc.wait_ge(sMM.h, tok_bs2)
        sc.copy(out=s2b[:], in_=pbc2[:])
    tok_s2b = sA.n

    @blk.tensor
    def _(t):
        t.wait_ge(sA.h, tok_s2b)
        for ts in range(NTS):
            last = t.matmul(pbc2[:, ts * TS:(ts + 1) * TS], ones1[:],
                            scr3[0:1, ts * TS:(ts + 1) * TS], start=True, stop=True)
        sMM.inc(last)
    tok_bc2m = sMM.n

    @SCA
    def _(sc):
        sc.wait_ge(sMM.h, tok_bc2m)
        sc.copy(out=c2b[:], in_=pbc2[:])
    tok_c2b = sA.n

    dbg_tensors.update(dict(Mx=(Mx, [128, S], F32), s2b=(s2b, [128, S], F32),
                            c2b=(c2b, [128, S], F32)))
    if stage_n == 2:
        emit_taps(["Mx", "s2b", "c2b", "q1", "c1b"])
        f1p2.close(); f15.close(); f1L.close(); stPF.close()
        st.close()
        return nc, sorted(dbg_tensors.keys() & dbg)

    f1p2.close()
    f15.close()
    f1L.close()
    stPF.close()

    f2 = contextlib.ExitStack()
    f2b = f2.enter_context
    qo_r = f2b(nc.sbuf_tensor("qo_r", [128, 2 * 4096], BF16))
    q2_r = f2b(nc.sbuf_tensor("q2_r", [128, (HC // 4) * 2048], BF16))
    o_ts = f2b(nc.sbuf_tensor("o_ts", [128, DC * TS], F32))
    lbuf = f2b(nc.sbuf_tensor("lbuf", [128, DC * TS], F32))
    sqb2 = f2b(nc.sbuf_tensor("sqb2", [128, 2 * TS], BF16))
    mu2b = f2b(nc.sbuf_tensor("mu2b", [128, TS], F32))
    rstd2b = f2b(nc.sbuf_tensor("rstd2b", [128, TS], F32))
    ub2 = f2b(nc.sbuf_tensor("ub2", [128, 4 * TS], F32))
    zt2 = f2b(nc.sbuf_tensor("zt2", [128, 4 * TS], F32))
    mq2 = f2b(nc.sbuf_tensor("mq2", [128, 4 * TS], F32))
    q2c = f2b(nc.sbuf_tensor("q2c", [128, 4 * TS], BF16))
    f2p = contextlib.ExitStack()
    pout = f2p.enter_context(nc.psum_tensor("pout", [128, 8 * TS], F32))

    o_3 = o_ts.ap().rearrange("p (c w) -> p c w", c=DC)

    HYB_S = [-64.0, -32.0, -16.0, -8.0, -4.0, -2.0, -1.0, -0.5]

    BND = ([(sV.h, tok_s2row), (sV.h, tok_qodone), (sA.h, tok_woq),
            (sA.h, tok_c2b), (sLn1.h, tok_ln1_all)]
           + [TOK[("usp", HC - 2)], TOK[("usp", HC - 1)]] + list(tok_qosp))

    vfin = {}
    ub2_toks = {}
    st_tok = {}
    NP = HC // 4

    def emit_q2_slice(ts, mid_cb=None):
        vb, ab = sV.n, sA.n

        def soff(p):
            return 8 if (mid_cb is not None and p >= NP // 2) else 0
        for u in range(HC // 2):
            pp, lane = divmod(u, 2)
            vfin[(ts, u)] = vb + 20 * pp + 19 + lane
        sl = slice(ts * TS, (ts + 1) * TS)

        @blk.gpsimd
        def _(g, ts=ts):
            for u in range(HC // 2):
                lane = u % 2
                if ts == 0:
                    g.wait_ge(*TOK[("usp", 2 * u)])
                    g.wait_ge(*TOK[("usp", 2 * u + 1)])
                    if u < 2:
                        for h_, v_ in BND:
                            g.wait_ge(h_, v_)
                if ts > 0 or u >= 2:
                    pu = (ts, u - 2) if u >= 2 else (ts - 1, u + HC // 2 - 2)
                    g.wait_ge(sV.h, vfin[pu])
                for half in range(2):
                    tk = rUB.inc(u, g.dma_start(
                        out=ub2[:, (2 * lane + half) * TS:(2 * lane + half + 1) * TS],
                        in_=u_sp[2 * u + half, :, ts * TS:(ts + 1) * TS]))
                ub2_toks[(ts, u)] = tk

        @VEC
        def _(v, ts=ts, vb=vb, ab=ab, sl=sl):
            s2bc = s2b.ap()[:, sl].rearrange("p (c w) -> p c w", c=1) \
                                  .broadcast_to((128, 2, TS))
            for p in range(NP):
                for lane in range(2):
                    u = 2 * p + lane
                    v.wait_ge(*ub2_toks[(ts, u)])
                    ubv = ub2.ap()[:, lane * 2 * TS:(lane + 1) * 2 * TS] \
                             .rearrange("p (c w) -> p c w", c=2)
                    v.tensor_tensor(out=ubv, in0=ubv, in1=s2bc, op=ALU.mult)
                for k in range(8):
                    for lane in range(2):
                        c0, c1 = lane * 2 * TS, (lane + 1) * 2 * TS
                        v.wait_ge(sA.h, ab + 9 * p + soff(p) + k + 1)
                        v.scalar_tensor_tensor(
                            out=zt2[:, c0:c1],
                            in0=mq2[:, c0:c1], scalar=HYB_S[k],
                            in1=(ub2 if k == 0 else zt2)[:, c0:c1],
                            op0=ALU.mult, op1=ALU.add)
                for lane in range(2):
                    c0, c1 = lane * 2 * TS, (lane + 1) * 2 * TS
                    v.scalar_tensor_tensor(
                        out=mq2[:, c0:c1], in0=zt2[:, c0:c1], scalar=-1.0,
                        in1=ub2[:, c0:c1], op0=ALU.mult, op1=ALU.add)
        assert sV.n == vb + 20 * NP

        @SCA
        def _(sc, ts=ts, vb=vb, ab=ab):
            for p in range(NP):
                if mid_cb is not None and p == NP // 2:
                    mid_cb(sc)
                for k in range(8):
                    sc.wait_ge(sV.h, vb + 20 * p + (2 if k == 0 else 2 * k + 2))
                    sc.activation(out=mq2[:, 0:4 * TS],
                                  in_=(ub2 if k == 0 else zt2)[:, 0:4 * TS],
                                  func=AF.Sign, bias=pbh[:])
                sc.wait_ge(sV.h, vfin[(ts, 2 * p + 1)])
                for u in (2 * p, 2 * p + 1):
                    pu = ((ts, u - 2) if u >= 2
                          else (ts - 1, u + HC // 2 - 2) if ts > 0 else None)
                    if pu is not None:
                        sc.wait_ge(*st_tok[pu])
                sc.activation(out=q2c[:, 0:2048],
                              in_=mq2[:, 0:4 * TS], func=AF.Copy, bias=-0.5)
                for u in (2 * p, 2 * p + 1):
                    lane = u % 2
                    st_tok[(ts, u)] = rQ2S.inc(lane, sc.dma_start(
                        out=q2_sp[ts, u // 2, :, (u % 2) * 1024:(u % 2 + 1) * 1024],
                        in_=q2c[:, lane * 1024:(lane + 1) * 1024]))
        assert sA.n == ab + 9 * NP + (8 if mid_cb is not None else 0)

    prev_next_ok = None
    prev_res2 = None
    prev_mm_end = None
    prev_f2_m0 = None

    def emit_f2_loads_mms(ts, dg1_gate):
        nonlocal prev_mm_end, prev_f2_m0
        f2_m0 = sMM.n
        f2_toks = {}
        q2tok = {}

        @blk.sync
        def _(s, ts=ts, f2_m0=f2_m0, prev_mm_end=prev_mm_end,
              prev_f2_m0=prev_f2_m0):
            for h_, v_ in tok_qosp:
                s.wait_ge(h_, v_)
            if ts == 0:
                for h_, v_ in BND:
                    s.wait_ge(h_, v_)
            for cj in range(HC // 4):
                s.wait_ge(*st_tok[(ts, 2 * cj)])
                s.wait_ge(*st_tok[(ts, 2 * cj + 1)])
                if prev_f2_m0 is not None:
                    s.wait_ge(sMM.h, prev_f2_m0 + HC + 4 * cj + 4)
                q2tok[cj] = rQ2R.inc(cj, s.dma_start(
                    out=q2_r[:, cj * 2048:(cj + 1) * 2048],
                    in_=q2_sp[ts, cj]))
            for c in range(2 * (HC // 4)):
                dg, cj = divmod(c, HC // 4)
                if c >= 2:
                    s.wait_ge(sMM.h, f2_m0 + (c - 1) * 4)
                elif prev_mm_end is not None:
                    s.wait_ge(sMM.h, prev_mm_end)
                f2_toks[c] = rF2.inc(c, s.dma_start(
                    out=qo_r[:, (c % 2) * 4096:(c % 2 + 1) * 4096],
                    in_=qo_sp[dg, cj]))

        @blk.tensor
        def _(t, ts=ts, dg1_gate=dg1_gate, prev_next_ok=prev_next_ok):
            if prev_next_ok is not None:
                t.wait_ge(sA.h, prev_next_ok)
            else:
                t.wait_ge(sA.h, tok_c2b)
            for dg in range(2):
                if dg == 1:
                    t.wait_ge(sA.h, dg1_gate)
                for j in range(HC):
                    c = dg * (HC // 4) + j // 4
                    t.wait_ge(*f2_toks[c])
                    if dg == 0:
                        t.wait_ge(*q2tok[j // 4])
                    base = (c % 2) * 4096 + (j % 4) * 1024
                    rhs = q2_r[:, (j // 4) * 2048 + (j % 4) * TS:
                               (j // 4) * 2048 + (j % 4) * TS + TS]
                    for d in range(8):
                        last = t.matmul(pout[:, d * TS:(d + 1) * TS],
                                        qo_r[:, base + d * 128:base + (d + 1) * 128],
                                        rhs, start=(j == 0), stop=(j == HC - 1))
                    sMM.inc(last)
        prev_mm_end = sMM.n
        prev_f2_m0 = f2_m0
        assert sMM.n == f2_m0 + 2 * HC
        return f2_m0

    def mk_evdg0(ts, f2_m0):
        def cb(sc):
            sc.wait_ge(sMM.h, f2_m0 + HC)
            if ts > 0:
                sc.wait_ge(sOut.h, 16 * DC * ts)
            for d in range(8):
                sc.copy(out=o_ts[:, d * TS:(d + 1) * TS],
                        in_=pout[:, d * TS:(d + 1) * TS])
        return cb

    def emit_f2_epilogue(ts, f2_m0):
        nonlocal prev_next_ok, prev_res2

        @SCA
        def _(sc, ts=ts, f2_m0=f2_m0):
            sc.wait_ge(sMM.h, f2_m0 + 2 * HC)
            for d in range(8):
                sc.copy(out=o_ts[:, (8 + d) * TS:(9 + d) * TS],
                        in_=pout[:, d * TS:(d + 1) * TS])
        tok_ev2 = sA.n

        @blk.sync
        def _(s, ts=ts, prev_res2=prev_res2):
            s.wait_ge(sLn1.h, tok_ln1_all)
            if ts == 0:
                for h_, v_ in BND:
                    s.wait_ge(h_, v_)
            if prev_res2 is not None:
                s.wait_ge(sV.h, prev_res2)
            for d in range(DC):
                sLb.inc(s.dma_start(out=lbuf[:, d * TS:(d + 1) * TS],
                                    in_=ln1_sp[d, :, ts * TS:(ts + 1) * TS]), 16)
        tok_lb = sLb.n

        c2sl = c2b.ap()[:, ts * TS:(ts + 1) * TS].rearrange(
            "p (c w) -> p c w", c=1).broadcast_to((128, DC, TS))

        if "f2dbg" in dbg and ts == 0:
            draw = nc.dram_tensor("dbg_raw0", [128, DC * TS], F32, kind="ExternalOutput").ap()
            _ev, _lg0 = tok_ev2, sLG.n

            @blk.gpsimd
            def _(g):
                g.wait_ge(sA.h, _ev)
                sLG.inc(g.dma_start(out=draw, in_=o_ts[:]), 16)
            _dump1 = sLG.n

        @VEC
        def _(v, tok_ev2=tok_ev2, tok_lb=tok_lb, c2sl=c2sl, ts=ts):
            v.wait_ge(sA.h, tok_ev2)
            v.wait_ge(sLb.h, tok_lb)
            if "f2dbg" in dbg and ts == 0:
                v.wait_ge(sLG.h, _dump1)
            v.tensor_tensor(out=o_3, in0=o_3, in1=c2sl, op=ALU.mult)
            v.scalar_tensor_tensor(out=o_ts[:], in0=lbuf[:], scalar=2.0,
                                   in1=o_ts[:], op0=ALU.mult, op1=ALU.add)
        tok_res2 = sV.n
        prev_res2 = tok_res2
        if "f2dbg" in dbg and ts == 0:
            dres = nc.dram_tensor("dbg_res20", [128, DC * TS], F32, kind="ExternalOutput").ap()
            dlb = nc.dram_tensor("dbg_lbuf0", [128, DC * TS], F32, kind="ExternalOutput").ap()
            _r2 = tok_res2

            @blk.gpsimd
            def _(g):
                g.wait_ge(sV.h, _r2)
                sLG.inc(g.dma_start(out=dres, in_=o_ts[:]), 16)
                sLG.inc(g.dma_start(out=dlb, in_=lbuf[:]), 16)
            _dump2 = sLG.n

            @SCA
            def _(sc):
                sc.wait_ge(sLG.h, _dump2)


        abase_sq3 = sA.n
        mbase_sq3 = sMM.n
        @SCA
        def _(sc, tok_res2=tok_res2, mbase_sq3=mbase_sq3):
            sc.wait_ge(sV.h, tok_res2)
            for d in range(DC):
                if d >= 2:
                    sc.wait_ge(sMM.h, mbase_sq3 + DC + d - 1)
                sc.activation(out=sqb2[:, (d % 2) * TS:(d % 2 + 1) * TS],
                                     in_=o_ts[:, d * TS:(d + 1) * TS], func=AF.Square)
        tok_sq3 = sA.n

        @blk.tensor
        def _(t, tok_res2=tok_res2, abase_sq3=abase_sq3):
            t.wait_ge(sV.h, tok_res2)
            for d in range(DC):
                sMM.inc(t.matmul(pout[0:1, 0:TS], onesc[:], o_ts[:, d * TS:(d + 1) * TS],
                                 start=(d == 0), stop=(d == DC - 1)))
            for d in range(DC):
                t.wait_ge(sA.h, abase_sq3 + d + 1)
                sMM.inc(t.matmul(pout[0:1, TS:2 * TS], onescb[:],
                                 sqb2[:, (d % 2) * TS:(d % 2 + 1) * TS],
                                 start=(d == 0), stop=(d == DC - 1)))
        tok_st3 = sMM.n

        @VEC
        def _(v, tok_st3=tok_st3):
            v.wait_ge(sMM.h, tok_st3)
            v.tensor_scalar_mul(out=scr1[:], in0=pout[0:1, 0:TS], scalar1=1.0 / D)
            v.tensor_scalar_mul(out=scr2[:], in0=pout[0:1, TS:2 * TS], scalar1=1.0 / D)
            v.tensor_mul(out=scr4[:], in0=scr1[:], in1=scr1[:])
            v.tensor_sub(out=scr2[:], in0=scr2[:], in1=scr4[:])
        tok_var2 = sV.n

        @SCA
        def _(sc, tok_var2=tok_var2):
            sc.wait_ge(sV.h, tok_var2)
            sc.activation(out=scr2[:], in_=scr2[:], func=AF.Sqrt, bias=epsc[:])
        tok_sqr3 = sA.n

        @VEC
        def _(v, tok_sqr3=tok_sqr3):
            v.wait_ge(sA.h, tok_sqr3)
            v.reciprocal(out=scr2[:], in_=scr2[:])
        tok_rstd2 = sV.n

        @blk.tensor
        def _(t, tok_rstd2=tok_rstd2):
            t.wait_ge(sV.h, tok_rstd2)
            t.matmul(pout[:, 2 * TS:3 * TS], ones1[:], scr1[:], start=True, stop=True)
            sMM.inc(t.matmul(pout[:, 3 * TS:4 * TS], ones1[:], scr2[:], start=True,
                             stop=True))
        tok_bc6 = sMM.n

        @SCA
        def _(sc, tok_bc6=tok_bc6):
            sc.wait_ge(sMM.h, tok_bc6)
            sc.copy(out=mu2b[:], in_=pout[:, 2 * TS:3 * TS])
            sc.copy(out=rstd2b[:], in_=pout[:, 3 * TS:4 * TS])
        tok_bc6c = sA.n
        prev_next_ok = tok_bc6c

        @VEC
        def _(v, tok_bc6c=tok_bc6c):
            v.wait_ge(sA.h, tok_bc6c)
            v.tensor_tensor(out=o_3, in0=o_3, in1=rep_bcast(mu2b, DC, TS),
                            op=ALU.subtract)
            v.tensor_tensor(out=o_3, in0=o_3, in1=rep_bcast(rstd2b, DC, TS),
                                   op=ALU.mult)
        tok_out = sV.n

        @SCA
        def _(sc, ts=ts, tok_out=tok_out):
            sc.wait_ge(sV.h, tok_out)
            for d in range(DC):
                sOut.inc(sc.dma_start(out=outT[d, :, ts * TS:(ts + 1) * TS],
                                      in_=o_ts[:, d * TS:(d + 1) * TS]), 16)

    emit_q2_slice(0)
    for ts in range(NTS):
        dg1_gate = sA.n + (9 * (NP // 2) + 8 if ts + 1 < NTS else 8)
        f2_m0 = emit_f2_loads_mms(ts, dg1_gate)
        if ts + 1 < NTS:
            emit_q2_slice(ts + 1, mid_cb=mk_evdg0(ts, f2_m0))
        else:
            @SCA
            def _(sc, ts=ts, f2_m0=f2_m0, cb=mk_evdg0(ts, f2_m0)):
                cb(sc)
        emit_f2_epilogue(ts, f2_m0)

    tok_allout = sOut.n

    @blk.sync
    def _(s):
        s.wait_ge(sOut.h, tok_allout)

    f2p.close()
    f2.close()
    st.close()
    return nc, sorted(dbg_tensors.keys() & dbg)


def _prep_shared(Wi, Wg, Wdt, Wb, Wc, Wo, A_log, b_dt, w_norm, w_gv, w_out):
    f = np.ascontiguousarray
    shared = {
        "wiT": f(Wi.T), "wgT": f(Wg.T), "wdtT": f(Wdt.T),
        "wbrep": f(np.broadcast_to(Wb[:, :, None], (KST, NSC, 128))),
        "wcrep": f(np.broadcast_to(Wc[:, :, None], (KST, NSC, 128))),
        "woT": f(Wo.T), "alog": f(A_log), "bdt": f(b_dt.reshape(NSC, 1)),
        "wnorm": f(w_norm.reshape(DC, 128).T),
    }
    W4 = w_gv.reshape(2, HC, 128, DC, 128)
    shared["wgvt"] = f(W4.transpose(1, 0, 4, 3, 2).reshape(HC, 2, 128, DC * 128))
    V5 = w_out.reshape(2, 8, 128, HC, 128)
    shared["woutT"] = f(V5.transpose(0, 3, 4, 1, 2).reshape(2, HC, 128, 8 * 128))
    return shared


def kernel(L0_attn_x, L0_attn_Wi, L0_attn_Wo, L0_attn_A_log, L0_attn_Wb,
           L0_attn_Wc, L0_attn_Wdt, L0_attn_b_dt, L0_attn_Wg,
           L0_ffn_w_norm, L0_ffn_w_gv, L0_ffn_w_out):
    x = np.asarray(L0_attn_x, dtype=np.float32)
    B = x.shape[0]
    shared = _prep_shared(
        np.asarray(L0_attn_Wi, np.float32), np.asarray(L0_attn_Wg, np.float32),
        np.asarray(L0_attn_Wdt, np.float32), np.asarray(L0_attn_Wb, np.float32),
        np.asarray(L0_attn_Wc, np.float32), np.asarray(L0_attn_Wo, np.float32),
        np.asarray(L0_attn_A_log, np.float32), np.asarray(L0_attn_b_dt, np.float32),
        np.asarray(L0_ffn_w_norm, np.float32), np.asarray(L0_ffn_w_gv, np.float32),
        np.asarray(L0_ffn_w_out, np.float32))
    nc, _ = build_program()
    in_maps = []
    for b in range(B):
        m = dict(shared)
        m["xT"] = np.ascontiguousarray(x[b].T)
        in_maps.append(m)
    res = run_bass_kernel_spmd(nc, in_maps, core_ids=list(range(B)))
    out = np.empty((B, S, D), np.float32)
    for b in range(B):
        o = np.asarray(res.results[b]["outT"])
        out[b] = o.reshape(D, S).T
    return out


# version 14
# speedup vs baseline: 1.1262x; 1.1262x over previous
import contextlib
import numpy as np
import concourse.bass as bass
import concourse.mybir as mybir
from concourse.bass_utils import run_bass_kernel_spmd

F32 = mybir.dt.float32
BF16 = mybir.dt.bfloat16
ALU = mybir.AluOpType
AF = mybir.ActivationFunctionType
AX = mybir.AxisListType

S = 2048
D = 2048
NSC = 128
KST = 16
H = 8192
TS = 512
NTS = S // TS
DC = D // 128
HC = H // 128
EPS = 1e-5
QB = 127.0
STAGES = ("A0", "A1", "A2", "A3", "A", "P", "F15", "Q2", "F2")


class Sem:
    def __init__(self, stack, nc, name):
        self.h = stack.enter_context(nc.semaphore(name))
        self.n = 0

    def inc(self, instr, amt=1):
        instr.then_inc(self.h, amt)
        self.n += amt
        return self.n

    def tok(self):
        return (self.h, self.n)


class DRing:

    def __init__(self, stack, nc, name, slots):
        self.sems = [Sem(stack, nc, f"{name}{i}") for i in range(slots)]
        self.slots = slots

    def inc(self, idx, instr, amt=16):
        s = self.sems[idx % self.slots]
        s.inc(instr, amt)
        return (s.h, s.n)


def emit_peel(v, absbuf, tmpbuf, levels=7):
    for lvl in range(levels):
        hh = float(2 ** (levels - 1 - lvl))
        v.tensor_scalar(out=tmpbuf, in0=absbuf, scalar1=hh - 0.5, scalar2=hh,
                        op0=ALU.is_ge, op1=ALU.mult)
        v.tensor_tensor(out=absbuf, in0=absbuf, in1=tmpbuf, op=ALU.subtract)


def rep_bcast(t, c, w):
    return t.ap().rearrange("p (c w) -> p c w", c=1).broadcast_to((128, c, w))


class SyncEng:
    _PASS = ("wait_ge", "dma_start", "dma_start_transpose", "nop")

    def __init__(self, eng, sem):
        object.__setattr__(self, "_e", eng)
        object.__setattr__(self, "_s", sem)

    def __getattr__(self, name):
        f = getattr(self._e, name)
        if name in SyncEng._PASS or not callable(f):
            return f
        s = self._s
        e = self._e

        def call(*a, **kw):
            r = f(*a, **kw)
            s.inc(r)
            e.wait_ge(s.h, s.n)
            return r
        return call


def build_program(upto="F2", dbg=()):
    stage_n = STAGES.index(upto) - 4
    nc = bass.Bass("TRN2", target_bir_lowering=False, debug=False)
    st = contextlib.ExitStack()
    dbg = set(dbg)
    dbg_tensors = {}

    xT = nc.dram_tensor("xT", [D, S], F32, kind="ExternalInput").ap()
    wiT = nc.dram_tensor("wiT", [D, NSC], F32, kind="ExternalInput").ap()
    wgT = nc.dram_tensor("wgT", [D, NSC], F32, kind="ExternalInput").ap()
    wdtT = nc.dram_tensor("wdtT", [NSC, NSC], F32, kind="ExternalInput").ap()
    wbrep = nc.dram_tensor("wbrep", [KST, NSC, 128], F32, kind="ExternalInput").ap()
    wcrep = nc.dram_tensor("wcrep", [KST, NSC, 128], F32, kind="ExternalInput").ap()
    woT = nc.dram_tensor("woT", [NSC, D], F32, kind="ExternalInput").ap()
    alog = nc.dram_tensor("alog", [NSC, KST], F32, kind="ExternalInput").ap()
    bdt = nc.dram_tensor("bdt", [NSC, 1], F32, kind="ExternalInput").ap()
    wnorm = nc.dram_tensor("wnorm", [128, DC], F32, kind="ExternalInput").ap()
    wgvt = nc.dram_tensor("wgvt", [HC, 2, 128, DC * 128], F32, kind="ExternalInput").ap()
    woutT = nc.dram_tensor("woutT", [2, HC, 128, 8 * 128], F32, kind="ExternalInput").ap()
    outT = nc.dram_tensor("outT", [DC, 128, S], F32, kind="ExternalOutput").ap()

    u_sp = nc.dram_tensor("u_sp", [HC, 128, S], F32, kind="Internal").ap()
    ln1_sp = nc.dram_tensor("ln1_sp", [DC, 128, S], F32, kind="Internal").ap()
    qo_sp = nc.dram_tensor("qo_sp", [2, HC // 4, 128, 4 * 1024], BF16, kind="Internal").ap()
    q2_sp = nc.dram_tensor("q2_sp", [NTS, HC // 4, 128, 4 * TS], BF16, kind="Internal").ap()

    sLG = Sem(st, nc, "sLG")
    sSm = Sem(st, nc, "sSm")
    sMM = Sem(st, nc, "sMM")
    sV = Sem(st, nc, "sV")
    sA = Sem(st, nc, "sA")
    rW0 = DRing(st, nc, "rW0", 6)
    rPX = DRing(st, nc, "rPX", 2)
    rWLD = DRing(st, nc, "rWLD", 3)
    rWO = DRing(st, nc, "rWO", 2)
    rUB = DRing(st, nc, "rUB", 2)
    rF2 = DRing(st, nc, "rF2", 2)
    rQ2R = DRing(st, nc, "rQ2R", 16)
    rQ2S = DRing(st, nc, "rQ2S", 2)
    rUSP = DRing(st, nc, "rUSP", 2)
    sLn1 = Sem(st, nc, "sLn1")
    sLb = Sem(st, nc, "sLb")
    rQoS = DRing(st, nc, "rQoS", 2)
    sOut = Sem(st, nc, "sOut")

    sb = st.enter_context
    ones1 = sb(nc.sbuf_tensor("ones1", [1, 128], F32))
    onesc = sb(nc.sbuf_tensor("onesc", [128, 1], F32))
    onescb = sb(nc.sbuf_tensor("onescb", [128, 1], BF16))
    bdt_s = sb(nc.sbuf_tensor("bdt_s", [NSC, 1], F32))
    nega = sb(nc.sbuf_tensor("nega", [NSC, KST], F32))
    wn_s = sb(nc.sbuf_tensor("wn_s", [128, DC], F32))
    sgv128 = sb(nc.sbuf_tensor("sgv128", [128, 1], F32))
    sout128 = sb(nc.sbuf_tensor("sout128", [128, 1], F32))
    mwc1 = sb(nc.sbuf_tensor("mwc1", [1, 1], F32))
    mwc2 = sb(nc.sbuf_tensor("mwc2", [1, 1], F32))
    accW = sb(nc.sbuf_tensor("accW", [128, 256], F32))
    scr1 = sb(nc.sbuf_tensor("scr1", [1, TS], F32))
    scr2 = sb(nc.sbuf_tensor("scr2", [1, TS], F32))
    scr4 = sb(nc.sbuf_tensor("scr4", [1, TS], F32))
    epsc = sb(nc.sbuf_tensor("epsc", [1, 1], F32))

    ybf = sb(nc.sbuf_tensor("ybf", [NSC, S], BF16))
    c2b = sb(nc.sbuf_tensor("c2b", [128, S], F32))
    s2b = sb(nc.sbuf_tensor("s2b", [128, S], F32))
    pbh = sb(nc.sbuf_tensor("pbh", [128, 1], F32))
    pbm = sb(nc.sbuf_tensor("pbm", [128, 1], F32))
    stPF = contextlib.ExitStack()

    blk = st.enter_context(nc.Block())

    def VEC(f):
        blk.vector(lambda v: f(SyncEng(v, sV)))

    def SCA(f):
        blk.scalar(lambda sc: f(SyncEng(sc, sA)))

    def emit_taps(names):
        taps = []
        for nm in names:
            if nm not in dbg:
                continue
            hnd, shape, dt = dbg_tensors[nm]
            ap = nc.dram_tensor("dbg_" + nm, shape, F32, kind="ExternalOutput").ap()
            taps.append((hnd, ap))
        vv, aa, mm = sV.n, sA.n, sMM.n

        @blk.gpsimd
        def _(g):
            g.wait_ge(sV.h, vv)
            g.wait_ge(sA.h, aa)
            g.wait_ge(sMM.h, mm)
            for hnd, ap in taps:
                sLG.inc(g.dma_start(out=ap, in_=hnd[:]), 16)
            g.wait_ge(sLG.h, sLG.n)

    ast = contextlib.ExitStack()
    asb = ast.enter_context
    wdt_s = asb(nc.sbuf_tensor("wdt_s", [NSC, NSC], BF16))
    wb_s = asb(nc.sbuf_tensor("wb_s", [NSC, KST * 128], BF16))
    wc_s = asb(nc.sbuf_tensor("wc_s", [NSC, KST * 128], BF16))
    wo_s = asb(nc.sbuf_tensor("wo_s", [NSC, D], BF16))
    alog_s = asb(nc.sbuf_tensor("alog_s", [NSC, KST], F32))
    zf = asb(nc.sbuf_tensor("zf", [NSC, S], F32))
    zbf = asb(nc.sbuf_tensor("zbf", [NSC, S], BF16))
    gsil = asb(nc.sbuf_tensor("gsil", [NSC, S], F32))
    dt_s = asb(nc.sbuf_tensor("dt_s", [NSC, S], F32))
    w_s = asb(nc.sbuf_tensor("w_s", [NSC, S], F32))
    ak = asb(nc.sbuf_tensor("ak", [NSC, S], F32))
    uk = asb(nc.sbuf_tensor("uk", [NSC, S], F32))
    stk = asb(nc.sbuf_tensor("stk", [NSC, S], F32))
    yacc = asb(nc.sbuf_tensor("yacc", [NSC, S], F32))
    ptmp = asb(nc.sbuf_tensor("ptmp", [NSC, S], F32))
    sp1 = asb(nc.sbuf_tensor("sp1", [NSC, S], F32))
    sp2 = asb(nc.sbuf_tensor("sp2", [NSC, S], F32))
    ast0 = contextlib.ExitStack()
    xbf = ast0.enter_context(nc.sbuf_tensor("xbf", [128, DC * S], BF16))
    wi_s = ast0.enter_context(nc.sbuf_tensor("wi_s", [128, DC * NSC], BF16))
    wg_s = ast0.enter_context(nc.sbuf_tensor("wg_s", [128, DC * NSC], BF16))

    a1 = contextlib.ExitStack()
    pz = a1.enter_context(nc.psum_tensor("pz", [NSC, S], F32))
    pg = a1.enter_context(nc.psum_tensor("pg", [NSC, S], F32))

    @blk.gpsimd
    def _(g):
        for k in range(DC):
            sLG.inc(g.dma_start(out=xbf[:, k * S:(k + 1) * S],
                                in_=xT[k * 128:(k + 1) * 128, :]), 16)
        for k in range(DC):
            sLG.inc(g.dma_start(out=wi_s[:, k * NSC:(k + 1) * NSC],
                                in_=wiT[k * 128:(k + 1) * 128, :]), 16)
        for k in range(DC):
            sLG.inc(g.dma_start(out=wg_s[:, k * NSC:(k + 1) * NSC],
                                in_=wgT[k * 128:(k + 1) * 128, :]), 16)
        sLG.inc(g.dma_start(out=wdt_s[:], in_=wdtT[:]), 16)
        for k in range(KST):
            sLG.inc(g.dma_start(out=wb_s[:, k * 128:(k + 1) * 128], in_=wbrep[k]), 16)
        for k in range(KST):
            sLG.inc(g.dma_start(out=wc_s[:, k * 128:(k + 1) * 128], in_=wcrep[k]), 16)
        sLG.inc(g.dma_start(out=wo_s[:], in_=woT[:]), 16)
    tok_gload = sLG.n

    @blk.sync
    def _(s):
        sSm.inc(s.dma_start(out=alog_s[:], in_=alog[:]), 16)
        sSm.inc(s.dma_start(out=bdt_s[:], in_=bdt[:]), 16)
        sSm.inc(s.dma_start(out=wn_s[:], in_=wnorm[:]), 16)
    tok_sload = sSm.n

    @VEC
    def _(v):
        v.memset(ones1[:], 1.0)
        v.memset(onesc[:], 1.0)
        v.memset(onescb[:], 1.0)
        v.memset(epsc[:], EPS)
        v.memset(yacc[:], 0.0)
        v.memset(pbh[:], 0.5)
        v.memset(pbm[:], -0.5)
    tok_init = sV.n

    if stage_n == -4:
        emit_taps([])
        ast0.close(); a1.close(); ast.close(); stPF.close(); st.close()
        return nc, []

    @blk.tensor
    def _(t):
        t.wait_ge(sLG.h, tok_gload)
        for ts in range(NTS):
            for k in range(DC):
                last = t.matmul(pz[:, ts * TS:(ts + 1) * TS],
                                wi_s[:, k * NSC:(k + 1) * NSC],
                                xbf[:, k * S + ts * TS:k * S + (ts + 1) * TS],
                                start=(k == 0), stop=(k == DC - 1))
            sMM.inc(last)
        for ts in range(NTS):
            for k in range(DC):
                last = t.matmul(pg[:, ts * TS:(ts + 1) * TS],
                                wg_s[:, k * NSC:(k + 1) * NSC],
                                xbf[:, k * S + ts * TS:k * S + (ts + 1) * TS],
                                start=(k == 0), stop=(k == DC - 1))
            sMM.inc(last)
    tok_zg = sMM.n

    vbase_zf = sV.n

    @VEC
    def _(v):
        for ts in range(NTS):
            v.wait_ge(sMM.h, tok_zg - 8 + ts + 1)
            v.tensor_copy(out=zf[:, ts * TS:(ts + 1) * TS],
                          in_=pz[:, ts * TS:(ts + 1) * TS])
    tok_zf = sV.n

    @SCA
    def _(sc):
        for ts in range(NTS):
            sc.wait_ge(sV.h, vbase_zf + ts + 1)
            sc.copy(out=zbf[:, ts * TS:(ts + 1) * TS],
                    in_=pz[:, ts * TS:(ts + 1) * TS])
        for ts in range(NTS):
            sc.wait_ge(sMM.h, tok_zg - 4 + ts + 1)
            sc.activation(out=gsil[:, ts * TS:(ts + 1) * TS],
                                 in_=pg[:, ts * TS:(ts + 1) * TS], func=AF.Silu)
        sc.wait_ge(sSm.h, tok_sload)
        sc.activation(out=nega[:], in_=alog_s[:], func=AF.Exp)
    tok_aevac = sA.n

    @VEC
    def _(v):
        v.wait_ge(sA.h, tok_aevac)
        v.tensor_scalar_mul(out=nega[:], in0=nega[:], scalar1=-1.0)
    tok_nega = sV.n

    if stage_n == -3:
        emit_taps(["zf", "gsil"]) if False else None
        dbg_tensors.update(dict(zf=(zf, [NSC, S], F32), gsil=(gsil, [NSC, S], F32)))
        emit_taps(["zf", "gsil"])
        ast0.close(); a1.close(); ast.close(); stPF.close(); st.close()
        return nc, sorted(dbg_tensors.keys() & dbg)

    a1.close()
    a2 = contextlib.ExitStack()
    pdt = a2.enter_context(nc.psum_tensor("pdt", [NSC, S], F32))

    @blk.tensor
    def _(t):
        t.wait_ge(sA.h, tok_aevac)
        t.wait_ge(sV.h, tok_zf)
        for ts in range(NTS):
            last = t.matmul(pdt[:, ts * TS:(ts + 1) * TS], wdt_s[:],
                            zbf[:, ts * TS:(ts + 1) * TS], start=True, stop=True)
        sMM.inc(last)
    tok_dtmm = sMM.n

    @SCA
    def _(sc):
        sc.wait_ge(sMM.h, tok_dtmm)
        sc.activation(out=sp1[:], in_=pdt[:], func=AF.Relu, bias=bdt_s[:])
        sc.activation(out=sp2[:], in_=pdt[:], func=AF.Abs, bias=bdt_s[:])
        sc.wait_ge(sA.h, sA.n)
        sc.activation(out=sp2[:], in_=sp2[:], func=AF.Exp, scale=-1.0)
        sc.wait_ge(sA.h, sA.n)
        sc.activation(out=sp2[:], in_=sp2[:], func=AF.Ln, bias=1.0)
    tok_sp = sA.n

    @VEC
    def _(v):
        v.wait_ge(sA.h, tok_sp)
        v.tensor_add(out=dt_s[:], in0=sp1[:], in1=sp2[:])
        v.tensor_mul(out=w_s[:], in0=dt_s[:], in1=zf[:])
    tok_dt = sV.n

    if stage_n == -2:
        dbg_tensors.update(dict(dt_s=(dt_s, [NSC, S], F32)))
        emit_taps(["dt_s"])
        a2.close(); ast0.close(); ast.close(); stPF.close(); st.close()
        return nc, sorted(dbg_tensors.keys() & dbg)

    a2.close()
    a3 = contextlib.ExitStack()
    pb = a3.enter_context(nc.psum_tensor("pb", [NSC, S], F32))
    pc = a3.enter_context(nc.psum_tensor("pc", [NSC, S], F32))

    vbase_k = sV.n

    def _kcum(k):
        return 0 if k == 0 else 4 + 5 * (k - 1)

    def tok_uk(k):
        return vbase_k + _kcum(k) + 2

    def tok_pk(k):
        return vbase_k + _kcum(k) + (4 if k == 0 else 5)

    mbase_k = sMM.n

    def tok_bmm(k):
        return mbase_k + 2 * k + 1

    def tok_cmm(k):
        return mbase_k + 2 * k + 2

    @blk.tensor
    def _(t):
        t.wait_ge(sV.h, tok_dt)
        for k in range(KST):
            if k > 0:
                t.wait_ge(sV.h, tok_uk(k - 1))
            for ts in range(NTS):
                last = t.matmul(pb[:, ts * TS:(ts + 1) * TS],
                                wb_s[:, k * 128:(k + 1) * 128],
                                zbf[:, ts * TS:(ts + 1) * TS], start=True, stop=True)
            sMM.inc(last)
            if k > 0:
                t.wait_ge(sV.h, tok_pk(k - 1))
            for ts in range(NTS):
                last = t.matmul(pc[:, ts * TS:(ts + 1) * TS],
                                wc_s[:, k * 128:(k + 1) * 128],
                                zbf[:, ts * TS:(ts + 1) * TS], start=True, stop=True)
            sMM.inc(last)

    @VEC
    def _(v):
        v.wait_ge(sV.h, tok_nega)
        for k in range(KST):
            v.wait_ge(sMM.h, tok_bmm(k))
            v.tensor_scalar(out=ak[:], in0=dt_s[:], scalar1=nega[:, k:k + 1],
                            scalar2=1.0, op0=ALU.mult, op1=ALU.add)
            v.tensor_mul(out=uk[:], in0=w_s[:], in1=pb[:])
            v.tensor_tensor_scan(out=stk[:], data0=ak[:], data1=uk[:],
                                 initial=0.0, op0=ALU.mult, op1=ALU.add)
            v.wait_ge(sMM.h, tok_cmm(k))
            if k == 0:
                v.tensor_mul(out=yacc[:], in0=stk[:], in1=pc[:])
            else:
                v.tensor_mul(out=ptmp[:], in0=stk[:], in1=pc[:])
                v.tensor_add(out=yacc[:], in0=yacc[:], in1=ptmp[:])
        v.tensor_mul(out=ybf[:], in0=yacc[:], in1=gsil[:])
    tok_ybf = sV.n
    assert sV.n == vbase_k + _kcum(KST) + 1, (sV.n, vbase_k, _kcum(KST))

    if stage_n == -1:
        dbg_tensors.update(dict(ybf=(ybf, [NSC, S], BF16)))
        emit_taps(["ybf"])
        a3.close(); ast0.close(); ast.close(); stPF.close(); st.close()
        return nc, sorted(dbg_tensors.keys() & dbg)

    ast0.close()
    aw = contextlib.ExitStack()
    wmean = aw.enter_context(nc.sbuf_tensor("wmean", [128, 6 * 2048], F32))
    wabs = aw.enter_context(nc.sbuf_tensor("wabs", [128, 2048], F32))

    w0_tiles = []
    for j in range(HC):
        for vg in range(2):
            w0_tiles.append((wgvt[j, vg], 2048))
    for dg in range(2):
        for j in range(HC):
            w0_tiles.append((woutT[dg, j], 1024))

    abase_w0 = sA.n
    vbase_w0 = sV.n

    @SCA
    def _(sc):
        for i, (_src, width) in enumerate(w0_tiles):
            if i % 2:
                continue
            e = i // 2
            sl = 2 * (e % 3)
            sc.wait_ge(rW0.sems[sl].h, 16 * (e // 3 + 1))
            sc.activation(out=wabs[:, 0:width],
                          in_=wmean[:, sl * 2048:sl * 2048 + width],
                          func=AF.Abs, accum_out=accW[:, i:i + 1])
    tok_w0acc = sA.n

    @VEC
    def _(v):
        for i, (_src, width) in enumerate(w0_tiles):
            if i % 2 == 0:
                continue
            o = i // 2
            sl = 2 * (o % 3) + 1
            v.wait_ge(rW0.sems[sl].h, 16 * (o // 3 + 1))
            v.tensor_reduce(out=accW[:, i:i + 1],
                            in_=wmean[:, sl * 2048:sl * 2048 + width],
                            axis=AX.X, op=ALU.add, apply_absolute_value=True)
    tok_w0accv = sV.n

    @blk.sync
    def _(s):
        s.wait_ge(sMM.h, tok_zg)
        for i, (src, width) in enumerate(w0_tiles):
            if i % 2:
                continue
            e = i // 2
            sl = 2 * (e % 3)
            if e >= 3:
                s.wait_ge(sA.h, abase_w0 + (e - 3) + 1)
            rW0.inc(sl, s.dma_start(out=wmean[:, sl * 2048:sl * 2048 + width],
                                    in_=src))

    @blk.gpsimd
    def _(g):
        g.wait_ge(sMM.h, tok_zg)
        for i, (src, width) in enumerate(w0_tiles):
            if i % 2 == 0:
                continue
            o = i // 2
            sl = 2 * (o % 3) + 1
            if o >= 3:
                g.wait_ge(sV.h, vbase_w0 + (o - 3) + 1)
            rW0.inc(sl, g.dma_start(out=wmean[:, sl * 2048:sl * 2048 + width],
                                    in_=src))

    a3.close()
    a4 = contextlib.ExitStack()
    pw0 = a4.enter_context(nc.psum_tensor("pw0", [1, 2], F32))
    pw1 = a4.enter_context(nc.psum_tensor("pw1", [128, 2], F32))
    redW = a4.enter_context(nc.sbuf_tensor("redW", [128, 2], F32))
    rowW = a4.enter_context(nc.sbuf_tensor("rowW", [1, 2], F32))

    @VEC
    def _(v):
        v.wait_ge(sA.h, tok_w0acc)
        v.tensor_reduce(out=redW[:, 0:1], in_=accW[:, 0:128], axis=AX.X, op=ALU.add)
        v.tensor_reduce(out=redW[:, 1:2], in_=accW[:, 128:256], axis=AX.X,
                               op=ALU.add)
    tok_redW = sV.n

    @blk.tensor
    def _(t):
        t.wait_ge(sV.h, tok_redW)
        t.matmul(pw0[:, 0:1], onesc[:], redW[:, 0:1], start=True, stop=True)
        sMM.inc(t.matmul(pw0[:, 1:2], onesc[:], redW[:, 1:2], start=True, stop=True))
    tok_w0mm = sMM.n

    @VEC
    def _(v):
        v.wait_ge(sMM.h, tok_w0mm)
        v.tensor_scalar(out=rowW[:, 0:1], in0=pw0[:, 0:1], scalar1=1.0 / (2 * H * D),
                        scalar2=EPS, op0=ALU.mult, op1=ALU.max)
        v.tensor_scalar(out=rowW[:, 1:2], in0=pw0[:, 1:2], scalar1=1.0 / (D * H),
                        scalar2=EPS, op0=ALU.mult, op1=ALU.max)
        v.tensor_scalar_mul(out=mwc1[:], in0=rowW[:, 0:1], scalar1=1.0 / QB)
        v.tensor_scalar_mul(out=mwc2[:], in0=rowW[:, 1:2], scalar1=1.0 / QB)
        v.reciprocal(out=rowW[:], in_=rowW[:])
    tok_w0row = sV.n

    @blk.tensor
    def _(t):
        t.wait_ge(sV.h, tok_w0row)
        sMM.inc(t.matmul(pw1[:], ones1[:], rowW[:], start=True, stop=True))
    tok_w0bc = sMM.n

    @SCA
    def _(sc):
        sc.wait_ge(sMM.h, tok_w0bc)
        sc.copy(out=sgv128[:], in_=pw1[:, 0:1])
        sc.copy(out=sout128[:], in_=pw1[:, 1:2])
    tok_w0done = sA.n
    a4.close()

    dbg_tensors.update(dict(
        zf=(zf, [NSC, S], F32), gsil=(gsil, [NSC, S], F32),
        dt_s=(dt_s, [NSC, S], F32), yacc=(yacc, [NSC, S], F32),
        ybf=(ybf, [NSC, S], BF16), sgv128=(sgv128, [128, 1], F32),
        mwc1=(mwc1, [1, 1], F32), mwc2=(mwc2, [1, 1], F32),
    ))
    if stage_n == 0:
        emit_taps(["zf", "gsil", "dt_s", "yacc", "ybf", "sgv128", "mwc1", "mwc2"])
        aw.close(); ast0.close() if False else None; ast.close(); stPF.close()
        st.close()
        return nc, sorted(dbg_tensors.keys() & dbg)

    aw.close()
    ast.close()

    q1 = stPF.enter_context(nc.sbuf_tensor("q1", [128, NTS * DC * TS], BF16))
    c1b = stPF.enter_context(nc.sbuf_tensor("c1b", [128, S], F32))
    pst = contextlib.ExitStack()
    psb = pst.enter_context
    wo2_s = psb(nc.sbuf_tensor("wo2_s", [NSC, D], BF16))
    xr = psb(nc.sbuf_tensor("xr", [128, 2 * TS], F32))
    res1 = psb(nc.sbuf_tensor("res1", [128, DC * TS], F32))
    hbuf = psb(nc.sbuf_tensor("hbuf", [128, DC * TS], F32))
    paby = psb(nc.sbuf_tensor("paby", [128, DC * TS // 2], F32))
    sgn_p = psb(nc.sbuf_tensor("sgn_p", [128, DC * TS // 2], BF16))
    sqb = psb(nc.sbuf_tensor("sqb", [128, 2 * TS], BF16))
    m1t = psb(nc.sbuf_tensor("m1t", [128, TS], F32))
    trt = psb(nc.sbuf_tensor("trt", [64, TS], F32))
    mu_b = psb(nc.sbuf_tensor("mu_b", [128, TS], F32))
    rstd_b = psb(nc.sbuf_tensor("rstd_b", [128, TS], F32))
    ppst = contextlib.ExitStack()
    pwo = ppst.enter_context(nc.psum_tensor("pwo", [128, 2 * TS], F32))
    pstat = ppst.enter_context(nc.psum_tensor("pstat", [1, 2 * TS], F32))
    pbc = ppst.enter_context(nc.psum_tensor("pbc", [128, TS], F32))

    @blk.gpsimd
    def _(g):
        g.wait_ge(sA.h, tok_w0done)
        sLG.inc(g.dma_start(out=wo2_s[:], in_=woT[:]), 16)
    tok_wo2 = sLG.n

    pA_v, pA_a, pA_m = sV.n, sA.n, sMM.n
    prev_q1ts = None
    prev_xadd_end = None
    for ts in range(NTS):
        abase_evac = sA.n
        @blk.tensor
        def _(t, ts=ts, abase_evac=abase_evac):
            t.wait_ge(sLG.h, tok_wo2)
            t.wait_ge(sV.h, tok_ybf)
            for d in range(DC):
                if d >= 2:
                    t.wait_ge(sA.h, abase_evac + (d - 1))
                sMM.inc(t.matmul(pwo[:, (d % 2) * TS:(d % 2 + 1) * TS],
                                 wo2_s[:, d * 128:(d + 1) * 128],
                                 ybf[:, ts * TS:(ts + 1) * TS], start=True, stop=True))
        tok_womm = sMM.n

        @SCA
        def _(sc, tok_womm=tok_womm, prev_q1ts=prev_q1ts):
            if prev_q1ts is not None:
                sc.wait_ge(sV.h, prev_q1ts)
            for d in range(DC):
                sc.wait_ge(sMM.h, tok_womm - DC + d + 1)
                sc.copy(out=res1[:, d * TS:(d + 1) * TS],
                        in_=pwo[:, (d % 2) * TS:(d % 2 + 1) * TS])
        tok_evac = sA.n

        vbase_x = sV.n

        xr_toks = {}

        @blk.sync
        def _(s, ts=ts, vbase_x=vbase_x, prev_xadd_end=prev_xadd_end):
            if ts == 0:
                s.wait_ge(sV.h, pA_v)
                s.wait_ge(sA.h, pA_a)
                s.wait_ge(sMM.h, pA_m)
            for d in range(DC):
                if d >= 2:
                    s.wait_ge(sV.h, vbase_x + (d - 1))
                elif prev_xadd_end is not None:
                    s.wait_ge(sV.h, prev_xadd_end)
                xr_toks[d] = rPX.inc(d, s.dma_start(
                    out=xr[:, (d % 2) * TS:(d % 2 + 1) * TS],
                    in_=xT[d * 128:(d + 1) * 128, ts * TS:(ts + 1) * TS]))

        @VEC
        def _(v, tok_evac=tok_evac):
            for d in range(DC):
                v.wait_ge(*xr_toks[d])
                v.wait_ge(sA.h, tok_evac - DC + d + 1)
                v.tensor_add(out=res1[:, d * TS:(d + 1) * TS],
                                    in0=res1[:, d * TS:(d + 1) * TS],
                                    in1=xr[:, (d % 2) * TS:(d % 2 + 1) * TS])
        tok_res1 = sV.n
        prev_xadd_end = tok_res1

        abase_sq = sA.n
        mbase_sq = sMM.n
        @SCA
        def _(sc, tok_res1=tok_res1, mbase_sq=mbase_sq):
            sc.wait_ge(sV.h, tok_res1)
            for d in range(DC):
                if d >= 2:
                    sc.wait_ge(sMM.h, mbase_sq + DC + d - 1)
                sc.activation(out=sqb[:, (d % 2) * TS:(d % 2 + 1) * TS],
                                     in_=res1[:, d * TS:(d + 1) * TS], func=AF.Square)
        tok_sq = sA.n

        @blk.tensor
        def _(t, tok_res1=tok_res1, abase_sq=abase_sq):
            t.wait_ge(sV.h, tok_res1)
            for d in range(DC):
                sMM.inc(t.matmul(pstat[:, 0:TS], onesc[:], res1[:, d * TS:(d + 1) * TS],
                                 start=(d == 0), stop=(d == DC - 1)))
            for d in range(DC):
                t.wait_ge(sA.h, abase_sq + d + 1)
                sMM.inc(t.matmul(pstat[:, TS:2 * TS], onescb[:],
                                 sqb[:, (d % 2) * TS:(d % 2 + 1) * TS],
                                 start=(d == 0), stop=(d == DC - 1)))
        tok_st12 = sMM.n

        @VEC
        def _(v, tok_st12=tok_st12):
            v.wait_ge(sMM.h, tok_st12)
            v.tensor_scalar_mul(out=scr1[:], in0=pstat[:, 0:TS], scalar1=1.0 / D)
            v.tensor_scalar_mul(out=scr2[:], in0=pstat[:, TS:2 * TS], scalar1=1.0 / D)
            v.tensor_mul(out=scr4[:], in0=scr1[:], in1=scr1[:])
            v.tensor_sub(out=scr2[:], in0=scr2[:], in1=scr4[:])
        tok_var = sV.n

        @SCA
        def _(sc, tok_var=tok_var):
            sc.wait_ge(sV.h, tok_var)
            sc.activation(out=scr2[:], in_=scr2[:], func=AF.Sqrt, bias=epsc[:])
        tok_sqv = sA.n

        @VEC
        def _(v, tok_sqv=tok_sqv):
            v.wait_ge(sA.h, tok_sqv)
            v.reciprocal(out=scr2[:], in_=scr2[:])
        tok_rstd = sV.n

        @blk.tensor
        def _(t, tok_rstd=tok_rstd):
            t.wait_ge(sV.h, tok_rstd)
            sMM.inc(t.matmul(pbc[:], ones1[:], scr1[:], start=True, stop=True))
        tok_bc1 = sMM.n

        @SCA
        def _(sc, tok_bc1=tok_bc1):
            sc.wait_ge(sMM.h, tok_bc1)
            sc.copy(out=mu_b[:], in_=pbc[:])
        tok_mub = sA.n

        @blk.tensor
        def _(t, tok_mub=tok_mub):
            t.wait_ge(sA.h, tok_mub)
            sMM.inc(t.matmul(pbc[:], ones1[:], scr2[:], start=True, stop=True))
        tok_bc2 = sMM.n

        @SCA
        def _(sc, tok_bc2=tok_bc2):
            sc.wait_ge(sMM.h, tok_bc2)
            sc.copy(out=rstd_b[:], in_=pbc[:])
        tok_rstdb = sA.n

        res1_3 = res1.ap().rearrange("p (c w) -> p c w", c=DC)
        hbuf_3 = hbuf.ap().rearrange("p (c w) -> p c w", c=DC)

        @VEC
        def _(v, tok_rstdb=tok_rstdb):
            v.wait_ge(sA.h, tok_rstdb)
            v.tensor_tensor(out=res1_3, in0=res1_3, in1=rep_bcast(mu_b, DC, TS),
                            op=ALU.subtract)
            v.tensor_tensor(out=res1_3, in0=res1_3,
                                   in1=rep_bcast(rstd_b, DC, TS), op=ALU.mult)
        tok_ln1 = sV.n

        @SCA
        def _(sc, ts=ts, tok_ln1=tok_ln1):
            sc.wait_ge(sV.h, tok_ln1)
            for d in range(DC):
                sLn1.inc(sc.dma_start(out=ln1_sp[d, :, ts * TS:(ts + 1) * TS],
                                      in_=res1[:, d * TS:(d + 1) * TS]), 16)
        tok_ln1sp = sLn1.n

        abase_sq2 = sA.n
        mbase_sq2 = sMM.n
        @SCA
        def _(sc, tok_ln1=tok_ln1, mbase_sq2=mbase_sq2):
            for d in range(DC):
                if d >= 2:
                    sc.wait_ge(sMM.h, mbase_sq2 + d - 1)
                sc.activation(out=sqb[:, (d % 2) * TS:(d % 2 + 1) * TS],
                                     in_=res1[:, d * TS:(d + 1) * TS], func=AF.Square)
        tok_sq2 = sA.n

        @blk.tensor
        def _(t, abase_sq2=abase_sq2):
            for d in range(DC):
                t.wait_ge(sA.h, abase_sq2 + d + 1)
                sMM.inc(t.matmul(pstat[:, TS:2 * TS], onescb[:],
                                 sqb[:, (d % 2) * TS:(d % 2 + 1) * TS],
                                 start=(d == 0), stop=(d == DC - 1)))
        tok_ms = sMM.n

        @VEC
        def _(v, tok_ms=tok_ms):
            v.wait_ge(sMM.h, tok_ms)
            v.tensor_scalar_mul(out=scr2[:], in0=pstat[:, TS:2 * TS],
                                       scalar1=1.0 / D)
        tok_msr = sV.n

        @SCA
        def _(sc, tok_msr=tok_msr):
            sc.wait_ge(sV.h, tok_msr)
            sc.activation(out=scr2[:], in_=scr2[:], func=AF.Sqrt, bias=epsc[:])
        tok_sqr2 = sA.n

        @VEC
        def _(v, tok_sqr2=tok_sqr2):
            v.wait_ge(sA.h, tok_sqr2)
            v.reciprocal(out=scr2[:], in_=scr2[:])
        tok_rms = sV.n

        @blk.tensor
        def _(t, tok_rms=tok_rms):
            t.wait_ge(sV.h, tok_rms)
            sMM.inc(t.matmul(pbc[:], ones1[:], scr2[:], start=True, stop=True))
        tok_bc3 = sMM.n

        @SCA
        def _(sc, tok_bc3=tok_bc3):
            sc.wait_ge(sMM.h, tok_bc3)
            sc.copy(out=rstd_b[:], in_=pbc[:])
        tok_rmsb = sA.n

        @VEC
        def _(v, tok_rmsb=tok_rmsb):
            v.wait_ge(sA.h, tok_rmsb)
            for d in range(DC):
                v.tensor_scalar(out=hbuf[:, d * TS:(d + 1) * TS],
                                in0=res1[:, d * TS:(d + 1) * TS],
                                scalar1=wn_s[:, d:d + 1], scalar2=None, op0=ALU.mult)
            v.tensor_tensor(out=hbuf_3, in0=hbuf_3,
                                   in1=rep_bcast(rstd_b, DC, TS), op=ALU.mult)
        tok_h = sV.n

        abase_habs = sA.n
        vbase_m1 = sV.n

        @SCA
        def _(sc, tok_h=tok_h, vbase_m1=vbase_m1):
            sc.wait_ge(sV.h, tok_h)
            for d in range(DC):
                if d >= 2:
                    sc.wait_ge(sV.h, vbase_m1 + d - 1)
                sc.activation(out=paby[:, (d % 2) * TS:(d % 2 + 1) * TS],
                                     in_=hbuf[:, d * TS:(d + 1) * TS], func=AF.Abs)
        tok_habs = sA.n

        @VEC
        def _(v, abase_habs=abase_habs):
            for d in range(DC):
                v.wait_ge(sA.h, abase_habs + d + 1)
                if d == 0:
                    v.tensor_copy(out=m1t[:], in_=paby[:, 0:TS])
                else:
                    v.tensor_max(out=m1t[:], in0=m1t[:],
                                        in1=paby[:, (d % 2) * TS:(d % 2 + 1) * TS])
            for width in (64, 32):
                v.tensor_copy(out=trt[0:width, :], in_=m1t[width:2 * width, :])
                v.tensor_max(out=m1t[0:width, :], in0=m1t[0:width, :],
                             in1=trt[0:width, :])
            for width in (16, 8, 4, 2, 1):
                mask = [width + i if i < width else 0 for i in range(32)]
                v.stream_shuffle(out=trt[0:32, :], in_=m1t[0:32, :], mask=mask)
                v.tensor_max(out=m1t[0:width, :], in0=m1t[0:width, :],
                             in1=trt[0:width, :])
            v.tensor_scalar(out=scr1[:], in0=m1t[0:1, :], scalar1=EPS, scalar2=None,
                            op0=ALU.max)
            v.reciprocal(out=scr2[:], in_=scr1[:])
            v.tensor_scalar_mul(out=scr2[:], in0=scr2[:], scalar1=QB)
            v.wait_ge(sA.h, tok_w0done)
            v.tensor_scalar(out=scr1[:], in0=scr1[:], scalar1=mwc1[0:1, 0:1],
                                   scalar2=None, op0=ALU.mult)
        tok_s1 = sV.n

        @blk.tensor
        def _(t, tok_s1=tok_s1):
            t.wait_ge(sV.h, tok_s1)
            sMM.inc(t.matmul(pbc[:], ones1[:], scr2[:], start=True, stop=True))
        tok_bs1 = sMM.n

        @SCA
        def _(sc, tok_bs1=tok_bs1):
            sc.wait_ge(sMM.h, tok_bs1)
            sc.copy(out=mu_b[:], in_=pbc[:])
        tok_s1b = sA.n

        @blk.tensor
        def _(t, tok_s1b=tok_s1b):
            t.wait_ge(sA.h, tok_s1b)
            sMM.inc(t.matmul(pbc[:], ones1[:], scr1[:], start=True, stop=True))
        tok_bc1r = sMM.n

        @SCA
        def _(sc, ts=ts, tok_bc1r=tok_bc1r):
            sc.wait_ge(sMM.h, tok_bc1r)
            sc.copy(out=c1b[:, ts * TS:(ts + 1) * TS], in_=pbc[:])
        tok_c1b = sA.n

        @VEC
        def _(v, tok_s1b=tok_s1b):
            v.wait_ge(sA.h, tok_s1b)
            v.tensor_tensor(out=hbuf_3, in0=hbuf_3,
                                   in1=rep_bcast(mu_b, DC, TS), op=ALU.mult)
        tok_y1 = sV.n

        HW = DC * TS // 2
        for hh in range(2):
            @SCA
            def _(sc, hh=hh, tok_y1=tok_y1):
                sc.wait_ge(sV.h, tok_y1 if hh == 0 else sV.n)
                sc.activation(out=sgn_p[:], in_=hbuf[:, hh * HW:(hh + 1) * HW],
                              func=AF.Sign)
                sc.activation(out=paby[:], in_=hbuf[:, hh * HW:(hh + 1) * HW],
                                     func=AF.Abs)
            tok_absgn = sA.n

            @VEC
            def _(v, ts=ts, hh=hh, tok_absgn=tok_absgn, tok_ln1sp=tok_ln1sp):
                v.wait_ge(sA.h, tok_absgn)
                if hh == 0:
                    v.wait_ge(sLn1.h, tok_ln1sp)
                emit_peel(v, paby[:], res1[:, 0:HW])
                v.tensor_mul(out=res1[:, 0:HW], in0=paby[:], in1=sgn_p[:])
                v.tensor_tensor(
                    out=q1[:, ts * DC * TS + hh * HW:ts * DC * TS + (hh + 1) * HW],
                    in0=hbuf[:, hh * HW:(hh + 1) * HW], in1=res1[:, 0:HW],
                    op=ALU.subtract)
            tok_q1ts = sV.n
        prev_q1ts = tok_q1ts

    tok_q1 = sV.n
    tok_ln1_all = sLn1.n
    dbg_tensors.update(dict(q1=(q1, [128, NTS * DC * TS], BF16), c1b=(c1b, [128, S], F32)))
    if stage_n == 1:
        emit_taps(["q1", "c1b", "ybf", "sgv128", "mwc1", "mwc2"])
        ppst.close(); pst.close(); stPF.close()
        st.close()
        return nc, sorted(dbg_tensors.keys() & dbg)

    ppst.close()
    pst.close()

    f1L = contextlib.ExitStack()
    Mx = f1L.enter_context(nc.sbuf_tensor("Mx", [128, S], F32))
    scr3 = f1L.enter_context(nc.sbuf_tensor("scr3", [1, S], F32))
    f1 = contextlib.ExitStack()
    fsb = f1.enter_context
    wld = fsb(nc.sbuf_tensor("wld", [128, 3 * 2048], F32))
    qwv = fsb(nc.sbuf_tensor("qwv", [128, 2 * 2048], BF16))
    qwg = fsb(nc.sbuf_tensor("qwg", [128, 2 * 2048], BF16))
    wqt = fsb(nc.sbuf_tensor("wqt", [128, 2048], F32))
    wqs = fsb(nc.sbuf_tensor("wqs", [128, 2048], BF16))
    u_r = fsb(nc.sbuf_tensor("u_r", [128, 2 * S], F32))
    gc_t = fsb(nc.sbuf_tensor("gc_t", [128, TS], F32))
    sg_t = fsb(nc.sbuf_tensor("sg_t", [128, TS], F32))
    wo_r = fsb(nc.sbuf_tensor("wo_r", [128, 2 * 1024], F32))
    qo_t = fsb(nc.sbuf_tensor("qo_t", [128, 2 * 1024], BF16))
    woqt = fsb(nc.sbuf_tensor("woqt", [128, 1024], F32))
    woqs = fsb(nc.sbuf_tensor("woqs", [128, 1024], BF16))
    f1p = contextlib.ExitStack()
    pvg = f1p.enter_context(nc.psum_tensor("pvg", [128, 8 * TS], F32))

    @VEC
    def _(v):
        v.memset(Mx[:], 0.0)
    tok_mx0 = sV.n

    a0, v0, m0 = sA.n, sV.n, sMM.n
    TOK = {}
    a, v_, m = a0, v0, m0
    _wld_cnt = [0, 0, 0]
    for j in range(HC):
        for h, key in ((0, ("ldv", j)), (1, ("ldg", j))):
            slot = (2 * j + h) % 3
            _wld_cnt[slot] += 16
            TOK[key] = (rWLD.sems[slot].h, _wld_cnt[slot])
    _usp_cnt = [0, 0]
    for j in range(HC + 1):
        if j < HC:
            a += 2; TOK[("qv", j)] = a
            a += 2; TOK[("qg", j)] = a
        if j >= 1:
            jj = j - 1
            for ts in range(NTS):
                a += 1; TOK[("silu", jj, ts)] = a
                a += 1; TOK[("au", jj, ts)] = a
            _usp_cnt[jj % 2] += 16
            TOK[("usp", jj)] = (rUSP.sems[jj % 2].h, _usp_cnt[jj % 2])
        if j < HC:
            for h in (0, 1):
                a += 2; TOK[("woq", 2 * j + h)] = a
    for j in range(HC + 1):
        if j < HC:
            v_ += 1; TOK[("ternv", j)] = v_
            v_ += 1; TOK[("terng", j)] = v_
        if j >= 1:
            jj = j - 1
            for ts in range(NTS):
                v_ += 1; TOK[("gc", jj, ts)] = v_
                v_ += 1; TOK[("sg2", jj, ts)] = v_
                v_ += 1; TOK[("u", jj, ts)] = v_
                v_ += 1; TOK[("mx", jj, ts)] = v_
        if j < HC:
            v_ += 1; TOK[("wot", 2 * j)] = v_
            v_ += 1; TOK[("wot", 2 * j + 1)] = v_
    for j in range(HC):
        for ts in range(NTS):
            m += 1
            TOK[("mm", j, ts)] = m
    end_a, end_v, end_m = a, v_, m

    pP_v, pP_a = tok_q1, sA.n

    @blk.sync
    def _(s):
        for j in range(HC):
            if j < 2:
                s.wait_ge(sV.h, pP_v)
                s.wait_ge(sA.h, pP_a)
                s.wait_ge(sLn1.h, tok_ln1_all)
            else:
                s.wait_ge(sV.h, TOK[("terng", j - 2)])
            rWLD.inc(2 * j, s.dma_start(
                out=wld[:, (2 * j % 3) * 2048:(2 * j % 3 + 1) * 2048], in_=wgvt[j, 0]))
            if j >= 1:
                s.wait_ge(sV.h, TOK[("ternv", j - 1)])
            rWLD.inc(2 * j + 1, s.dma_start(
                out=wld[:, ((2 * j + 1) % 3) * 2048:((2 * j + 1) % 3 + 1) * 2048],
                in_=wgvt[j, 1]))

    @SCA
    def _(sc):
        sc.wait_ge(sA.h, tok_w0done)
        for j in range(HC + 1):
            if j < HC:
                sc.wait_ge(*TOK[("ldv", j)])
                if j >= 1:
                    sc.wait_ge(sV.h, TOK[("terng", j - 1)])
                sc.activation(out=wqt[:], in_=wld[:, (2 * j % 3) * 2048:(2 * j % 3 + 1) * 2048],
                              func=AF.Abs, scale=sgv128[:])
                sc.activation(out=wqs[:], in_=wld[:, (2 * j % 3) * 2048:(2 * j % 3 + 1) * 2048],
                                     func=AF.Sign, scale=sgv128[:])
                sc.wait_ge(*TOK[("ldg", j)])
                sc.wait_ge(sV.h, TOK[("ternv", j)])
                sc.activation(out=wqt[:], in_=wld[:, ((2 * j + 1) % 3) * 2048:((2 * j + 1) % 3 + 1) * 2048],
                              func=AF.Abs, scale=sgv128[:])
                sc.activation(out=wqs[:], in_=wld[:, ((2 * j + 1) % 3) * 2048:((2 * j + 1) % 3 + 1) * 2048],
                                     func=AF.Sign, scale=sgv128[:])
            if j >= 1:
                jj = j - 1
                r = jj % 2
                for ts in range(NTS):
                    sc.wait_ge(sV.h, TOK[("gc", jj, ts)])
                    sc.activation(out=sg_t[:], in_=gc_t[:], func=AF.Silu)
                    sc.wait_ge(sV.h, TOK[("u", jj, ts)])
                    sc.activation(out=gc_t[:],
                                         in_=u_r[:, r * S + ts * TS:r * S + (ts + 1) * TS],
                                         func=AF.Abs)
            for h in ((0, 1) if j < HC else ()):
                i = 2 * j + h
                wl = i % 2
                if i >= 2:
                    sc.wait_ge(sV.h, TOK[("wot", i - 2)])
                    dgp, jwp = divmod(i - 2, HC)
                    rQoS.inc(i - 2, sc.dma_start(
                        out=qo_sp[dgp, jwp // 4, :, (jwp % 4) * 1024:(jwp % 4 + 1) * 1024],
                        in_=qo_t[:, wl * 1024:(wl + 1) * 1024]))
                sc.wait_ge(rWO.sems[wl].h, 16 * (i // 2 + 1))
                if i >= 1:
                    sc.wait_ge(sV.h, TOK[("wot", i - 1)])
                sc.activation(out=woqt[:],
                              in_=wo_r[:, wl * 1024:(wl + 1) * 1024],
                              func=AF.Abs, scale=sout128[:])
                sc.activation(out=woqs[:],
                              in_=wo_r[:, wl * 1024:(wl + 1) * 1024],
                              func=AF.Sign, scale=sout128[:])
            if j == HC:
                for i in (2 * HC - 2, 2 * HC - 1):
                    wl = i % 2
                    sc.wait_ge(sV.h, TOK[("wot", i)])
                    dgp, jwp = divmod(i, HC)
                    rQoS.inc(i, sc.dma_start(
                        out=qo_sp[dgp, jwp // 4, :, (jwp % 4) * 1024:(jwp % 4 + 1) * 1024],
                        in_=qo_t[:, wl * 1024:(wl + 1) * 1024]))
    assert sA.n == end_a

    @VEC
    def _(v):
        for j in range(HC + 1):
            if j < HC:
                v.wait_ge(sA.h, TOK[("qv", j)])
                if j >= 2:
                    v.wait_ge(sMM.h, TOK[("mm", j - 2, NTS - 1)])
                v.scalar_tensor_tensor(
                    out=qwv[:, (j % 2) * 2048:(j % 2 + 1) * 2048], in0=wqt[:],
                    scalar=0.5, in1=wqs[:], op0=ALU.is_ge, op1=ALU.mult)
                v.wait_ge(sA.h, TOK[("qg", j)])
                v.scalar_tensor_tensor(
                    out=qwg[:, (j % 2) * 2048:(j % 2 + 1) * 2048], in0=wqt[:],
                    scalar=0.5, in1=wqs[:], op0=ALU.is_ge, op1=ALU.mult)
            if j >= 1:
                jj = j - 1
                r = jj % 2
                for ts in range(NTS):
                    q = (jj * NTS + ts) % 4
                    v.wait_ge(sMM.h, TOK[("mm", jj, ts)])
                    v.tensor_mul(out=gc_t[:],
                                 in0=pvg[:, (2 * q + 1) * TS:(2 * q + 2) * TS],
                                 in1=c1b[:, ts * TS:(ts + 1) * TS])
                    v.wait_ge(sA.h, TOK[("silu", jj, ts)])
                    v.tensor_mul(out=sg_t[:], in0=sg_t[:],
                                 in1=c1b[:, ts * TS:(ts + 1) * TS])
                    if jj >= 2:
                        v.wait_ge(*TOK[("usp", jj - 2)])
                        v.wait_ge(sA.h, TOK[("au", jj - 2, ts)])
                    v.tensor_mul(out=u_r[:, r * S + ts * TS:r * S + (ts + 1) * TS],
                                 in0=pvg[:, 2 * q * TS:(2 * q + 1) * TS],
                                 in1=sg_t[:])
                    v.wait_ge(sA.h, TOK[("au", jj, ts)])
                    v.tensor_max(out=Mx[:, ts * TS:(ts + 1) * TS],
                                 in0=Mx[:, ts * TS:(ts + 1) * TS], in1=gc_t[:])
            for h in ((0, 1) if j < HC else ()):
                i = 2 * j + h
                wl = i % 2
                v.wait_ge(sA.h, TOK[("woq", i)])
                if i >= 2:
                    v.wait_ge(rQoS.sems[wl].h, 16 * (i // 2))
                v.scalar_tensor_tensor(out=qo_t[:, wl * 1024:(wl + 1) * 1024],
                                       in0=woqt[:], scalar=0.5, in1=woqs[:],
                                       op0=ALU.is_ge, op1=ALU.mult)
    assert sV.n == end_v

    @blk.tensor
    def _(t):
        for j in range(HC):
            r = j % 2
            t.wait_ge(sV.h, TOK[("terng", j)])
            for ts in range(NTS):
                q = (j * NTS + ts) % 4
                if j >= 1:
                    t.wait_ge(sV.h, TOK[("u", j - 1, ts)])
                for k in range(DC):
                    t.matmul(pvg[:, 2 * q * TS:(2 * q + 1) * TS],
                             qwv[:, r * 2048 + k * 128:r * 2048 + (k + 1) * 128],
                             q1[:, (ts * DC + k) * TS:(ts * DC + k + 1) * TS],
                             start=(k == 0), stop=(k == DC - 1))
                for k in range(DC):
                    last = t.matmul(pvg[:, (2 * q + 1) * TS:(2 * q + 2) * TS],
                                    qwg[:, r * 2048 + k * 128:r * 2048 + (k + 1) * 128],
                                    q1[:, (ts * DC + k) * TS:(ts * DC + k + 1) * TS],
                                    start=(k == 0), stop=(k == DC - 1))
                sMM.inc(last)
    assert sMM.n == end_m

    @blk.gpsimd
    def _(g):
        for j in range(HC + 1):
            for h in ((0, 1) if j < HC else ()):
                i = 2 * j + h
                if i < 2:
                    g.wait_ge(sV.h, pP_v)
                    g.wait_ge(sA.h, pP_a)
                    g.wait_ge(sLn1.h, tok_ln1_all)
                else:
                    g.wait_ge(sA.h, TOK[("woq", i - 2)])
                dg, jw = divmod(i, HC)
                rWO.inc(i, g.dma_start(
                    out=wo_r[:, (i % 2) * 1024:(i % 2 + 1) * 1024],
                    in_=woutT[dg, jw]))
            if j >= 1:
                jj = j - 1
                g.wait_ge(sV.h, TOK[("mx", jj, NTS - 1)])
                rUSP.inc(jj, g.dma_start(out=u_sp[jj],
                                         in_=u_r[:, (jj % 2) * S:(jj % 2 + 1) * S]))

    tok_woq = end_a
    tok_qodone = end_v
    tok_qosp = [s.tok() for s in rQoS.sems]

    f1p.close()
    f1.close()
    f15 = contextlib.ExitStack()
    trt2 = f15.enter_context(nc.sbuf_tensor("trt2", [64, S], F32))
    f1p2 = contextlib.ExitStack()
    pbc2 = f1p2.enter_context(nc.psum_tensor("pbc2", [128, S], F32))

    @VEC
    def _(v):
        v.wait_ge(sV.h, end_v)
        for width in (64, 32):
            v.tensor_copy(out=trt2[0:width, :], in_=Mx[width:2 * width, :])
            v.tensor_max(out=Mx[0:width, :], in0=Mx[0:width, :], in1=trt2[0:width, :])
        for width in (16, 8, 4, 2, 1):
            mask = [width + i if i < width else 0 for i in range(32)]
            v.stream_shuffle(out=trt2[0:32, :], in_=Mx[0:32, :], mask=mask)
            v.tensor_max(out=Mx[0:width, :], in0=Mx[0:width, :], in1=trt2[0:width, :])
        v.tensor_scalar(out=scr3[:], in0=Mx[0:1, :], scalar1=EPS, scalar2=None,
                        op0=ALU.max)
        v.reciprocal(out=Mx[0:1, :], in_=scr3[:])
        v.tensor_scalar_mul(out=Mx[0:1, :], in0=Mx[0:1, :], scalar1=QB)
        v.wait_ge(sA.h, tok_w0done)
        v.tensor_scalar(out=scr3[:], in0=scr3[:], scalar1=mwc2[0:1, 0:1],
                               scalar2=None, op0=ALU.mult)
    tok_s2row = sV.n

    @blk.tensor
    def _(t):
        t.wait_ge(sV.h, tok_s2row)
        for ts in range(NTS):
            last = t.matmul(pbc2[:, ts * TS:(ts + 1) * TS], ones1[:],
                            Mx[0:1, ts * TS:(ts + 1) * TS], start=True, stop=True)
        sMM.inc(last)
    tok_bs2 = sMM.n

    @SCA
    def _(sc):
        sc.wait_ge(sMM.h, tok_bs2)
        sc.copy(out=s2b[:], in_=pbc2[:])
    tok_s2b = sA.n

    @blk.tensor
    def _(t):
        t.wait_ge(sA.h, tok_s2b)
        for ts in range(NTS):
            last = t.matmul(pbc2[:, ts * TS:(ts + 1) * TS], ones1[:],
                            scr3[0:1, ts * TS:(ts + 1) * TS], start=True, stop=True)
        sMM.inc(last)
    tok_bc2m = sMM.n

    @SCA
    def _(sc):
        sc.wait_ge(sMM.h, tok_bc2m)
        sc.copy(out=c2b[:], in_=pbc2[:])
    tok_c2b = sA.n

    dbg_tensors.update(dict(Mx=(Mx, [128, S], F32), s2b=(s2b, [128, S], F32),
                            c2b=(c2b, [128, S], F32)))
    if stage_n == 2:
        emit_taps(["Mx", "s2b", "c2b", "q1", "c1b"])
        f1p2.close(); f15.close(); f1L.close(); stPF.close()
        st.close()
        return nc, sorted(dbg_tensors.keys() & dbg)

    f1p2.close()
    f15.close()
    f1L.close()
    stPF.close()

    f2 = contextlib.ExitStack()
    f2b = f2.enter_context
    qo_r = f2b(nc.sbuf_tensor("qo_r", [128, 2 * 4096], BF16))
    q2_r = f2b(nc.sbuf_tensor("q2_r", [128, (HC // 4) * 2048], BF16))
    o_ts = f2b(nc.sbuf_tensor("o_ts", [128, DC * TS], F32))
    lbuf = f2b(nc.sbuf_tensor("lbuf", [128, DC * TS], F32))
    sqb2 = f2b(nc.sbuf_tensor("sqb2", [128, 2 * TS], BF16))
    mu2b = f2b(nc.sbuf_tensor("mu2b", [128, TS], F32))
    rstd2b = f2b(nc.sbuf_tensor("rstd2b", [128, TS], F32))
    ub2 = f2b(nc.sbuf_tensor("ub2", [128, 4 * TS], F32))
    zt2 = f2b(nc.sbuf_tensor("zt2", [128, 4 * TS], F32))
    mq2 = f2b(nc.sbuf_tensor("mq2", [128, 4 * TS], F32))
    q2c = f2b(nc.sbuf_tensor("q2c", [128, 4 * TS], BF16))
    f2p = contextlib.ExitStack()
    pout = f2p.enter_context(nc.psum_tensor("pout", [128, 8 * TS], F32))

    o_3 = o_ts.ap().rearrange("p (c w) -> p c w", c=DC)

    HYB_S = [-64.0, -32.0, -16.0, -8.0, -4.0, -2.0, -1.0, -0.5]

    BND = ([(sV.h, tok_s2row), (sV.h, tok_qodone), (sA.h, tok_woq),
            (sA.h, tok_c2b), (sLn1.h, tok_ln1_all)]
           + [TOK[("usp", HC - 2)], TOK[("usp", HC - 1)]] + list(tok_qosp))

    vfin = {}
    ub2_toks = {}
    st_tok = {}
    NP = HC // 4

    def emit_q2_slice(ts, mid_cb=None):
        vb, ab = sV.n, sA.n

        INJ = 12

        def soff(p):
            return 8 if (mid_cb is not None and p >= INJ) else 0
        for u in range(HC // 2):
            pp, lane = divmod(u, 2)
            vfin[(ts, u)] = vb + 20 * pp + 19 + lane
        sl = slice(ts * TS, (ts + 1) * TS)

        @blk.gpsimd
        def _(g, ts=ts):
            for u in range(HC // 2):
                lane = u % 2
                if ts == 0:
                    g.wait_ge(*TOK[("usp", 2 * u)])
                    g.wait_ge(*TOK[("usp", 2 * u + 1)])
                    if u < 2:
                        for h_, v_ in BND:
                            g.wait_ge(h_, v_)
                if ts > 0 or u >= 2:
                    pu = (ts, u - 2) if u >= 2 else (ts - 1, u + HC // 2 - 2)
                    g.wait_ge(sV.h, vfin[pu])
                for half in range(2):
                    tk = rUB.inc(u, g.dma_start(
                        out=ub2[:, (2 * lane + half) * TS:(2 * lane + half + 1) * TS],
                        in_=u_sp[2 * u + half, :, ts * TS:(ts + 1) * TS]))
                ub2_toks[(ts, u)] = tk

        @VEC
        def _(v, ts=ts, vb=vb, ab=ab, sl=sl):
            s2bc = s2b.ap()[:, sl].rearrange("p (c w) -> p c w", c=1) \
                                  .broadcast_to((128, 2, TS))
            for p in range(NP):
                for lane in range(2):
                    u = 2 * p + lane
                    v.wait_ge(*ub2_toks[(ts, u)])
                    ubv = ub2.ap()[:, lane * 2 * TS:(lane + 1) * 2 * TS] \
                             .rearrange("p (c w) -> p c w", c=2)
                    v.tensor_tensor(out=ubv, in0=ubv, in1=s2bc, op=ALU.mult)
                for k in range(8):
                    for lane in range(2):
                        c0, c1 = lane * 2 * TS, (lane + 1) * 2 * TS
                        v.wait_ge(sA.h, ab + 18 * p + soff(p) + 2 * k + lane + 1)
                        v.scalar_tensor_tensor(
                            out=zt2[:, c0:c1],
                            in0=mq2[:, c0:c1], scalar=HYB_S[k],
                            in1=(ub2 if k == 0 else zt2)[:, c0:c1],
                            op0=ALU.mult, op1=ALU.add)
                for lane in range(2):
                    c0, c1 = lane * 2 * TS, (lane + 1) * 2 * TS
                    v.scalar_tensor_tensor(
                        out=mq2[:, c0:c1], in0=zt2[:, c0:c1], scalar=-1.0,
                        in1=ub2[:, c0:c1], op0=ALU.mult, op1=ALU.add)
        assert sV.n == vb + 20 * NP

        @SCA
        def _(sc, ts=ts, vb=vb, ab=ab):
            for p in range(NP):
                if mid_cb is not None and p == INJ:
                    mid_cb(sc)
                for k in range(8):
                    for lane in range(2):
                        c0, c1 = lane * 2 * TS, (lane + 1) * 2 * TS
                        sc.wait_ge(sV.h, vb + 20 * p
                                   + (1 + lane if k == 0 else 2 * k + 1 + lane))
                        sc.activation(out=mq2[:, c0:c1],
                                      in_=(ub2 if k == 0 else zt2)[:, c0:c1],
                                      func=AF.Sign, bias=pbh[:])
                for lane in range(2):
                    u = 2 * p + lane
                    c0, c1 = lane * 2 * TS, (lane + 1) * 2 * TS
                    sc.wait_ge(sV.h, vfin[(ts, u)])
                    pu = ((ts, u - 2) if u >= 2
                          else (ts - 1, u + HC // 2 - 2) if ts > 0 else None)
                    if pu is not None:
                        sc.wait_ge(*st_tok[pu])
                    sc.activation(out=q2c[:, lane * 1024:(lane + 1) * 1024],
                                  in_=mq2[:, c0:c1], func=AF.Copy, bias=-0.5)
                    st_tok[(ts, u)] = rQ2S.inc(lane, sc.dma_start(
                        out=q2_sp[ts, u // 2, :, (u % 2) * 1024:(u % 2 + 1) * 1024],
                        in_=q2c[:, lane * 1024:(lane + 1) * 1024]))
        assert sA.n == ab + 18 * NP + (8 if mid_cb is not None else 0)

    prev_next_ok = None
    prev_res2 = None
    prev_mm_end = None
    prev_f2_m0 = None

    def emit_f2_loads_mms(ts, dg1_gate):
        nonlocal prev_mm_end, prev_f2_m0
        f2_m0 = sMM.n
        f2_toks = {}
        q2tok = {}

        @blk.sync
        def _(s, ts=ts, f2_m0=f2_m0, prev_mm_end=prev_mm_end,
              prev_f2_m0=prev_f2_m0):
            for h_, v_ in tok_qosp:
                s.wait_ge(h_, v_)
            if ts == 0:
                for h_, v_ in BND:
                    s.wait_ge(h_, v_)
            for cj in range(HC // 4):
                s.wait_ge(*st_tok[(ts, 2 * cj)])
                s.wait_ge(*st_tok[(ts, 2 * cj + 1)])
                if prev_f2_m0 is not None:
                    s.wait_ge(sMM.h, prev_f2_m0 + HC + 4 * cj + 4)
                q2tok[cj] = rQ2R.inc(cj, s.dma_start(
                    out=q2_r[:, cj * 2048:(cj + 1) * 2048],
                    in_=q2_sp[ts, cj]))
            for c in range(2 * (HC // 4)):
                dg, cj = divmod(c, HC // 4)
                if c >= 2:
                    s.wait_ge(sMM.h, f2_m0 + (c - 1) * 4)
                elif prev_mm_end is not None:
                    s.wait_ge(sMM.h, prev_mm_end)
                f2_toks[c] = rF2.inc(c, s.dma_start(
                    out=qo_r[:, (c % 2) * 4096:(c % 2 + 1) * 4096],
                    in_=qo_sp[dg, cj]))

        @blk.tensor
        def _(t, ts=ts, dg1_gate=dg1_gate, prev_next_ok=prev_next_ok):
            if prev_next_ok is not None:
                t.wait_ge(sA.h, prev_next_ok)
            else:
                t.wait_ge(sA.h, tok_c2b)
            for dg in range(2):
                if dg == 1:
                    t.wait_ge(sA.h, dg1_gate)
                for j in range(HC):
                    c = dg * (HC // 4) + j // 4
                    t.wait_ge(*f2_toks[c])
                    if dg == 0:
                        t.wait_ge(*q2tok[j // 4])
                    base = (c % 2) * 4096 + (j % 4) * 1024
                    rhs = q2_r[:, (j // 4) * 2048 + (j % 4) * TS:
                               (j // 4) * 2048 + (j % 4) * TS + TS]
                    for d in range(8):
                        last = t.matmul(pout[:, d * TS:(d + 1) * TS],
                                        qo_r[:, base + d * 128:base + (d + 1) * 128],
                                        rhs, start=(j == 0), stop=(j == HC - 1))
                    sMM.inc(last)
        prev_mm_end = sMM.n
        prev_f2_m0 = f2_m0
        assert sMM.n == f2_m0 + 2 * HC
        return f2_m0

    def mk_evdg0(ts, f2_m0):
        def cb(sc):
            sc.wait_ge(sMM.h, f2_m0 + HC)
            if ts > 0:
                sc.wait_ge(sOut.h, 16 * DC * ts)
            for d in range(8):
                sc.copy(out=o_ts[:, d * TS:(d + 1) * TS],
                        in_=pout[:, d * TS:(d + 1) * TS])
        return cb

    def emit_f2_epilogue(ts, f2_m0):
        nonlocal prev_next_ok, prev_res2

        @SCA
        def _(sc, ts=ts, f2_m0=f2_m0):
            sc.wait_ge(sMM.h, f2_m0 + 2 * HC)
            for d in range(8):
                sc.copy(out=o_ts[:, (8 + d) * TS:(9 + d) * TS],
                        in_=pout[:, d * TS:(d + 1) * TS])
        tok_ev2 = sA.n

        @blk.sync
        def _(s, ts=ts, prev_res2=prev_res2):
            s.wait_ge(sLn1.h, tok_ln1_all)
            if ts == 0:
                for h_, v_ in BND:
                    s.wait_ge(h_, v_)
            if prev_res2 is not None:
                s.wait_ge(sV.h, prev_res2)
            for d in range(DC):
                sLb.inc(s.dma_start(out=lbuf[:, d * TS:(d + 1) * TS],
                                    in_=ln1_sp[d, :, ts * TS:(ts + 1) * TS]), 16)
        tok_lb = sLb.n

        c2sl = c2b.ap()[:, ts * TS:(ts + 1) * TS].rearrange(
            "p (c w) -> p c w", c=1).broadcast_to((128, DC, TS))

        if "f2dbg" in dbg and ts == 0:
            draw = nc.dram_tensor("dbg_raw0", [128, DC * TS], F32, kind="ExternalOutput").ap()
            _ev, _lg0 = tok_ev2, sLG.n

            @blk.gpsimd
            def _(g):
                g.wait_ge(sA.h, _ev)
                sLG.inc(g.dma_start(out=draw, in_=o_ts[:]), 16)
            _dump1 = sLG.n

        @VEC
        def _(v, tok_ev2=tok_ev2, tok_lb=tok_lb, c2sl=c2sl, ts=ts):
            v.wait_ge(sA.h, tok_ev2)
            v.wait_ge(sLb.h, tok_lb)
            if "f2dbg" in dbg and ts == 0:
                v.wait_ge(sLG.h, _dump1)
            v.tensor_tensor(out=o_3, in0=o_3, in1=c2sl, op=ALU.mult)
            v.scalar_tensor_tensor(out=o_ts[:], in0=lbuf[:], scalar=2.0,
                                   in1=o_ts[:], op0=ALU.mult, op1=ALU.add)
        tok_res2 = sV.n
        prev_res2 = tok_res2
        if "f2dbg" in dbg and ts == 0:
            dres = nc.dram_tensor("dbg_res20", [128, DC * TS], F32, kind="ExternalOutput").ap()
            dlb = nc.dram_tensor("dbg_lbuf0", [128, DC * TS], F32, kind="ExternalOutput").ap()
            _r2 = tok_res2

            @blk.gpsimd
            def _(g):
                g.wait_ge(sV.h, _r2)
                sLG.inc(g.dma_start(out=dres, in_=o_ts[:]), 16)
                sLG.inc(g.dma_start(out=dlb, in_=lbuf[:]), 16)
            _dump2 = sLG.n

            @SCA
            def _(sc):
                sc.wait_ge(sLG.h, _dump2)


        abase_sq3 = sA.n
        mbase_sq3 = sMM.n
        @SCA
        def _(sc, tok_res2=tok_res2, mbase_sq3=mbase_sq3):
            sc.wait_ge(sV.h, tok_res2)
            for d in range(DC):
                if d >= 2:
                    sc.wait_ge(sMM.h, mbase_sq3 + DC + d - 1)
                sc.activation(out=sqb2[:, (d % 2) * TS:(d % 2 + 1) * TS],
                                     in_=o_ts[:, d * TS:(d + 1) * TS], func=AF.Square)
        tok_sq3 = sA.n

        @blk.tensor
        def _(t, tok_res2=tok_res2, abase_sq3=abase_sq3):
            t.wait_ge(sV.h, tok_res2)
            for d in range(DC):
                sMM.inc(t.matmul(pout[0:1, 0:TS], onesc[:], o_ts[:, d * TS:(d + 1) * TS],
                                 start=(d == 0), stop=(d == DC - 1)))
            for d in range(DC):
                t.wait_ge(sA.h, abase_sq3 + d + 1)
                sMM.inc(t.matmul(pout[0:1, TS:2 * TS], onescb[:],
                                 sqb2[:, (d % 2) * TS:(d % 2 + 1) * TS],
                                 start=(d == 0), stop=(d == DC - 1)))
        tok_st3 = sMM.n

        @VEC
        def _(v, tok_st3=tok_st3):
            v.wait_ge(sMM.h, tok_st3)
            v.tensor_scalar_mul(out=scr1[:], in0=pout[0:1, 0:TS], scalar1=1.0 / D)
            v.tensor_scalar_mul(out=scr2[:], in0=pout[0:1, TS:2 * TS], scalar1=1.0 / D)
            v.tensor_mul(out=scr4[:], in0=scr1[:], in1=scr1[:])
            v.tensor_sub(out=scr2[:], in0=scr2[:], in1=scr4[:])
        tok_var2 = sV.n

        @SCA
        def _(sc, tok_var2=tok_var2):
            sc.wait_ge(sV.h, tok_var2)
            sc.activation(out=scr2[:], in_=scr2[:], func=AF.Sqrt, bias=epsc[:])
        tok_sqr3 = sA.n

        @VEC
        def _(v, tok_sqr3=tok_sqr3):
            v.wait_ge(sA.h, tok_sqr3)
            v.reciprocal(out=scr2[:], in_=scr2[:])
        tok_rstd2 = sV.n

        @blk.tensor
        def _(t, tok_rstd2=tok_rstd2):
            t.wait_ge(sV.h, tok_rstd2)
            t.matmul(pout[:, 2 * TS:3 * TS], ones1[:], scr1[:], start=True, stop=True)
            sMM.inc(t.matmul(pout[:, 3 * TS:4 * TS], ones1[:], scr2[:], start=True,
                             stop=True))
        tok_bc6 = sMM.n

        @SCA
        def _(sc, tok_bc6=tok_bc6):
            sc.wait_ge(sMM.h, tok_bc6)
            sc.copy(out=mu2b[:], in_=pout[:, 2 * TS:3 * TS])
            sc.copy(out=rstd2b[:], in_=pout[:, 3 * TS:4 * TS])
        tok_bc6c = sA.n
        prev_next_ok = tok_bc6c

        @VEC
        def _(v, tok_bc6c=tok_bc6c):
            v.wait_ge(sA.h, tok_bc6c)
            v.tensor_tensor(out=o_3, in0=o_3, in1=rep_bcast(mu2b, DC, TS),
                            op=ALU.subtract)
            v.tensor_tensor(out=o_3, in0=o_3, in1=rep_bcast(rstd2b, DC, TS),
                                   op=ALU.mult)
        tok_out = sV.n

        @SCA
        def _(sc, ts=ts, tok_out=tok_out):
            sc.wait_ge(sV.h, tok_out)
            for d in range(DC):
                sOut.inc(sc.dma_start(out=outT[d, :, ts * TS:(ts + 1) * TS],
                                      in_=o_ts[:, d * TS:(d + 1) * TS]), 16)

    emit_q2_slice(0)
    for ts in range(NTS):
        dg1_gate = sA.n + (18 * 12 + 8 if ts + 1 < NTS else 8)
        f2_m0 = emit_f2_loads_mms(ts, dg1_gate)
        if ts + 1 < NTS:
            emit_q2_slice(ts + 1, mid_cb=mk_evdg0(ts, f2_m0))
        else:
            @SCA
            def _(sc, ts=ts, f2_m0=f2_m0, cb=mk_evdg0(ts, f2_m0)):
                cb(sc)
        emit_f2_epilogue(ts, f2_m0)

    tok_allout = sOut.n

    @blk.sync
    def _(s):
        s.wait_ge(sOut.h, tok_allout)

    f2p.close()
    f2.close()
    st.close()
    return nc, sorted(dbg_tensors.keys() & dbg)


def _prep_shared(Wi, Wg, Wdt, Wb, Wc, Wo, A_log, b_dt, w_norm, w_gv, w_out):
    f = np.ascontiguousarray
    shared = {
        "wiT": f(Wi.T), "wgT": f(Wg.T), "wdtT": f(Wdt.T),
        "wbrep": f(np.broadcast_to(Wb[:, :, None], (KST, NSC, 128))),
        "wcrep": f(np.broadcast_to(Wc[:, :, None], (KST, NSC, 128))),
        "woT": f(Wo.T), "alog": f(A_log), "bdt": f(b_dt.reshape(NSC, 1)),
        "wnorm": f(w_norm.reshape(DC, 128).T),
    }
    W4 = w_gv.reshape(2, HC, 128, DC, 128)
    shared["wgvt"] = f(W4.transpose(1, 0, 4, 3, 2).reshape(HC, 2, 128, DC * 128))
    V5 = w_out.reshape(2, 8, 128, HC, 128)
    shared["woutT"] = f(V5.transpose(0, 3, 4, 1, 2).reshape(2, HC, 128, 8 * 128))
    return shared


def kernel(L0_attn_x, L0_attn_Wi, L0_attn_Wo, L0_attn_A_log, L0_attn_Wb,
           L0_attn_Wc, L0_attn_Wdt, L0_attn_b_dt, L0_attn_Wg,
           L0_ffn_w_norm, L0_ffn_w_gv, L0_ffn_w_out):
    x = np.asarray(L0_attn_x, dtype=np.float32)
    B = x.shape[0]
    shared = _prep_shared(
        np.asarray(L0_attn_Wi, np.float32), np.asarray(L0_attn_Wg, np.float32),
        np.asarray(L0_attn_Wdt, np.float32), np.asarray(L0_attn_Wb, np.float32),
        np.asarray(L0_attn_Wc, np.float32), np.asarray(L0_attn_Wo, np.float32),
        np.asarray(L0_attn_A_log, np.float32), np.asarray(L0_attn_b_dt, np.float32),
        np.asarray(L0_ffn_w_norm, np.float32), np.asarray(L0_ffn_w_gv, np.float32),
        np.asarray(L0_ffn_w_out, np.float32))
    nc, _ = build_program()
    in_maps = []
    for b in range(B):
        m = dict(shared)
        m["xT"] = np.ascontiguousarray(x[b].T)
        in_maps.append(m)
    res = run_bass_kernel_spmd(nc, in_maps, core_ids=list(range(B)))
    out = np.empty((B, S, D), np.float32)
    for b in range(B):
        o = np.asarray(res.results[b]["outT"])
        out[b] = o.reshape(D, S).T
    return out


# version 15
# speedup vs baseline: 1.1331x; 1.0061x over previous
import contextlib
import numpy as np
import concourse.bass as bass
import concourse.mybir as mybir
from concourse.bass_utils import run_bass_kernel_spmd

F32 = mybir.dt.float32
BF16 = mybir.dt.bfloat16
ALU = mybir.AluOpType
AF = mybir.ActivationFunctionType
AX = mybir.AxisListType

S = 2048
D = 2048
NSC = 128
KST = 16
H = 8192
TS = 512
NTS = S // TS
DC = D // 128
HC = H // 128
EPS = 1e-5
QB = 127.0
STAGES = ("A0", "A1", "A2", "A3", "A", "P", "F15", "Q2", "F2")


class Sem:
    def __init__(self, stack, nc, name):
        self.h = stack.enter_context(nc.semaphore(name))
        self.n = 0

    def inc(self, instr, amt=1):
        instr.then_inc(self.h, amt)
        self.n += amt
        return self.n

    def tok(self):
        return (self.h, self.n)


class DRing:

    def __init__(self, stack, nc, name, slots):
        self.sems = [Sem(stack, nc, f"{name}{i}") for i in range(slots)]
        self.slots = slots

    def inc(self, idx, instr, amt=16):
        s = self.sems[idx % self.slots]
        s.inc(instr, amt)
        return (s.h, s.n)


def emit_peel(v, absbuf, tmpbuf, levels=7):
    for lvl in range(levels):
        hh = float(2 ** (levels - 1 - lvl))
        v.tensor_scalar(out=tmpbuf, in0=absbuf, scalar1=hh - 0.5, scalar2=hh,
                        op0=ALU.is_ge, op1=ALU.mult)
        v.tensor_tensor(out=absbuf, in0=absbuf, in1=tmpbuf, op=ALU.subtract)


def rep_bcast(t, c, w):
    return t.ap().rearrange("p (c w) -> p c w", c=1).broadcast_to((128, c, w))


class SyncEng:
    _PASS = ("wait_ge", "dma_start", "dma_start_transpose", "nop")

    def __init__(self, eng, sem):
        object.__setattr__(self, "_e", eng)
        object.__setattr__(self, "_s", sem)

    def __getattr__(self, name):
        f = getattr(self._e, name)
        if name in SyncEng._PASS or not callable(f):
            return f
        s = self._s
        e = self._e

        def call(*a, **kw):
            r = f(*a, **kw)
            s.inc(r)
            e.wait_ge(s.h, s.n)
            return r
        return call


def build_program(upto="F2", dbg=()):
    stage_n = STAGES.index(upto) - 4
    nc = bass.Bass("TRN2", target_bir_lowering=False, debug=False)
    st = contextlib.ExitStack()
    dbg = set(dbg)
    dbg_tensors = {}

    xT = nc.dram_tensor("xT", [D, S], F32, kind="ExternalInput").ap()
    wiT = nc.dram_tensor("wiT", [D, NSC], F32, kind="ExternalInput").ap()
    wgT = nc.dram_tensor("wgT", [D, NSC], F32, kind="ExternalInput").ap()
    wdtT = nc.dram_tensor("wdtT", [NSC, NSC], F32, kind="ExternalInput").ap()
    wbrep = nc.dram_tensor("wbrep", [KST, NSC, 128], F32, kind="ExternalInput").ap()
    wcrep = nc.dram_tensor("wcrep", [KST, NSC, 128], F32, kind="ExternalInput").ap()
    woT = nc.dram_tensor("woT", [NSC, D], F32, kind="ExternalInput").ap()
    alog = nc.dram_tensor("alog", [NSC, KST], F32, kind="ExternalInput").ap()
    bdt = nc.dram_tensor("bdt", [NSC, 1], F32, kind="ExternalInput").ap()
    wnorm = nc.dram_tensor("wnorm", [128, DC], F32, kind="ExternalInput").ap()
    wgvt = nc.dram_tensor("wgvt", [HC, 2, 128, DC * 128], F32, kind="ExternalInput").ap()
    woutT = nc.dram_tensor("woutT", [2, HC, 128, 8 * 128], F32, kind="ExternalInput").ap()
    outT = nc.dram_tensor("outT", [DC, 128, S], F32, kind="ExternalOutput").ap()

    u_sp = nc.dram_tensor("u_sp", [HC, 128, S], F32, kind="Internal").ap()
    ln1_sp = nc.dram_tensor("ln1_sp", [DC, 128, S], F32, kind="Internal").ap()
    qo_sp = nc.dram_tensor("qo_sp", [2, HC // 4, 128, 4 * 1024], BF16, kind="Internal").ap()
    q2_sp = nc.dram_tensor("q2_sp", [NTS, HC // 4, 128, 4 * TS], BF16, kind="Internal").ap()

    sLG = Sem(st, nc, "sLG")
    sSm = Sem(st, nc, "sSm")
    sMM = Sem(st, nc, "sMM")
    sV = Sem(st, nc, "sV")
    sA = Sem(st, nc, "sA")
    rW0 = DRing(st, nc, "rW0", 6)
    rPX = DRing(st, nc, "rPX", 2)
    rWLD = DRing(st, nc, "rWLD", 3)
    rWO = DRing(st, nc, "rWO", 2)
    rUB = DRing(st, nc, "rUB", 2)
    rF2 = DRing(st, nc, "rF2", 2)
    rQ2R = DRing(st, nc, "rQ2R", 16)
    rQ2S = DRing(st, nc, "rQ2S", 2)
    rUSP = DRing(st, nc, "rUSP", 2)
    sLn1 = Sem(st, nc, "sLn1")
    sLb = Sem(st, nc, "sLb")
    rQoS = DRing(st, nc, "rQoS", 2)
    sOut = Sem(st, nc, "sOut")

    sb = st.enter_context
    ones1 = sb(nc.sbuf_tensor("ones1", [1, 128], F32))
    onesc = sb(nc.sbuf_tensor("onesc", [128, 1], F32))
    onescb = sb(nc.sbuf_tensor("onescb", [128, 1], BF16))
    bdt_s = sb(nc.sbuf_tensor("bdt_s", [NSC, 1], F32))
    nega = sb(nc.sbuf_tensor("nega", [NSC, KST], F32))
    wn_s = sb(nc.sbuf_tensor("wn_s", [128, DC], F32))
    sgv128 = sb(nc.sbuf_tensor("sgv128", [128, 1], F32))
    sout128 = sb(nc.sbuf_tensor("sout128", [128, 1], F32))
    mwc1 = sb(nc.sbuf_tensor("mwc1", [1, 1], F32))
    mwc2 = sb(nc.sbuf_tensor("mwc2", [1, 1], F32))
    accW = sb(nc.sbuf_tensor("accW", [128, 256], F32))
    scr1 = sb(nc.sbuf_tensor("scr1", [1, TS], F32))
    scr2 = sb(nc.sbuf_tensor("scr2", [1, TS], F32))
    scr4 = sb(nc.sbuf_tensor("scr4", [1, TS], F32))
    epsc = sb(nc.sbuf_tensor("epsc", [1, 1], F32))

    ybf = sb(nc.sbuf_tensor("ybf", [NSC, S], BF16))
    c2b = sb(nc.sbuf_tensor("c2b", [128, S], F32))
    s2b = sb(nc.sbuf_tensor("s2b", [128, S], F32))
    pbh = sb(nc.sbuf_tensor("pbh", [128, 1], F32))
    pbm = sb(nc.sbuf_tensor("pbm", [128, 1], F32))
    stPF = contextlib.ExitStack()

    blk = st.enter_context(nc.Block())

    def VEC(f):
        blk.vector(lambda v: f(SyncEng(v, sV)))

    def SCA(f):
        blk.scalar(lambda sc: f(SyncEng(sc, sA)))

    def emit_taps(names):
        taps = []
        for nm in names:
            if nm not in dbg:
                continue
            hnd, shape, dt = dbg_tensors[nm]
            ap = nc.dram_tensor("dbg_" + nm, shape, F32, kind="ExternalOutput").ap()
            taps.append((hnd, ap))
        vv, aa, mm = sV.n, sA.n, sMM.n

        @blk.gpsimd
        def _(g):
            g.wait_ge(sV.h, vv)
            g.wait_ge(sA.h, aa)
            g.wait_ge(sMM.h, mm)
            for hnd, ap in taps:
                sLG.inc(g.dma_start(out=ap, in_=hnd[:]), 16)
            g.wait_ge(sLG.h, sLG.n)

    ast = contextlib.ExitStack()
    asb = ast.enter_context
    wdt_s = asb(nc.sbuf_tensor("wdt_s", [NSC, NSC], BF16))
    wb_s = asb(nc.sbuf_tensor("wb_s", [NSC, KST * 128], BF16))
    wc_s = asb(nc.sbuf_tensor("wc_s", [NSC, KST * 128], BF16))
    wo_s = asb(nc.sbuf_tensor("wo_s", [NSC, D], BF16))
    alog_s = asb(nc.sbuf_tensor("alog_s", [NSC, KST], F32))
    zf = asb(nc.sbuf_tensor("zf", [NSC, S], F32))
    zbf = asb(nc.sbuf_tensor("zbf", [NSC, S], BF16))
    gsil = asb(nc.sbuf_tensor("gsil", [NSC, S], F32))
    dt_s = asb(nc.sbuf_tensor("dt_s", [NSC, S], F32))
    w_s = asb(nc.sbuf_tensor("w_s", [NSC, S], F32))
    ak = asb(nc.sbuf_tensor("ak", [NSC, S], F32))
    uk = asb(nc.sbuf_tensor("uk", [NSC, S], F32))
    stk = asb(nc.sbuf_tensor("stk", [NSC, S], F32))
    yacc = asb(nc.sbuf_tensor("yacc", [NSC, S], F32))
    ptmp = asb(nc.sbuf_tensor("ptmp", [NSC, S], F32))
    sp1 = asb(nc.sbuf_tensor("sp1", [NSC, S], F32))
    sp2 = asb(nc.sbuf_tensor("sp2", [NSC, S], F32))
    ast0 = contextlib.ExitStack()
    xbf = ast0.enter_context(nc.sbuf_tensor("xbf", [128, DC * S], BF16))
    wi_s = ast0.enter_context(nc.sbuf_tensor("wi_s", [128, DC * NSC], BF16))
    wg_s = ast0.enter_context(nc.sbuf_tensor("wg_s", [128, DC * NSC], BF16))

    a1 = contextlib.ExitStack()
    pz = a1.enter_context(nc.psum_tensor("pz", [NSC, S], F32))
    pg = a1.enter_context(nc.psum_tensor("pg", [NSC, S], F32))

    @blk.gpsimd
    def _(g):
        for k in range(DC):
            sLG.inc(g.dma_start(out=xbf[:, k * S:(k + 1) * S],
                                in_=xT[k * 128:(k + 1) * 128, :]), 16)
        for k in range(DC):
            sLG.inc(g.dma_start(out=wi_s[:, k * NSC:(k + 1) * NSC],
                                in_=wiT[k * 128:(k + 1) * 128, :]), 16)
        for k in range(DC):
            sLG.inc(g.dma_start(out=wg_s[:, k * NSC:(k + 1) * NSC],
                                in_=wgT[k * 128:(k + 1) * 128, :]), 16)
        sLG.inc(g.dma_start(out=wdt_s[:], in_=wdtT[:]), 16)
        for k in range(KST):
            sLG.inc(g.dma_start(out=wb_s[:, k * 128:(k + 1) * 128], in_=wbrep[k]), 16)
        for k in range(KST):
            sLG.inc(g.dma_start(out=wc_s[:, k * 128:(k + 1) * 128], in_=wcrep[k]), 16)
        sLG.inc(g.dma_start(out=wo_s[:], in_=woT[:]), 16)
    tok_gload = sLG.n

    @blk.sync
    def _(s):
        sSm.inc(s.dma_start(out=alog_s[:], in_=alog[:]), 16)
        sSm.inc(s.dma_start(out=bdt_s[:], in_=bdt[:]), 16)
        sSm.inc(s.dma_start(out=wn_s[:], in_=wnorm[:]), 16)
    tok_sload = sSm.n

    @VEC
    def _(v):
        v.memset(ones1[:], 1.0)
        v.memset(onesc[:], 1.0)
        v.memset(onescb[:], 1.0)
        v.memset(epsc[:], EPS)
        v.memset(yacc[:], 0.0)
        v.memset(pbh[:], 0.5)
        v.memset(pbm[:], -0.5)
    tok_init = sV.n

    if stage_n == -4:
        emit_taps([])
        ast0.close(); a1.close(); ast.close(); stPF.close(); st.close()
        return nc, []

    @blk.tensor
    def _(t):
        t.wait_ge(sLG.h, tok_gload)
        for ts in range(NTS):
            for k in range(DC):
                last = t.matmul(pz[:, ts * TS:(ts + 1) * TS],
                                wi_s[:, k * NSC:(k + 1) * NSC],
                                xbf[:, k * S + ts * TS:k * S + (ts + 1) * TS],
                                start=(k == 0), stop=(k == DC - 1))
            sMM.inc(last)
        for ts in range(NTS):
            for k in range(DC):
                last = t.matmul(pg[:, ts * TS:(ts + 1) * TS],
                                wg_s[:, k * NSC:(k + 1) * NSC],
                                xbf[:, k * S + ts * TS:k * S + (ts + 1) * TS],
                                start=(k == 0), stop=(k == DC - 1))
            sMM.inc(last)
    tok_zg = sMM.n

    vbase_zf = sV.n

    @VEC
    def _(v):
        for ts in range(NTS):
            v.wait_ge(sMM.h, tok_zg - 8 + ts + 1)
            v.tensor_copy(out=zf[:, ts * TS:(ts + 1) * TS],
                          in_=pz[:, ts * TS:(ts + 1) * TS])
    tok_zf = sV.n

    @SCA
    def _(sc):
        for ts in range(NTS):
            sc.wait_ge(sV.h, vbase_zf + ts + 1)
            sc.copy(out=zbf[:, ts * TS:(ts + 1) * TS],
                    in_=pz[:, ts * TS:(ts + 1) * TS])
        for ts in range(NTS):
            sc.wait_ge(sMM.h, tok_zg - 4 + ts + 1)
            sc.activation(out=gsil[:, ts * TS:(ts + 1) * TS],
                                 in_=pg[:, ts * TS:(ts + 1) * TS], func=AF.Silu)
        sc.wait_ge(sSm.h, tok_sload)
        sc.activation(out=nega[:], in_=alog_s[:], func=AF.Exp)
    tok_aevac = sA.n

    @VEC
    def _(v):
        v.wait_ge(sA.h, tok_aevac)
        v.tensor_scalar_mul(out=nega[:], in0=nega[:], scalar1=-1.0)
    tok_nega = sV.n

    if stage_n == -3:
        emit_taps(["zf", "gsil"]) if False else None
        dbg_tensors.update(dict(zf=(zf, [NSC, S], F32), gsil=(gsil, [NSC, S], F32)))
        emit_taps(["zf", "gsil"])
        ast0.close(); a1.close(); ast.close(); stPF.close(); st.close()
        return nc, sorted(dbg_tensors.keys() & dbg)

    a1.close()
    a2 = contextlib.ExitStack()
    pdt = a2.enter_context(nc.psum_tensor("pdt", [NSC, S], F32))

    @blk.tensor
    def _(t):
        t.wait_ge(sA.h, tok_aevac)
        t.wait_ge(sV.h, tok_zf)
        for ts in range(NTS):
            last = t.matmul(pdt[:, ts * TS:(ts + 1) * TS], wdt_s[:],
                            zbf[:, ts * TS:(ts + 1) * TS], start=True, stop=True)
        sMM.inc(last)
    tok_dtmm = sMM.n

    @SCA
    def _(sc):
        sc.wait_ge(sMM.h, tok_dtmm)
        sc.activation(out=sp1[:], in_=pdt[:], func=AF.Relu, bias=bdt_s[:])
        sc.activation(out=sp2[:], in_=pdt[:], func=AF.Abs, bias=bdt_s[:])
        sc.wait_ge(sA.h, sA.n)
        sc.activation(out=sp2[:], in_=sp2[:], func=AF.Exp, scale=-1.0)
        sc.wait_ge(sA.h, sA.n)
        sc.activation(out=sp2[:], in_=sp2[:], func=AF.Ln, bias=1.0)
    tok_sp = sA.n

    @VEC
    def _(v):
        v.wait_ge(sA.h, tok_sp)
        v.tensor_add(out=dt_s[:], in0=sp1[:], in1=sp2[:])
        v.tensor_mul(out=w_s[:], in0=dt_s[:], in1=zf[:])
    tok_dt = sV.n

    if stage_n == -2:
        dbg_tensors.update(dict(dt_s=(dt_s, [NSC, S], F32)))
        emit_taps(["dt_s"])
        a2.close(); ast0.close(); ast.close(); stPF.close(); st.close()
        return nc, sorted(dbg_tensors.keys() & dbg)

    a2.close()
    a3 = contextlib.ExitStack()
    pb = a3.enter_context(nc.psum_tensor("pb", [NSC, S], F32))
    pc = a3.enter_context(nc.psum_tensor("pc", [NSC, S], F32))

    vbase_k = sV.n

    def _kcum(k):
        return 0 if k == 0 else 4 + 5 * (k - 1)

    def tok_uk(k):
        return vbase_k + _kcum(k) + 2

    def tok_pk(k):
        return vbase_k + _kcum(k) + (4 if k == 0 else 5)

    mbase_k = sMM.n

    def tok_bmm(k):
        return mbase_k + 2 * k + 1

    def tok_cmm(k):
        return mbase_k + 2 * k + 2

    @blk.tensor
    def _(t):
        t.wait_ge(sV.h, tok_dt)
        for k in range(KST):
            if k > 0:
                t.wait_ge(sV.h, tok_uk(k - 1))
            for ts in range(NTS):
                last = t.matmul(pb[:, ts * TS:(ts + 1) * TS],
                                wb_s[:, k * 128:(k + 1) * 128],
                                zbf[:, ts * TS:(ts + 1) * TS], start=True, stop=True)
            sMM.inc(last)
            if k > 0:
                t.wait_ge(sV.h, tok_pk(k - 1))
            for ts in range(NTS):
                last = t.matmul(pc[:, ts * TS:(ts + 1) * TS],
                                wc_s[:, k * 128:(k + 1) * 128],
                                zbf[:, ts * TS:(ts + 1) * TS], start=True, stop=True)
            sMM.inc(last)

    @VEC
    def _(v):
        v.wait_ge(sV.h, tok_nega)
        for k in range(KST):
            v.wait_ge(sMM.h, tok_bmm(k))
            v.tensor_scalar(out=ak[:], in0=dt_s[:], scalar1=nega[:, k:k + 1],
                            scalar2=1.0, op0=ALU.mult, op1=ALU.add)
            v.tensor_mul(out=uk[:], in0=w_s[:], in1=pb[:])
            v.tensor_tensor_scan(out=stk[:], data0=ak[:], data1=uk[:],
                                 initial=0.0, op0=ALU.mult, op1=ALU.add)
            v.wait_ge(sMM.h, tok_cmm(k))
            if k == 0:
                v.tensor_mul(out=yacc[:], in0=stk[:], in1=pc[:])
            else:
                v.tensor_mul(out=ptmp[:], in0=stk[:], in1=pc[:])
                v.tensor_add(out=yacc[:], in0=yacc[:], in1=ptmp[:])
        v.tensor_mul(out=ybf[:], in0=yacc[:], in1=gsil[:])
    tok_ybf = sV.n
    assert sV.n == vbase_k + _kcum(KST) + 1, (sV.n, vbase_k, _kcum(KST))

    if stage_n == -1:
        dbg_tensors.update(dict(ybf=(ybf, [NSC, S], BF16)))
        emit_taps(["ybf"])
        a3.close(); ast0.close(); ast.close(); stPF.close(); st.close()
        return nc, sorted(dbg_tensors.keys() & dbg)

    ast0.close()
    aw = contextlib.ExitStack()
    wmean = aw.enter_context(nc.sbuf_tensor("wmean", [128, 6 * 2048], F32))
    wabs = aw.enter_context(nc.sbuf_tensor("wabs", [128, 2048], F32))

    w0_tiles = []
    for j in range(HC):
        for vg in range(2):
            w0_tiles.append((wgvt[j, vg], 2048))
    for dg in range(2):
        for j in range(HC):
            w0_tiles.append((woutT[dg, j], 1024))

    abase_w0 = sA.n
    vbase_w0 = sV.n

    @SCA
    def _(sc):
        for i, (_src, width) in enumerate(w0_tiles):
            if i % 2:
                continue
            e = i // 2
            sl = 2 * (e % 3)
            sc.wait_ge(rW0.sems[sl].h, 16 * (e // 3 + 1))
            sc.activation(out=wabs[:, 0:width],
                          in_=wmean[:, sl * 2048:sl * 2048 + width],
                          func=AF.Abs, accum_out=accW[:, i:i + 1])
    tok_w0acc = sA.n

    @VEC
    def _(v):
        for i, (_src, width) in enumerate(w0_tiles):
            if i % 2 == 0:
                continue
            o = i // 2
            sl = 2 * (o % 3) + 1
            v.wait_ge(rW0.sems[sl].h, 16 * (o // 3 + 1))
            v.tensor_reduce(out=accW[:, i:i + 1],
                            in_=wmean[:, sl * 2048:sl * 2048 + width],
                            axis=AX.X, op=ALU.add, apply_absolute_value=True)
    tok_w0accv = sV.n

    @blk.sync
    def _(s):
        s.wait_ge(sMM.h, tok_zg)
        for i, (src, width) in enumerate(w0_tiles):
            if i % 2:
                continue
            e = i // 2
            sl = 2 * (e % 3)
            if e >= 3:
                s.wait_ge(sA.h, abase_w0 + (e - 3) + 1)
            rW0.inc(sl, s.dma_start(out=wmean[:, sl * 2048:sl * 2048 + width],
                                    in_=src))

    @blk.gpsimd
    def _(g):
        g.wait_ge(sMM.h, tok_zg)
        for i, (src, width) in enumerate(w0_tiles):
            if i % 2 == 0:
                continue
            o = i // 2
            sl = 2 * (o % 3) + 1
            if o >= 3:
                g.wait_ge(sV.h, vbase_w0 + (o - 3) + 1)
            rW0.inc(sl, g.dma_start(out=wmean[:, sl * 2048:sl * 2048 + width],
                                    in_=src))

    a3.close()
    a4 = contextlib.ExitStack()
    pw0 = a4.enter_context(nc.psum_tensor("pw0", [1, 2], F32))
    pw1 = a4.enter_context(nc.psum_tensor("pw1", [128, 2], F32))
    redW = a4.enter_context(nc.sbuf_tensor("redW", [128, 2], F32))
    rowW = a4.enter_context(nc.sbuf_tensor("rowW", [1, 2], F32))

    @VEC
    def _(v):
        v.wait_ge(sA.h, tok_w0acc)
        v.tensor_reduce(out=redW[:, 0:1], in_=accW[:, 0:128], axis=AX.X, op=ALU.add)
        v.tensor_reduce(out=redW[:, 1:2], in_=accW[:, 128:256], axis=AX.X,
                               op=ALU.add)
    tok_redW = sV.n

    @blk.tensor
    def _(t):
        t.wait_ge(sV.h, tok_redW)
        t.matmul(pw0[:, 0:1], onesc[:], redW[:, 0:1], start=True, stop=True)
        sMM.inc(t.matmul(pw0[:, 1:2], onesc[:], redW[:, 1:2], start=True, stop=True))
    tok_w0mm = sMM.n

    @VEC
    def _(v):
        v.wait_ge(sMM.h, tok_w0mm)
        v.tensor_scalar(out=rowW[:, 0:1], in0=pw0[:, 0:1], scalar1=1.0 / (2 * H * D),
                        scalar2=EPS, op0=ALU.mult, op1=ALU.max)
        v.tensor_scalar(out=rowW[:, 1:2], in0=pw0[:, 1:2], scalar1=1.0 / (D * H),
                        scalar2=EPS, op0=ALU.mult, op1=ALU.max)
        v.tensor_scalar_mul(out=mwc1[:], in0=rowW[:, 0:1], scalar1=1.0 / QB)
        v.tensor_scalar_mul(out=mwc2[:], in0=rowW[:, 1:2], scalar1=1.0 / QB)
        v.reciprocal(out=rowW[:], in_=rowW[:])
    tok_w0row = sV.n

    @blk.tensor
    def _(t):
        t.wait_ge(sV.h, tok_w0row)
        sMM.inc(t.matmul(pw1[:], ones1[:], rowW[:], start=True, stop=True))
    tok_w0bc = sMM.n

    @SCA
    def _(sc):
        sc.wait_ge(sMM.h, tok_w0bc)
        sc.copy(out=sgv128[:], in_=pw1[:, 0:1])
        sc.copy(out=sout128[:], in_=pw1[:, 1:2])
    tok_w0done = sA.n
    a4.close()

    dbg_tensors.update(dict(
        zf=(zf, [NSC, S], F32), gsil=(gsil, [NSC, S], F32),
        dt_s=(dt_s, [NSC, S], F32), yacc=(yacc, [NSC, S], F32),
        ybf=(ybf, [NSC, S], BF16), sgv128=(sgv128, [128, 1], F32),
        mwc1=(mwc1, [1, 1], F32), mwc2=(mwc2, [1, 1], F32),
    ))
    if stage_n == 0:
        emit_taps(["zf", "gsil", "dt_s", "yacc", "ybf", "sgv128", "mwc1", "mwc2"])
        aw.close(); ast0.close() if False else None; ast.close(); stPF.close()
        st.close()
        return nc, sorted(dbg_tensors.keys() & dbg)

    aw.close()
    ast.close()

    q1 = stPF.enter_context(nc.sbuf_tensor("q1", [128, NTS * DC * TS], BF16))
    c1b = stPF.enter_context(nc.sbuf_tensor("c1b", [128, S], F32))
    pst = contextlib.ExitStack()
    psb = pst.enter_context
    wo2_s = psb(nc.sbuf_tensor("wo2_s", [NSC, D], BF16))
    xr = psb(nc.sbuf_tensor("xr", [128, 2 * TS], F32))
    res1 = psb(nc.sbuf_tensor("res1", [128, DC * TS], F32))
    hbuf = psb(nc.sbuf_tensor("hbuf", [128, DC * TS], F32))
    paby = psb(nc.sbuf_tensor("paby", [128, DC * TS // 2], F32))
    sgn_p = psb(nc.sbuf_tensor("sgn_p", [128, DC * TS // 2], BF16))
    sqb = psb(nc.sbuf_tensor("sqb", [128, 2 * TS], BF16))
    m1t = psb(nc.sbuf_tensor("m1t", [128, TS], F32))
    trt = psb(nc.sbuf_tensor("trt", [64, TS], F32))
    mu_b = psb(nc.sbuf_tensor("mu_b", [128, TS], F32))
    rstd_b = psb(nc.sbuf_tensor("rstd_b", [128, TS], F32))
    ppst = contextlib.ExitStack()
    pwo = ppst.enter_context(nc.psum_tensor("pwo", [128, 2 * TS], F32))
    pstat = ppst.enter_context(nc.psum_tensor("pstat", [1, 2 * TS], F32))
    pbc = ppst.enter_context(nc.psum_tensor("pbc", [128, TS], F32))

    @blk.gpsimd
    def _(g):
        g.wait_ge(sA.h, tok_w0done)
        sLG.inc(g.dma_start(out=wo2_s[:], in_=woT[:]), 16)
    tok_wo2 = sLG.n

    pA_v, pA_a, pA_m = sV.n, sA.n, sMM.n
    prev_q1ts = None
    prev_xadd_end = None
    for ts in range(NTS):
        abase_evac = sA.n
        @blk.tensor
        def _(t, ts=ts, abase_evac=abase_evac):
            t.wait_ge(sLG.h, tok_wo2)
            t.wait_ge(sV.h, tok_ybf)
            for d in range(DC):
                if d >= 2:
                    t.wait_ge(sA.h, abase_evac + (d - 1))
                sMM.inc(t.matmul(pwo[:, (d % 2) * TS:(d % 2 + 1) * TS],
                                 wo2_s[:, d * 128:(d + 1) * 128],
                                 ybf[:, ts * TS:(ts + 1) * TS], start=True, stop=True))
        tok_womm = sMM.n

        @SCA
        def _(sc, tok_womm=tok_womm, prev_q1ts=prev_q1ts):
            if prev_q1ts is not None:
                sc.wait_ge(sV.h, prev_q1ts)
            for d in range(DC):
                sc.wait_ge(sMM.h, tok_womm - DC + d + 1)
                sc.copy(out=res1[:, d * TS:(d + 1) * TS],
                        in_=pwo[:, (d % 2) * TS:(d % 2 + 1) * TS])
        tok_evac = sA.n

        vbase_x = sV.n

        xr_toks = {}

        @blk.sync
        def _(s, ts=ts, vbase_x=vbase_x, prev_xadd_end=prev_xadd_end):
            if ts == 0:
                s.wait_ge(sV.h, pA_v)
                s.wait_ge(sA.h, pA_a)
                s.wait_ge(sMM.h, pA_m)
            for d in range(DC):
                if d >= 2:
                    s.wait_ge(sV.h, vbase_x + (d - 1))
                elif prev_xadd_end is not None:
                    s.wait_ge(sV.h, prev_xadd_end)
                xr_toks[d] = rPX.inc(d, s.dma_start(
                    out=xr[:, (d % 2) * TS:(d % 2 + 1) * TS],
                    in_=xT[d * 128:(d + 1) * 128, ts * TS:(ts + 1) * TS]))

        @VEC
        def _(v, tok_evac=tok_evac):
            for d in range(DC):
                v.wait_ge(*xr_toks[d])
                v.wait_ge(sA.h, tok_evac - DC + d + 1)
                v.tensor_add(out=res1[:, d * TS:(d + 1) * TS],
                                    in0=res1[:, d * TS:(d + 1) * TS],
                                    in1=xr[:, (d % 2) * TS:(d % 2 + 1) * TS])
        tok_res1 = sV.n
        prev_xadd_end = tok_res1

        abase_sq = sA.n
        mbase_sq = sMM.n
        @SCA
        def _(sc, tok_res1=tok_res1, mbase_sq=mbase_sq):
            sc.wait_ge(sV.h, tok_res1)
            for d in range(DC):
                if d >= 2:
                    sc.wait_ge(sMM.h, mbase_sq + DC + d - 1)
                sc.activation(out=sqb[:, (d % 2) * TS:(d % 2 + 1) * TS],
                                     in_=res1[:, d * TS:(d + 1) * TS], func=AF.Square)
        tok_sq = sA.n

        @blk.tensor
        def _(t, tok_res1=tok_res1, abase_sq=abase_sq):
            t.wait_ge(sV.h, tok_res1)
            for d in range(DC):
                sMM.inc(t.matmul(pstat[:, 0:TS], onesc[:], res1[:, d * TS:(d + 1) * TS],
                                 start=(d == 0), stop=(d == DC - 1)))
            for d in range(DC):
                t.wait_ge(sA.h, abase_sq + d + 1)
                sMM.inc(t.matmul(pstat[:, TS:2 * TS], onescb[:],
                                 sqb[:, (d % 2) * TS:(d % 2 + 1) * TS],
                                 start=(d == 0), stop=(d == DC - 1)))
        tok_st12 = sMM.n

        @VEC
        def _(v, tok_st12=tok_st12):
            v.wait_ge(sMM.h, tok_st12)
            v.tensor_scalar_mul(out=scr1[:], in0=pstat[:, 0:TS], scalar1=1.0 / D)
            v.tensor_scalar_mul(out=scr2[:], in0=pstat[:, TS:2 * TS], scalar1=1.0 / D)
            v.tensor_mul(out=scr4[:], in0=scr1[:], in1=scr1[:])
            v.tensor_sub(out=scr2[:], in0=scr2[:], in1=scr4[:])
        tok_var = sV.n

        @SCA
        def _(sc, tok_var=tok_var):
            sc.wait_ge(sV.h, tok_var)
            sc.activation(out=scr2[:], in_=scr2[:], func=AF.Sqrt, bias=epsc[:])
        tok_sqv = sA.n

        @VEC
        def _(v, tok_sqv=tok_sqv):
            v.wait_ge(sA.h, tok_sqv)
            v.reciprocal(out=scr2[:], in_=scr2[:])
        tok_rstd = sV.n

        @blk.tensor
        def _(t, tok_rstd=tok_rstd):
            t.wait_ge(sV.h, tok_rstd)
            sMM.inc(t.matmul(pbc[:], ones1[:], scr1[:], start=True, stop=True))
        tok_bc1 = sMM.n

        @SCA
        def _(sc, tok_bc1=tok_bc1):
            sc.wait_ge(sMM.h, tok_bc1)
            sc.copy(out=mu_b[:], in_=pbc[:])
        tok_mub = sA.n

        @blk.tensor
        def _(t, tok_mub=tok_mub):
            t.wait_ge(sA.h, tok_mub)
            sMM.inc(t.matmul(pbc[:], ones1[:], scr2[:], start=True, stop=True))
        tok_bc2 = sMM.n

        @SCA
        def _(sc, tok_bc2=tok_bc2):
            sc.wait_ge(sMM.h, tok_bc2)
            sc.copy(out=rstd_b[:], in_=pbc[:])
        tok_rstdb = sA.n

        res1_3 = res1.ap().rearrange("p (c w) -> p c w", c=DC)
        hbuf_3 = hbuf.ap().rearrange("p (c w) -> p c w", c=DC)

        @VEC
        def _(v, tok_rstdb=tok_rstdb):
            v.wait_ge(sA.h, tok_rstdb)
            v.tensor_tensor(out=res1_3, in0=res1_3, in1=rep_bcast(mu_b, DC, TS),
                            op=ALU.subtract)
            v.tensor_tensor(out=res1_3, in0=res1_3,
                                   in1=rep_bcast(rstd_b, DC, TS), op=ALU.mult)
        tok_ln1 = sV.n

        @SCA
        def _(sc, ts=ts, tok_ln1=tok_ln1):
            sc.wait_ge(sV.h, tok_ln1)
            for d in range(DC):
                sLn1.inc(sc.dma_start(out=ln1_sp[d, :, ts * TS:(ts + 1) * TS],
                                      in_=res1[:, d * TS:(d + 1) * TS]), 16)
        tok_ln1sp = sLn1.n

        abase_sq2 = sA.n
        mbase_sq2 = sMM.n
        @SCA
        def _(sc, tok_ln1=tok_ln1, mbase_sq2=mbase_sq2):
            for d in range(DC):
                if d >= 2:
                    sc.wait_ge(sMM.h, mbase_sq2 + d - 1)
                sc.activation(out=sqb[:, (d % 2) * TS:(d % 2 + 1) * TS],
                                     in_=res1[:, d * TS:(d + 1) * TS], func=AF.Square)
        tok_sq2 = sA.n

        @blk.tensor
        def _(t, abase_sq2=abase_sq2):
            for d in range(DC):
                t.wait_ge(sA.h, abase_sq2 + d + 1)
                sMM.inc(t.matmul(pstat[:, TS:2 * TS], onescb[:],
                                 sqb[:, (d % 2) * TS:(d % 2 + 1) * TS],
                                 start=(d == 0), stop=(d == DC - 1)))
        tok_ms = sMM.n

        @VEC
        def _(v, tok_ms=tok_ms):
            v.wait_ge(sMM.h, tok_ms)
            v.tensor_scalar_mul(out=scr2[:], in0=pstat[:, TS:2 * TS],
                                       scalar1=1.0 / D)
        tok_msr = sV.n

        @SCA
        def _(sc, tok_msr=tok_msr):
            sc.wait_ge(sV.h, tok_msr)
            sc.activation(out=scr2[:], in_=scr2[:], func=AF.Sqrt, bias=epsc[:])
        tok_sqr2 = sA.n

        @VEC
        def _(v, tok_sqr2=tok_sqr2):
            v.wait_ge(sA.h, tok_sqr2)
            v.reciprocal(out=scr2[:], in_=scr2[:])
        tok_rms = sV.n

        @blk.tensor
        def _(t, tok_rms=tok_rms):
            t.wait_ge(sV.h, tok_rms)
            sMM.inc(t.matmul(pbc[:], ones1[:], scr2[:], start=True, stop=True))
        tok_bc3 = sMM.n

        @SCA
        def _(sc, tok_bc3=tok_bc3):
            sc.wait_ge(sMM.h, tok_bc3)
            sc.copy(out=rstd_b[:], in_=pbc[:])
        tok_rmsb = sA.n

        @VEC
        def _(v, tok_rmsb=tok_rmsb):
            v.wait_ge(sA.h, tok_rmsb)
            for d in range(DC):
                v.tensor_scalar(out=hbuf[:, d * TS:(d + 1) * TS],
                                in0=res1[:, d * TS:(d + 1) * TS],
                                scalar1=wn_s[:, d:d + 1], scalar2=None, op0=ALU.mult)
            v.tensor_tensor(out=hbuf_3, in0=hbuf_3,
                                   in1=rep_bcast(rstd_b, DC, TS), op=ALU.mult)
        tok_h = sV.n

        abase_habs = sA.n
        vbase_m1 = sV.n

        @SCA
        def _(sc, tok_h=tok_h, vbase_m1=vbase_m1):
            sc.wait_ge(sV.h, tok_h)
            for d in range(DC):
                if d >= 2:
                    sc.wait_ge(sV.h, vbase_m1 + d - 1)
                sc.activation(out=paby[:, (d % 2) * TS:(d % 2 + 1) * TS],
                                     in_=hbuf[:, d * TS:(d + 1) * TS], func=AF.Abs)
        tok_habs = sA.n

        @VEC
        def _(v, abase_habs=abase_habs):
            for d in range(DC):
                v.wait_ge(sA.h, abase_habs + d + 1)
                if d == 0:
                    v.tensor_copy(out=m1t[:], in_=paby[:, 0:TS])
                else:
                    v.tensor_max(out=m1t[:], in0=m1t[:],
                                        in1=paby[:, (d % 2) * TS:(d % 2 + 1) * TS])
            for width in (64, 32):
                v.tensor_copy(out=trt[0:width, :], in_=m1t[width:2 * width, :])
                v.tensor_max(out=m1t[0:width, :], in0=m1t[0:width, :],
                             in1=trt[0:width, :])
            for width in (16, 8, 4, 2, 1):
                mask = [width + i if i < width else 0 for i in range(32)]
                v.stream_shuffle(out=trt[0:32, :], in_=m1t[0:32, :], mask=mask)
                v.tensor_max(out=m1t[0:width, :], in0=m1t[0:width, :],
                             in1=trt[0:width, :])
            v.tensor_scalar(out=scr1[:], in0=m1t[0:1, :], scalar1=EPS, scalar2=None,
                            op0=ALU.max)
            v.reciprocal(out=scr2[:], in_=scr1[:])
            v.tensor_scalar_mul(out=scr2[:], in0=scr2[:], scalar1=QB)
            v.wait_ge(sA.h, tok_w0done)
            v.tensor_scalar(out=scr1[:], in0=scr1[:], scalar1=mwc1[0:1, 0:1],
                                   scalar2=None, op0=ALU.mult)
        tok_s1 = sV.n

        @blk.tensor
        def _(t, tok_s1=tok_s1):
            t.wait_ge(sV.h, tok_s1)
            sMM.inc(t.matmul(pbc[:], ones1[:], scr2[:], start=True, stop=True))
        tok_bs1 = sMM.n

        @SCA
        def _(sc, tok_bs1=tok_bs1):
            sc.wait_ge(sMM.h, tok_bs1)
            sc.copy(out=mu_b[:], in_=pbc[:])
        tok_s1b = sA.n

        @blk.tensor
        def _(t, tok_s1b=tok_s1b):
            t.wait_ge(sA.h, tok_s1b)
            sMM.inc(t.matmul(pbc[:], ones1[:], scr1[:], start=True, stop=True))
        tok_bc1r = sMM.n

        @SCA
        def _(sc, ts=ts, tok_bc1r=tok_bc1r):
            sc.wait_ge(sMM.h, tok_bc1r)
            sc.copy(out=c1b[:, ts * TS:(ts + 1) * TS], in_=pbc[:])
        tok_c1b = sA.n

        @VEC
        def _(v, tok_s1b=tok_s1b):
            v.wait_ge(sA.h, tok_s1b)
            v.tensor_tensor(out=hbuf_3, in0=hbuf_3,
                                   in1=rep_bcast(mu_b, DC, TS), op=ALU.mult)
        tok_y1 = sV.n

        HW = DC * TS // 2
        for hh in range(2):
            @SCA
            def _(sc, hh=hh, tok_y1=tok_y1):
                sc.wait_ge(sV.h, tok_y1 if hh == 0 else sV.n)
                sc.activation(out=sgn_p[:], in_=hbuf[:, hh * HW:(hh + 1) * HW],
                              func=AF.Sign)
                sc.activation(out=paby[:], in_=hbuf[:, hh * HW:(hh + 1) * HW],
                                     func=AF.Abs)
            tok_absgn = sA.n

            @VEC
            def _(v, ts=ts, hh=hh, tok_absgn=tok_absgn, tok_ln1sp=tok_ln1sp):
                v.wait_ge(sA.h, tok_absgn)
                if hh == 0:
                    v.wait_ge(sLn1.h, tok_ln1sp)
                emit_peel(v, paby[:], res1[:, 0:HW])
                v.tensor_mul(out=res1[:, 0:HW], in0=paby[:], in1=sgn_p[:])
                v.tensor_tensor(
                    out=q1[:, ts * DC * TS + hh * HW:ts * DC * TS + (hh + 1) * HW],
                    in0=hbuf[:, hh * HW:(hh + 1) * HW], in1=res1[:, 0:HW],
                    op=ALU.subtract)
            tok_q1ts = sV.n
        prev_q1ts = tok_q1ts

    tok_q1 = sV.n
    tok_ln1_all = sLn1.n
    dbg_tensors.update(dict(q1=(q1, [128, NTS * DC * TS], BF16), c1b=(c1b, [128, S], F32)))
    if stage_n == 1:
        emit_taps(["q1", "c1b", "ybf", "sgv128", "mwc1", "mwc2"])
        ppst.close(); pst.close(); stPF.close()
        st.close()
        return nc, sorted(dbg_tensors.keys() & dbg)

    ppst.close()
    pst.close()

    f1L = contextlib.ExitStack()
    Mx = f1L.enter_context(nc.sbuf_tensor("Mx", [128, S], F32))
    scr3 = f1L.enter_context(nc.sbuf_tensor("scr3", [1, S], F32))
    f1 = contextlib.ExitStack()
    fsb = f1.enter_context
    wld = fsb(nc.sbuf_tensor("wld", [128, 3 * 2048], F32))
    qwv = fsb(nc.sbuf_tensor("qwv", [128, 2 * 2048], BF16))
    qwg = fsb(nc.sbuf_tensor("qwg", [128, 2 * 2048], BF16))
    wqt = fsb(nc.sbuf_tensor("wqt", [128, 2048], F32))
    wqs = fsb(nc.sbuf_tensor("wqs", [128, 2048], BF16))
    u_r = fsb(nc.sbuf_tensor("u_r", [128, 2 * S], F32))
    gc_t = fsb(nc.sbuf_tensor("gc_t", [128, TS], F32))
    sg_t = fsb(nc.sbuf_tensor("sg_t", [128, TS], F32))
    wo_r = fsb(nc.sbuf_tensor("wo_r", [128, 2 * 1024], F32))
    qo_t = fsb(nc.sbuf_tensor("qo_t", [128, 2 * 1024], BF16))
    woqt = fsb(nc.sbuf_tensor("woqt", [128, 1024], F32))
    woqs = fsb(nc.sbuf_tensor("woqs", [128, 1024], BF16))
    f1p = contextlib.ExitStack()
    pvg = f1p.enter_context(nc.psum_tensor("pvg", [128, 8 * TS], F32))

    @VEC
    def _(v):
        v.memset(Mx[:], 0.0)
    tok_mx0 = sV.n

    a0, v0, m0 = sA.n, sV.n, sMM.n
    TOK = {}
    a, v_, m = a0, v0, m0
    _wld_cnt = [0, 0, 0]
    for j in range(HC):
        for h, key in ((0, ("ldv", j)), (1, ("ldg", j))):
            slot = (2 * j + h) % 3
            _wld_cnt[slot] += 16
            TOK[key] = (rWLD.sems[slot].h, _wld_cnt[slot])
    _usp_cnt = [0, 0]
    for j in range(HC + 1):
        if j < HC:
            a += 2; TOK[("qv", j)] = a
            a += 2; TOK[("qg", j)] = a
        if j >= 1:
            jj = j - 1
            for ts in range(NTS):
                a += 1; TOK[("silu", jj, ts)] = a
                a += 1; TOK[("au", jj, ts)] = a
            _usp_cnt[jj % 2] += 16
            TOK[("usp", jj)] = (rUSP.sems[jj % 2].h, _usp_cnt[jj % 2])
        if j < HC:
            for h in (0, 1):
                a += 2; TOK[("woq", 2 * j + h)] = a
    for j in range(HC + 1):
        if j < HC:
            v_ += 1; TOK[("ternv", j)] = v_
            v_ += 1; TOK[("terng", j)] = v_
        if j >= 1:
            jj = j - 1
            for ts in range(NTS):
                v_ += 1; TOK[("gc", jj, ts)] = v_
                v_ += 1; TOK[("sg2", jj, ts)] = v_
                v_ += 1; TOK[("u", jj, ts)] = v_
                v_ += 1; TOK[("mx", jj, ts)] = v_
        if j < HC:
            v_ += 1; TOK[("wot", 2 * j)] = v_
            v_ += 1; TOK[("wot", 2 * j + 1)] = v_
    for j in range(HC):
        for ts in range(NTS):
            m += 1
            TOK[("mm", j, ts)] = m
    end_a, end_v, end_m = a, v_, m

    pP_v, pP_a = tok_q1, sA.n

    @blk.sync
    def _(s):
        for j in range(HC):
            if j < 2:
                s.wait_ge(sV.h, pP_v)
                s.wait_ge(sA.h, pP_a)
                s.wait_ge(sLn1.h, tok_ln1_all)
            else:
                s.wait_ge(sV.h, TOK[("terng", j - 2)])
            rWLD.inc(2 * j, s.dma_start(
                out=wld[:, (2 * j % 3) * 2048:(2 * j % 3 + 1) * 2048], in_=wgvt[j, 0]))
            if j >= 1:
                s.wait_ge(sV.h, TOK[("ternv", j - 1)])
            rWLD.inc(2 * j + 1, s.dma_start(
                out=wld[:, ((2 * j + 1) % 3) * 2048:((2 * j + 1) % 3 + 1) * 2048],
                in_=wgvt[j, 1]))

    @SCA
    def _(sc):
        sc.wait_ge(sA.h, tok_w0done)
        for j in range(HC + 1):
            if j < HC:
                sc.wait_ge(*TOK[("ldv", j)])
                if j >= 1:
                    sc.wait_ge(sV.h, TOK[("terng", j - 1)])
                sc.activation(out=wqt[:], in_=wld[:, (2 * j % 3) * 2048:(2 * j % 3 + 1) * 2048],
                              func=AF.Abs, scale=sgv128[:])
                sc.activation(out=wqs[:], in_=wld[:, (2 * j % 3) * 2048:(2 * j % 3 + 1) * 2048],
                                     func=AF.Sign, scale=sgv128[:])
                sc.wait_ge(*TOK[("ldg", j)])
                sc.wait_ge(sV.h, TOK[("ternv", j)])
                sc.activation(out=wqt[:], in_=wld[:, ((2 * j + 1) % 3) * 2048:((2 * j + 1) % 3 + 1) * 2048],
                              func=AF.Abs, scale=sgv128[:])
                sc.activation(out=wqs[:], in_=wld[:, ((2 * j + 1) % 3) * 2048:((2 * j + 1) % 3 + 1) * 2048],
                                     func=AF.Sign, scale=sgv128[:])
            if j >= 1:
                jj = j - 1
                r = jj % 2
                for ts in range(NTS):
                    sc.wait_ge(sV.h, TOK[("gc", jj, ts)])
                    sc.activation(out=sg_t[:], in_=gc_t[:], func=AF.Silu)
                    sc.wait_ge(sV.h, TOK[("u", jj, ts)])
                    sc.activation(out=gc_t[:],
                                         in_=u_r[:, r * S + ts * TS:r * S + (ts + 1) * TS],
                                         func=AF.Abs)
            for h in ((0, 1) if j < HC else ()):
                i = 2 * j + h
                wl = i % 2
                if i >= 2:
                    sc.wait_ge(sV.h, TOK[("wot", i - 2)])
                    dgp, jwp = divmod(i - 2, HC)
                    rQoS.inc(i - 2, sc.dma_start(
                        out=qo_sp[dgp, jwp // 4, :, (jwp % 4) * 1024:(jwp % 4 + 1) * 1024],
                        in_=qo_t[:, wl * 1024:(wl + 1) * 1024]))
                sc.wait_ge(rWO.sems[wl].h, 16 * (i // 2 + 1))
                if i >= 1:
                    sc.wait_ge(sV.h, TOK[("wot", i - 1)])
                sc.activation(out=woqt[:],
                              in_=wo_r[:, wl * 1024:(wl + 1) * 1024],
                              func=AF.Abs, scale=sout128[:])
                sc.activation(out=woqs[:],
                              in_=wo_r[:, wl * 1024:(wl + 1) * 1024],
                              func=AF.Sign, scale=sout128[:])
            if j == HC:
                for i in (2 * HC - 2, 2 * HC - 1):
                    wl = i % 2
                    sc.wait_ge(sV.h, TOK[("wot", i)])
                    dgp, jwp = divmod(i, HC)
                    rQoS.inc(i, sc.dma_start(
                        out=qo_sp[dgp, jwp // 4, :, (jwp % 4) * 1024:(jwp % 4 + 1) * 1024],
                        in_=qo_t[:, wl * 1024:(wl + 1) * 1024]))
    assert sA.n == end_a

    @VEC
    def _(v):
        for j in range(HC + 1):
            if j < HC:
                v.wait_ge(sA.h, TOK[("qv", j)])
                if j >= 2:
                    v.wait_ge(sMM.h, TOK[("mm", j - 2, NTS - 1)])
                v.scalar_tensor_tensor(
                    out=qwv[:, (j % 2) * 2048:(j % 2 + 1) * 2048], in0=wqt[:],
                    scalar=0.5, in1=wqs[:], op0=ALU.is_ge, op1=ALU.mult)
                v.wait_ge(sA.h, TOK[("qg", j)])
                v.scalar_tensor_tensor(
                    out=qwg[:, (j % 2) * 2048:(j % 2 + 1) * 2048], in0=wqt[:],
                    scalar=0.5, in1=wqs[:], op0=ALU.is_ge, op1=ALU.mult)
            if j >= 1:
                jj = j - 1
                r = jj % 2
                for ts in range(NTS):
                    q = (jj * NTS + ts) % 4
                    v.wait_ge(sMM.h, TOK[("mm", jj, ts)])
                    v.tensor_mul(out=gc_t[:],
                                 in0=pvg[:, (2 * q + 1) * TS:(2 * q + 2) * TS],
                                 in1=c1b[:, ts * TS:(ts + 1) * TS])
                    v.wait_ge(sA.h, TOK[("silu", jj, ts)])
                    v.tensor_mul(out=sg_t[:], in0=sg_t[:],
                                 in1=c1b[:, ts * TS:(ts + 1) * TS])
                    if jj >= 2:
                        v.wait_ge(*TOK[("usp", jj - 2)])
                        v.wait_ge(sA.h, TOK[("au", jj - 2, ts)])
                    v.tensor_mul(out=u_r[:, r * S + ts * TS:r * S + (ts + 1) * TS],
                                 in0=pvg[:, 2 * q * TS:(2 * q + 1) * TS],
                                 in1=sg_t[:])
                    v.wait_ge(sA.h, TOK[("au", jj, ts)])
                    v.tensor_max(out=Mx[:, ts * TS:(ts + 1) * TS],
                                 in0=Mx[:, ts * TS:(ts + 1) * TS], in1=gc_t[:])
            for h in ((0, 1) if j < HC else ()):
                i = 2 * j + h
                wl = i % 2
                v.wait_ge(sA.h, TOK[("woq", i)])
                if i >= 2:
                    v.wait_ge(rQoS.sems[wl].h, 16 * (i // 2))
                v.scalar_tensor_tensor(out=qo_t[:, wl * 1024:(wl + 1) * 1024],
                                       in0=woqt[:], scalar=0.5, in1=woqs[:],
                                       op0=ALU.is_ge, op1=ALU.mult)
    assert sV.n == end_v

    @blk.tensor
    def _(t):
        for j in range(HC):
            r = j % 2
            t.wait_ge(sV.h, TOK[("terng", j)])
            for ts in range(NTS):
                q = (j * NTS + ts) % 4
                if j >= 1:
                    t.wait_ge(sV.h, TOK[("u", j - 1, ts)])
                for k in range(DC):
                    t.matmul(pvg[:, 2 * q * TS:(2 * q + 1) * TS],
                             qwv[:, r * 2048 + k * 128:r * 2048 + (k + 1) * 128],
                             q1[:, (ts * DC + k) * TS:(ts * DC + k + 1) * TS],
                             start=(k == 0), stop=(k == DC - 1))
                for k in range(DC):
                    last = t.matmul(pvg[:, (2 * q + 1) * TS:(2 * q + 2) * TS],
                                    qwg[:, r * 2048 + k * 128:r * 2048 + (k + 1) * 128],
                                    q1[:, (ts * DC + k) * TS:(ts * DC + k + 1) * TS],
                                    start=(k == 0), stop=(k == DC - 1))
                sMM.inc(last)
    assert sMM.n == end_m

    @blk.gpsimd
    def _(g):
        for j in range(HC + 1):
            for h in ((0, 1) if j < HC else ()):
                i = 2 * j + h
                if i < 2:
                    g.wait_ge(sV.h, pP_v)
                    g.wait_ge(sA.h, pP_a)
                    g.wait_ge(sLn1.h, tok_ln1_all)
                else:
                    g.wait_ge(sA.h, TOK[("woq", i - 2)])
                dg, jw = divmod(i, HC)
                rWO.inc(i, g.dma_start(
                    out=wo_r[:, (i % 2) * 1024:(i % 2 + 1) * 1024],
                    in_=woutT[dg, jw]))
            if j >= 1:
                jj = j - 1
                g.wait_ge(sV.h, TOK[("mx", jj, NTS - 1)])
                rUSP.inc(jj, g.dma_start(out=u_sp[jj],
                                         in_=u_r[:, (jj % 2) * S:(jj % 2 + 1) * S]))

    tok_woq = end_a
    tok_qodone = end_v
    tok_qosp = [s.tok() for s in rQoS.sems]

    f1p.close()
    f1.close()
    f15 = contextlib.ExitStack()
    trt2 = f15.enter_context(nc.sbuf_tensor("trt2", [64, S], F32))
    f1p2 = contextlib.ExitStack()
    pbc2 = f1p2.enter_context(nc.psum_tensor("pbc2", [128, S], F32))

    @VEC
    def _(v):
        v.wait_ge(sV.h, end_v)
        for width in (64, 32):
            v.tensor_copy(out=trt2[0:width, :], in_=Mx[width:2 * width, :])
            v.tensor_max(out=Mx[0:width, :], in0=Mx[0:width, :], in1=trt2[0:width, :])
        for width in (16, 8, 4, 2, 1):
            mask = [width + i if i < width else 0 for i in range(32)]
            v.stream_shuffle(out=trt2[0:32, :], in_=Mx[0:32, :], mask=mask)
            v.tensor_max(out=Mx[0:width, :], in0=Mx[0:width, :], in1=trt2[0:width, :])
        v.tensor_scalar(out=scr3[:], in0=Mx[0:1, :], scalar1=EPS, scalar2=None,
                        op0=ALU.max)
        v.reciprocal(out=Mx[0:1, :], in_=scr3[:])
        v.tensor_scalar_mul(out=Mx[0:1, :], in0=Mx[0:1, :], scalar1=QB)
        v.wait_ge(sA.h, tok_w0done)
        v.tensor_scalar(out=scr3[:], in0=scr3[:], scalar1=mwc2[0:1, 0:1],
                               scalar2=None, op0=ALU.mult)
    tok_s2row = sV.n

    @blk.tensor
    def _(t):
        t.wait_ge(sV.h, tok_s2row)
        for ts in range(NTS):
            last = t.matmul(pbc2[:, ts * TS:(ts + 1) * TS], ones1[:],
                            Mx[0:1, ts * TS:(ts + 1) * TS], start=True, stop=True)
        sMM.inc(last)
    tok_bs2 = sMM.n

    @SCA
    def _(sc):
        sc.wait_ge(sMM.h, tok_bs2)
        sc.copy(out=s2b[:], in_=pbc2[:])
    tok_s2b = sA.n

    @blk.tensor
    def _(t):
        t.wait_ge(sA.h, tok_s2b)
        for ts in range(NTS):
            last = t.matmul(pbc2[:, ts * TS:(ts + 1) * TS], ones1[:],
                            scr3[0:1, ts * TS:(ts + 1) * TS], start=True, stop=True)
        sMM.inc(last)
    tok_bc2m = sMM.n

    @SCA
    def _(sc):
        sc.wait_ge(sMM.h, tok_bc2m)
        sc.copy(out=c2b[:], in_=pbc2[:])
    tok_c2b = sA.n

    dbg_tensors.update(dict(Mx=(Mx, [128, S], F32), s2b=(s2b, [128, S], F32),
                            c2b=(c2b, [128, S], F32)))
    if stage_n == 2:
        emit_taps(["Mx", "s2b", "c2b", "q1", "c1b"])
        f1p2.close(); f15.close(); f1L.close(); stPF.close()
        st.close()
        return nc, sorted(dbg_tensors.keys() & dbg)

    f1p2.close()
    f15.close()
    f1L.close()
    stPF.close()

    f2 = contextlib.ExitStack()
    f2b = f2.enter_context
    qo_r = f2b(nc.sbuf_tensor("qo_r", [128, 2 * 4096], BF16))
    q2_r = f2b(nc.sbuf_tensor("q2_r", [128, (HC // 4) * 2048], BF16))
    o_ts = f2b(nc.sbuf_tensor("o_ts", [128, DC * TS], F32))
    lbuf = f2b(nc.sbuf_tensor("lbuf", [128, DC * TS], F32))
    sqb2 = f2b(nc.sbuf_tensor("sqb2", [128, 2 * TS], BF16))
    mu2b = f2b(nc.sbuf_tensor("mu2b", [128, TS], F32))
    rstd2b = f2b(nc.sbuf_tensor("rstd2b", [128, TS], F32))
    ub2 = f2b(nc.sbuf_tensor("ub2", [128, 4 * TS], F32))
    zt2 = f2b(nc.sbuf_tensor("zt2", [128, 4 * TS], F32))
    mq2 = f2b(nc.sbuf_tensor("mq2", [128, 4 * TS], F32))
    q2c = f2b(nc.sbuf_tensor("q2c", [128, 4 * TS], BF16))
    f2p = contextlib.ExitStack()
    pout = f2p.enter_context(nc.psum_tensor("pout", [128, 8 * TS], F32))

    o_3 = o_ts.ap().rearrange("p (c w) -> p c w", c=DC)

    HYB_S = [-64.0, -32.0, -16.0, -8.0, -4.0, -2.0, -1.0, -0.5]

    BND = ([(sV.h, tok_s2row), (sV.h, tok_qodone), (sA.h, tok_woq),
            (sA.h, tok_c2b), (sLn1.h, tok_ln1_all)]
           + [TOK[("usp", HC - 2)], TOK[("usp", HC - 1)]] + list(tok_qosp))

    vfin = {}
    ub2_toks = {}
    st_tok = {}
    NP = HC // 4

    def emit_q2_slice(ts, mid_cb=None):
        vb, ab = sV.n, sA.n

        def voff(p):
            return 1 if (mid_cb is not None and p >= NP // 2) else 0
        for u in range(HC // 2):
            pp, lane = divmod(u, 2)
            vfin[(ts, u)] = vb + 20 * pp + voff(pp) + 19 + lane
        sl = slice(ts * TS, (ts + 1) * TS)

        @blk.gpsimd
        def _(g, ts=ts):
            for u in range(HC // 2):
                lane = u % 2
                if ts == 0:
                    g.wait_ge(*TOK[("usp", 2 * u)])
                    g.wait_ge(*TOK[("usp", 2 * u + 1)])
                    if u < 2:
                        for h_, v_ in BND:
                            g.wait_ge(h_, v_)
                if ts > 0 or u >= 2:
                    pu = (ts, u - 2) if u >= 2 else (ts - 1, u + HC // 2 - 2)
                    g.wait_ge(sV.h, vfin[pu])
                for half in range(2):
                    tk = rUB.inc(u, g.dma_start(
                        out=ub2[:, (2 * lane + half) * TS:(2 * lane + half + 1) * TS],
                        in_=u_sp[2 * u + half, :, ts * TS:(ts + 1) * TS]))
                ub2_toks[(ts, u)] = tk

        @VEC
        def _(v, ts=ts, vb=vb, ab=ab, sl=sl):
            s2bc = s2b.ap()[:, sl].rearrange("p (c w) -> p c w", c=1) \
                                  .broadcast_to((128, 2, TS))
            for p in range(NP):
                if mid_cb is not None and p == NP // 2:
                    mid_cb(v)
                for lane in range(2):
                    u = 2 * p + lane
                    v.wait_ge(*ub2_toks[(ts, u)])
                    ubv = ub2.ap()[:, lane * 2 * TS:(lane + 1) * 2 * TS] \
                             .rearrange("p (c w) -> p c w", c=2)
                    v.tensor_tensor(out=ubv, in0=ubv, in1=s2bc, op=ALU.mult)
                for k in range(8):
                    for lane in range(2):
                        c0, c1 = lane * 2 * TS, (lane + 1) * 2 * TS
                        v.wait_ge(sA.h, ab + 18 * p + 2 * k + lane + 1)
                        v.scalar_tensor_tensor(
                            out=zt2[:, c0:c1],
                            in0=mq2[:, c0:c1], scalar=HYB_S[k],
                            in1=(ub2 if k == 0 else zt2)[:, c0:c1],
                            op0=ALU.mult, op1=ALU.add)
                for lane in range(2):
                    c0, c1 = lane * 2 * TS, (lane + 1) * 2 * TS
                    v.scalar_tensor_tensor(
                        out=mq2[:, c0:c1], in0=zt2[:, c0:c1], scalar=-1.0,
                        in1=ub2[:, c0:c1], op0=ALU.mult, op1=ALU.add)
        assert sV.n == vb + 20 * NP + (1 if mid_cb is not None else 0)

        @SCA
        def _(sc, ts=ts, vb=vb, ab=ab):
            for p in range(NP):
                for k in range(8):
                    for lane in range(2):
                        c0, c1 = lane * 2 * TS, (lane + 1) * 2 * TS
                        sc.wait_ge(sV.h, vb + 20 * p + voff(p)
                                   + (1 + lane if k == 0 else 2 * k + 1 + lane))
                        sc.activation(out=mq2[:, c0:c1],
                                      in_=(ub2 if k == 0 else zt2)[:, c0:c1],
                                      func=AF.Sign, bias=pbh[:])
                for lane in range(2):
                    u = 2 * p + lane
                    c0, c1 = lane * 2 * TS, (lane + 1) * 2 * TS
                    sc.wait_ge(sV.h, vfin[(ts, u)])
                    pu = ((ts, u - 2) if u >= 2
                          else (ts - 1, u + HC // 2 - 2) if ts > 0 else None)
                    if pu is not None:
                        sc.wait_ge(*st_tok[pu])
                    sc.activation(out=q2c[:, lane * 1024:(lane + 1) * 1024],
                                  in_=mq2[:, c0:c1], func=AF.Copy, bias=-0.5)
                    st_tok[(ts, u)] = rQ2S.inc(lane, sc.dma_start(
                        out=q2_sp[ts, u // 2, :, (u % 2) * 1024:(u % 2 + 1) * 1024],
                        in_=q2c[:, lane * 1024:(lane + 1) * 1024]))
        assert sA.n == ab + 18 * NP

    prev_next_ok = None
    prev_res2 = None
    prev_mm_end = None
    prev_f2_m0 = None

    def emit_f2_loads_mms(ts, dg1_gate):
        nonlocal prev_mm_end, prev_f2_m0
        f2_m0 = sMM.n
        f2_toks = {}
        q2tok = {}

        @blk.sync
        def _(s, ts=ts, f2_m0=f2_m0, prev_mm_end=prev_mm_end,
              prev_f2_m0=prev_f2_m0):
            for h_, v_ in tok_qosp:
                s.wait_ge(h_, v_)
            if ts == 0:
                for h_, v_ in BND:
                    s.wait_ge(h_, v_)
            for cj in range(HC // 4):
                s.wait_ge(*st_tok[(ts, 2 * cj)])
                s.wait_ge(*st_tok[(ts, 2 * cj + 1)])
                if prev_f2_m0 is not None:
                    s.wait_ge(sMM.h, prev_f2_m0 + HC + 4 * cj + 4)
                q2tok[cj] = rQ2R.inc(cj, s.dma_start(
                    out=q2_r[:, cj * 2048:(cj + 1) * 2048],
                    in_=q2_sp[ts, cj]))
            for c in range(2 * (HC // 4)):
                dg, cj = divmod(c, HC // 4)
                if c >= 2:
                    s.wait_ge(sMM.h, f2_m0 + (c - 1) * 4)
                elif prev_mm_end is not None:
                    s.wait_ge(sMM.h, prev_mm_end)
                f2_toks[c] = rF2.inc(c, s.dma_start(
                    out=qo_r[:, (c % 2) * 4096:(c % 2 + 1) * 4096],
                    in_=qo_sp[dg, cj]))

        @blk.tensor
        def _(t, ts=ts, dg1_gate=dg1_gate, prev_next_ok=prev_next_ok):
            if prev_next_ok is not None:
                t.wait_ge(sA.h, prev_next_ok)
            else:
                t.wait_ge(sA.h, tok_c2b)
            for dg in range(2):
                if dg == 1:
                    t.wait_ge(sV.h, dg1_gate)
                for j in range(HC):
                    c = dg * (HC // 4) + j // 4
                    t.wait_ge(*f2_toks[c])
                    if dg == 0:
                        t.wait_ge(*q2tok[j // 4])
                    base = (c % 2) * 4096 + (j % 4) * 1024
                    rhs = q2_r[:, (j // 4) * 2048 + (j % 4) * TS:
                               (j // 4) * 2048 + (j % 4) * TS + TS]
                    for d in range(8):
                        last = t.matmul(pout[:, d * TS:(d + 1) * TS],
                                        qo_r[:, base + d * 128:base + (d + 1) * 128],
                                        rhs, start=(j == 0), stop=(j == HC - 1))
                    sMM.inc(last)
        prev_mm_end = sMM.n
        prev_f2_m0 = f2_m0
        assert sMM.n == f2_m0 + 2 * HC
        return f2_m0

    def mk_evdg0(ts, f2_m0):
        def cb(v):
            v.wait_ge(sMM.h, f2_m0 + HC)
            if ts > 0:
                v.wait_ge(sOut.h, 16 * DC * ts)
            c2h = c2b.ap()[:, ts * TS:(ts + 1) * TS].rearrange(
                "p (c w) -> p c w", c=1).broadcast_to((128, 8, TS))
            v.tensor_tensor(
                out=o_ts.ap()[:, 0:8 * TS].rearrange("p (c w) -> p c w", c=8),
                in0=pout.ap()[:, 0:8 * TS].rearrange("p (c w) -> p c w", c=8),
                in1=c2h, op=ALU.mult)
        return cb

    def emit_f2_epilogue(ts, f2_m0):
        nonlocal prev_next_ok, prev_res2

        @VEC
        def _(v, ts=ts, f2_m0=f2_m0):
            v.wait_ge(sMM.h, f2_m0 + 2 * HC)
            c2h = c2b.ap()[:, ts * TS:(ts + 1) * TS].rearrange(
                "p (c w) -> p c w", c=1).broadcast_to((128, 8, TS))
            v.tensor_tensor(
                out=o_ts.ap()[:, 8 * TS:16 * TS].rearrange("p (c w) -> p c w", c=8),
                in0=pout.ap()[:, 0:8 * TS].rearrange("p (c w) -> p c w", c=8),
                in1=c2h, op=ALU.mult)
        tok_ev2 = sV.n

        @blk.sync
        def _(s, ts=ts, prev_res2=prev_res2):
            s.wait_ge(sLn1.h, tok_ln1_all)
            if ts == 0:
                for h_, v_ in BND:
                    s.wait_ge(h_, v_)
            if prev_res2 is not None:
                s.wait_ge(sV.h, prev_res2)
            for d in range(DC):
                sLb.inc(s.dma_start(out=lbuf[:, d * TS:(d + 1) * TS],
                                    in_=ln1_sp[d, :, ts * TS:(ts + 1) * TS]), 16)
        tok_lb = sLb.n

        c2sl = c2b.ap()[:, ts * TS:(ts + 1) * TS].rearrange(
            "p (c w) -> p c w", c=1).broadcast_to((128, DC, TS))

        if "f2dbg" in dbg and ts == 0:
            draw = nc.dram_tensor("dbg_raw0", [128, DC * TS], F32, kind="ExternalOutput").ap()
            _ev, _lg0 = tok_ev2, sLG.n

            @blk.gpsimd
            def _(g):
                g.wait_ge(sV.h, _ev)
                sLG.inc(g.dma_start(out=draw, in_=o_ts[:]), 16)
            _dump1 = sLG.n

        @VEC
        def _(v, tok_lb=tok_lb, ts=ts):
            v.wait_ge(sLb.h, tok_lb)
            if "f2dbg" in dbg and ts == 0:
                v.wait_ge(sLG.h, _dump1)
            v.scalar_tensor_tensor(out=o_ts[:], in0=lbuf[:], scalar=2.0,
                                   in1=o_ts[:], op0=ALU.mult, op1=ALU.add)
        tok_res2 = sV.n
        prev_res2 = tok_res2
        if "f2dbg" in dbg and ts == 0:
            dres = nc.dram_tensor("dbg_res20", [128, DC * TS], F32, kind="ExternalOutput").ap()
            dlb = nc.dram_tensor("dbg_lbuf0", [128, DC * TS], F32, kind="ExternalOutput").ap()
            _r2 = tok_res2

            @blk.gpsimd
            def _(g):
                g.wait_ge(sV.h, _r2)
                sLG.inc(g.dma_start(out=dres, in_=o_ts[:]), 16)
                sLG.inc(g.dma_start(out=dlb, in_=lbuf[:]), 16)
            _dump2 = sLG.n

            @SCA
            def _(sc):
                sc.wait_ge(sLG.h, _dump2)


        abase_sq3 = sA.n
        mbase_sq3 = sMM.n
        @SCA
        def _(sc, tok_res2=tok_res2, mbase_sq3=mbase_sq3):
            sc.wait_ge(sV.h, tok_res2)
            for d in range(DC):
                if d >= 2:
                    sc.wait_ge(sMM.h, mbase_sq3 + DC + d - 1)
                sc.activation(out=sqb2[:, (d % 2) * TS:(d % 2 + 1) * TS],
                                     in_=o_ts[:, d * TS:(d + 1) * TS], func=AF.Square)
        tok_sq3 = sA.n

        @blk.tensor
        def _(t, tok_res2=tok_res2, abase_sq3=abase_sq3):
            t.wait_ge(sV.h, tok_res2)
            for d in range(DC):
                sMM.inc(t.matmul(pout[0:1, 0:TS], onesc[:], o_ts[:, d * TS:(d + 1) * TS],
                                 start=(d == 0), stop=(d == DC - 1)))
            for d in range(DC):
                t.wait_ge(sA.h, abase_sq3 + d + 1)
                sMM.inc(t.matmul(pout[0:1, TS:2 * TS], onescb[:],
                                 sqb2[:, (d % 2) * TS:(d % 2 + 1) * TS],
                                 start=(d == 0), stop=(d == DC - 1)))
        tok_st3 = sMM.n

        @VEC
        def _(v, tok_st3=tok_st3):
            v.wait_ge(sMM.h, tok_st3)
            v.tensor_scalar_mul(out=scr1[:], in0=pout[0:1, 0:TS], scalar1=1.0 / D)
            v.tensor_scalar_mul(out=scr2[:], in0=pout[0:1, TS:2 * TS], scalar1=1.0 / D)
            v.tensor_mul(out=scr4[:], in0=scr1[:], in1=scr1[:])
            v.tensor_sub(out=scr2[:], in0=scr2[:], in1=scr4[:])
        tok_var2 = sV.n

        @SCA
        def _(sc, tok_var2=tok_var2):
            sc.wait_ge(sV.h, tok_var2)
            sc.activation(out=scr2[:], in_=scr2[:], func=AF.Sqrt, bias=epsc[:])
        tok_sqr3 = sA.n

        @VEC
        def _(v, tok_sqr3=tok_sqr3):
            v.wait_ge(sA.h, tok_sqr3)
            v.reciprocal(out=scr2[:], in_=scr2[:])
        tok_rstd2 = sV.n

        @blk.tensor
        def _(t, tok_rstd2=tok_rstd2):
            t.wait_ge(sV.h, tok_rstd2)
            t.matmul(pout[:, 2 * TS:3 * TS], ones1[:], scr1[:], start=True, stop=True)
            sMM.inc(t.matmul(pout[:, 3 * TS:4 * TS], ones1[:], scr2[:], start=True,
                             stop=True))
        tok_bc6 = sMM.n

        @SCA
        def _(sc, tok_bc6=tok_bc6):
            sc.wait_ge(sMM.h, tok_bc6)
            sc.copy(out=mu2b[:], in_=pout[:, 2 * TS:3 * TS])
            sc.copy(out=rstd2b[:], in_=pout[:, 3 * TS:4 * TS])
        tok_bc6c = sA.n
        prev_next_ok = tok_bc6c

        @VEC
        def _(v, tok_bc6c=tok_bc6c):
            v.wait_ge(sA.h, tok_bc6c)
            v.tensor_tensor(out=o_3, in0=o_3, in1=rep_bcast(mu2b, DC, TS),
                            op=ALU.subtract)
            v.tensor_tensor(out=o_3, in0=o_3, in1=rep_bcast(rstd2b, DC, TS),
                                   op=ALU.mult)
        tok_out = sV.n

        @SCA
        def _(sc, ts=ts, tok_out=tok_out):
            sc.wait_ge(sV.h, tok_out)
            for d in range(DC):
                sOut.inc(sc.dma_start(out=outT[d, :, ts * TS:(ts + 1) * TS],
                                      in_=o_ts[:, d * TS:(d + 1) * TS]), 16)

    emit_q2_slice(0)
    for ts in range(NTS):
        dg1_gate = sV.n + (20 * (NP // 2) + 1 if ts + 1 < NTS else 1)
        f2_m0 = emit_f2_loads_mms(ts, dg1_gate)
        if ts + 1 < NTS:
            emit_q2_slice(ts + 1, mid_cb=mk_evdg0(ts, f2_m0))
        else:
            @VEC
            def _(v, ts=ts, f2_m0=f2_m0, cb=mk_evdg0(ts, f2_m0)):
                cb(v)
        emit_f2_epilogue(ts, f2_m0)

    tok_allout = sOut.n

    @blk.sync
    def _(s):
        s.wait_ge(sOut.h, tok_allout)

    f2p.close()
    f2.close()
    st.close()
    return nc, sorted(dbg_tensors.keys() & dbg)


def _prep_shared(Wi, Wg, Wdt, Wb, Wc, Wo, A_log, b_dt, w_norm, w_gv, w_out):
    f = np.ascontiguousarray
    shared = {
        "wiT": f(Wi.T), "wgT": f(Wg.T), "wdtT": f(Wdt.T),
        "wbrep": f(np.broadcast_to(Wb[:, :, None], (KST, NSC, 128))),
        "wcrep": f(np.broadcast_to(Wc[:, :, None], (KST, NSC, 128))),
        "woT": f(Wo.T), "alog": f(A_log), "bdt": f(b_dt.reshape(NSC, 1)),
        "wnorm": f(w_norm.reshape(DC, 128).T),
    }
    W4 = w_gv.reshape(2, HC, 128, DC, 128)
    shared["wgvt"] = f(W4.transpose(1, 0, 4, 3, 2).reshape(HC, 2, 128, DC * 128))
    V5 = w_out.reshape(2, 8, 128, HC, 128)
    shared["woutT"] = f(V5.transpose(0, 3, 4, 1, 2).reshape(2, HC, 128, 8 * 128))
    return shared


def kernel(L0_attn_x, L0_attn_Wi, L0_attn_Wo, L0_attn_A_log, L0_attn_Wb,
           L0_attn_Wc, L0_attn_Wdt, L0_attn_b_dt, L0_attn_Wg,
           L0_ffn_w_norm, L0_ffn_w_gv, L0_ffn_w_out):
    x = np.asarray(L0_attn_x, dtype=np.float32)
    B = x.shape[0]
    shared = _prep_shared(
        np.asarray(L0_attn_Wi, np.float32), np.asarray(L0_attn_Wg, np.float32),
        np.asarray(L0_attn_Wdt, np.float32), np.asarray(L0_attn_Wb, np.float32),
        np.asarray(L0_attn_Wc, np.float32), np.asarray(L0_attn_Wo, np.float32),
        np.asarray(L0_attn_A_log, np.float32), np.asarray(L0_attn_b_dt, np.float32),
        np.asarray(L0_ffn_w_norm, np.float32), np.asarray(L0_ffn_w_gv, np.float32),
        np.asarray(L0_ffn_w_out, np.float32))
    nc, _ = build_program()
    in_maps = []
    for b in range(B):
        m = dict(shared)
        m["xT"] = np.ascontiguousarray(x[b].T)
        in_maps.append(m)
    res = run_bass_kernel_spmd(nc, in_maps, core_ids=list(range(B)))
    out = np.empty((B, S, D), np.float32)
    for b in range(B):
        o = np.asarray(res.results[b]["outT"])
        out[b] = o.reshape(D, S).T
    return out


# version 16
# speedup vs baseline: 1.1470x; 1.0123x over previous
import contextlib
import numpy as np
import concourse.bass as bass
import concourse.mybir as mybir
from concourse.bass_utils import run_bass_kernel_spmd

F32 = mybir.dt.float32
BF16 = mybir.dt.bfloat16
ALU = mybir.AluOpType
AF = mybir.ActivationFunctionType
AX = mybir.AxisListType

S = 2048
D = 2048
NSC = 128
KST = 16
H = 8192
TS = 512
NTS = S // TS
DC = D // 128
HC = H // 128
EPS = 1e-5
QB = 127.0
STAGES = ("A0", "A1", "A2", "A3", "A", "P", "F15", "Q2", "F2")


class Sem:
    def __init__(self, stack, nc, name):
        self.h = stack.enter_context(nc.semaphore(name))
        self.n = 0

    def inc(self, instr, amt=1):
        instr.then_inc(self.h, amt)
        self.n += amt
        return self.n

    def tok(self):
        return (self.h, self.n)


class DRing:

    def __init__(self, stack, nc, name, slots):
        self.sems = [Sem(stack, nc, f"{name}{i}") for i in range(slots)]
        self.slots = slots

    def inc(self, idx, instr, amt=16):
        s = self.sems[idx % self.slots]
        s.inc(instr, amt)
        return (s.h, s.n)


def emit_peel(v, absbuf, tmpbuf, levels=7):
    for lvl in range(levels):
        hh = float(2 ** (levels - 1 - lvl))
        v.tensor_scalar(out=tmpbuf, in0=absbuf, scalar1=hh - 0.5, scalar2=hh,
                        op0=ALU.is_ge, op1=ALU.mult)
        v.tensor_tensor(out=absbuf, in0=absbuf, in1=tmpbuf, op=ALU.subtract)


def rep_bcast(t, c, w):
    return t.ap().rearrange("p (c w) -> p c w", c=1).broadcast_to((128, c, w))


class SyncEng:
    _PASS = ("wait_ge", "dma_start", "dma_start_transpose", "nop")

    def __init__(self, eng, sem):
        object.__setattr__(self, "_e", eng)
        object.__setattr__(self, "_s", sem)

    def __getattr__(self, name):
        f = getattr(self._e, name)
        if name in SyncEng._PASS or not callable(f):
            return f
        s = self._s
        e = self._e

        def call(*a, **kw):
            r = f(*a, **kw)
            s.inc(r)
            e.wait_ge(s.h, s.n)
            return r
        return call


def build_program(upto="F2", dbg=()):
    stage_n = STAGES.index(upto) - 4
    nc = bass.Bass("TRN2", target_bir_lowering=False, debug=False)
    st = contextlib.ExitStack()
    dbg = set(dbg)
    dbg_tensors = {}

    xT = nc.dram_tensor("xT", [D, S], F32, kind="ExternalInput").ap()
    wiT = nc.dram_tensor("wiT", [D, NSC], F32, kind="ExternalInput").ap()
    wgT = nc.dram_tensor("wgT", [D, NSC], F32, kind="ExternalInput").ap()
    wdtT = nc.dram_tensor("wdtT", [NSC, NSC], F32, kind="ExternalInput").ap()
    wbrep = nc.dram_tensor("wbrep", [KST, NSC, 128], F32, kind="ExternalInput").ap()
    wcrep = nc.dram_tensor("wcrep", [KST, NSC, 128], F32, kind="ExternalInput").ap()
    woT = nc.dram_tensor("woT", [NSC, D], F32, kind="ExternalInput").ap()
    alog = nc.dram_tensor("alog", [NSC, KST], F32, kind="ExternalInput").ap()
    bdt = nc.dram_tensor("bdt", [NSC, 1], F32, kind="ExternalInput").ap()
    wnorm = nc.dram_tensor("wnorm", [128, DC], F32, kind="ExternalInput").ap()
    wgvt = nc.dram_tensor("wgvt", [HC, 2, 128, DC * 128], F32, kind="ExternalInput").ap()
    woutT = nc.dram_tensor("woutT", [2, HC, 128, 8 * 128], F32, kind="ExternalInput").ap()
    outT = nc.dram_tensor("outT", [128, DC, S], F32, kind="ExternalOutput").ap()

    u_sp = nc.dram_tensor("u_sp", [HC, 128, S], F32, kind="Internal").ap()
    ln1_sp = nc.dram_tensor("ln1_sp", [128, DC, S], F32, kind="Internal").ap()
    qo_sp = nc.dram_tensor("qo_sp", [2, HC // 4, 128, 4 * 1024], BF16, kind="Internal").ap()
    q2_sp = nc.dram_tensor("q2_sp", [NTS, HC // 4, 128, 4 * TS], BF16, kind="Internal").ap()

    sLG = Sem(st, nc, "sLG")
    sSm = Sem(st, nc, "sSm")
    sMM = Sem(st, nc, "sMM")
    sV = Sem(st, nc, "sV")
    sA = Sem(st, nc, "sA")
    rW0 = DRing(st, nc, "rW0", 6)
    rPX = DRing(st, nc, "rPX", 2)
    rWLD = DRing(st, nc, "rWLD", 3)
    rWO = DRing(st, nc, "rWO", 2)
    rUB = DRing(st, nc, "rUB", 2)
    rF2 = DRing(st, nc, "rF2", 2)
    rQ2R = DRing(st, nc, "rQ2R", 16)
    rQ2S = DRing(st, nc, "rQ2S", 2)
    rUSP = DRing(st, nc, "rUSP", 2)
    sLn1 = Sem(st, nc, "sLn1")
    sLb = Sem(st, nc, "sLb")
    rQoS = DRing(st, nc, "rQoS", 2)
    sOut = Sem(st, nc, "sOut")

    sb = st.enter_context
    ones1 = sb(nc.sbuf_tensor("ones1", [1, 128], F32))
    onesc = sb(nc.sbuf_tensor("onesc", [128, 1], F32))
    onescb = sb(nc.sbuf_tensor("onescb", [128, 1], BF16))
    bdt_s = sb(nc.sbuf_tensor("bdt_s", [NSC, 1], F32))
    nega = sb(nc.sbuf_tensor("nega", [NSC, KST], F32))
    wn_s = sb(nc.sbuf_tensor("wn_s", [128, DC], F32))
    sgv128 = sb(nc.sbuf_tensor("sgv128", [128, 1], F32))
    sout128 = sb(nc.sbuf_tensor("sout128", [128, 1], F32))
    mwc1 = sb(nc.sbuf_tensor("mwc1", [1, 1], F32))
    mwc2 = sb(nc.sbuf_tensor("mwc2", [1, 1], F32))
    accW = sb(nc.sbuf_tensor("accW", [128, 256], F32))
    scr1 = sb(nc.sbuf_tensor("scr1", [1, TS], F32))
    scr2 = sb(nc.sbuf_tensor("scr2", [1, TS], F32))
    scr4 = sb(nc.sbuf_tensor("scr4", [1, TS], F32))
    epsc = sb(nc.sbuf_tensor("epsc", [1, 1], F32))

    ybf = sb(nc.sbuf_tensor("ybf", [NSC, S], BF16))
    c2b = sb(nc.sbuf_tensor("c2b", [128, S], F32))
    s2b = sb(nc.sbuf_tensor("s2b", [128, S], F32))
    pbh = sb(nc.sbuf_tensor("pbh", [128, 1], F32))
    pbm = sb(nc.sbuf_tensor("pbm", [128, 1], F32))
    stPF = contextlib.ExitStack()

    blk = st.enter_context(nc.Block())

    def VEC(f):
        blk.vector(lambda v: f(SyncEng(v, sV)))

    def SCA(f):
        blk.scalar(lambda sc: f(SyncEng(sc, sA)))

    def emit_taps(names):
        taps = []
        for nm in names:
            if nm not in dbg:
                continue
            hnd, shape, dt = dbg_tensors[nm]
            ap = nc.dram_tensor("dbg_" + nm, shape, F32, kind="ExternalOutput").ap()
            taps.append((hnd, ap))
        vv, aa, mm = sV.n, sA.n, sMM.n

        @blk.gpsimd
        def _(g):
            g.wait_ge(sV.h, vv)
            g.wait_ge(sA.h, aa)
            g.wait_ge(sMM.h, mm)
            for hnd, ap in taps:
                sLG.inc(g.dma_start(out=ap, in_=hnd[:]), 16)
            g.wait_ge(sLG.h, sLG.n)

    ast = contextlib.ExitStack()
    asb = ast.enter_context
    wdt_s = asb(nc.sbuf_tensor("wdt_s", [NSC, NSC], BF16))
    wb_s = asb(nc.sbuf_tensor("wb_s", [NSC, KST * 128], BF16))
    wc_s = asb(nc.sbuf_tensor("wc_s", [NSC, KST * 128], BF16))
    wo_s = asb(nc.sbuf_tensor("wo_s", [NSC, D], BF16))
    alog_s = asb(nc.sbuf_tensor("alog_s", [NSC, KST], F32))
    zf = asb(nc.sbuf_tensor("zf", [NSC, S], F32))
    zbf = asb(nc.sbuf_tensor("zbf", [NSC, S], BF16))
    gsil = asb(nc.sbuf_tensor("gsil", [NSC, S], F32))
    dt_s = asb(nc.sbuf_tensor("dt_s", [NSC, S], F32))
    w_s = asb(nc.sbuf_tensor("w_s", [NSC, S], F32))
    ak = asb(nc.sbuf_tensor("ak", [NSC, S], F32))
    uk = asb(nc.sbuf_tensor("uk", [NSC, S], F32))
    stk = asb(nc.sbuf_tensor("stk", [NSC, S], F32))
    yacc = asb(nc.sbuf_tensor("yacc", [NSC, S], F32))
    ptmp = asb(nc.sbuf_tensor("ptmp", [NSC, S], F32))
    sp1 = asb(nc.sbuf_tensor("sp1", [NSC, S], F32))
    sp2 = asb(nc.sbuf_tensor("sp2", [NSC, S], F32))
    ast0 = contextlib.ExitStack()
    xbf = ast0.enter_context(nc.sbuf_tensor("xbf", [128, DC * S], BF16))
    wi_s = ast0.enter_context(nc.sbuf_tensor("wi_s", [128, DC * NSC], BF16))
    wg_s = ast0.enter_context(nc.sbuf_tensor("wg_s", [128, DC * NSC], BF16))

    a1 = contextlib.ExitStack()
    pz = a1.enter_context(nc.psum_tensor("pz", [NSC, S], F32))
    pg = a1.enter_context(nc.psum_tensor("pg", [NSC, S], F32))

    @blk.gpsimd
    def _(g):
        for k in range(DC):
            sLG.inc(g.dma_start(out=xbf[:, k * S:(k + 1) * S],
                                in_=xT[k * 128:(k + 1) * 128, :]), 16)
        for k in range(DC):
            sLG.inc(g.dma_start(out=wi_s[:, k * NSC:(k + 1) * NSC],
                                in_=wiT[k * 128:(k + 1) * 128, :]), 16)
        for k in range(DC):
            sLG.inc(g.dma_start(out=wg_s[:, k * NSC:(k + 1) * NSC],
                                in_=wgT[k * 128:(k + 1) * 128, :]), 16)
        sLG.inc(g.dma_start(out=wdt_s[:], in_=wdtT[:]), 16)
        for k in range(KST):
            sLG.inc(g.dma_start(out=wb_s[:, k * 128:(k + 1) * 128], in_=wbrep[k]), 16)
        for k in range(KST):
            sLG.inc(g.dma_start(out=wc_s[:, k * 128:(k + 1) * 128], in_=wcrep[k]), 16)
        sLG.inc(g.dma_start(out=wo_s[:], in_=woT[:]), 16)
    tok_gload = sLG.n

    @blk.sync
    def _(s):
        sSm.inc(s.dma_start(out=alog_s[:], in_=alog[:]), 16)
        sSm.inc(s.dma_start(out=bdt_s[:], in_=bdt[:]), 16)
        sSm.inc(s.dma_start(out=wn_s[:], in_=wnorm[:]), 16)
    tok_sload = sSm.n

    @VEC
    def _(v):
        v.memset(ones1[:], 1.0)
        v.memset(onesc[:], 1.0)
        v.memset(onescb[:], 1.0)
        v.memset(epsc[:], EPS)
        v.memset(yacc[:], 0.0)
        v.memset(pbh[:], 0.5)
        v.memset(pbm[:], -0.5)
    tok_init = sV.n

    if stage_n == -4:
        emit_taps([])
        ast0.close(); a1.close(); ast.close(); stPF.close(); st.close()
        return nc, []

    @blk.tensor
    def _(t):
        t.wait_ge(sLG.h, tok_gload)
        for ts in range(NTS):
            for k in range(DC):
                last = t.matmul(pz[:, ts * TS:(ts + 1) * TS],
                                wi_s[:, k * NSC:(k + 1) * NSC],
                                xbf[:, k * S + ts * TS:k * S + (ts + 1) * TS],
                                start=(k == 0), stop=(k == DC - 1))
            sMM.inc(last)
        for ts in range(NTS):
            for k in range(DC):
                last = t.matmul(pg[:, ts * TS:(ts + 1) * TS],
                                wg_s[:, k * NSC:(k + 1) * NSC],
                                xbf[:, k * S + ts * TS:k * S + (ts + 1) * TS],
                                start=(k == 0), stop=(k == DC - 1))
            sMM.inc(last)
    tok_zg = sMM.n

    vbase_zf = sV.n

    @VEC
    def _(v):
        for ts in range(NTS):
            v.wait_ge(sMM.h, tok_zg - 8 + ts + 1)
            v.tensor_copy(out=zf[:, ts * TS:(ts + 1) * TS],
                          in_=pz[:, ts * TS:(ts + 1) * TS])
    tok_zf = sV.n

    @SCA
    def _(sc):
        for ts in range(NTS):
            sc.wait_ge(sV.h, vbase_zf + ts + 1)
            sc.copy(out=zbf[:, ts * TS:(ts + 1) * TS],
                    in_=pz[:, ts * TS:(ts + 1) * TS])
        for ts in range(NTS):
            sc.wait_ge(sMM.h, tok_zg - 4 + ts + 1)
            sc.activation(out=gsil[:, ts * TS:(ts + 1) * TS],
                                 in_=pg[:, ts * TS:(ts + 1) * TS], func=AF.Silu)
        sc.wait_ge(sSm.h, tok_sload)
        sc.activation(out=nega[:], in_=alog_s[:], func=AF.Exp)
    tok_aevac = sA.n

    @VEC
    def _(v):
        v.wait_ge(sA.h, tok_aevac)
        v.tensor_scalar_mul(out=nega[:], in0=nega[:], scalar1=-1.0)
    tok_nega = sV.n

    if stage_n == -3:
        emit_taps(["zf", "gsil"]) if False else None
        dbg_tensors.update(dict(zf=(zf, [NSC, S], F32), gsil=(gsil, [NSC, S], F32)))
        emit_taps(["zf", "gsil"])
        ast0.close(); a1.close(); ast.close(); stPF.close(); st.close()
        return nc, sorted(dbg_tensors.keys() & dbg)

    a1.close()
    a2 = contextlib.ExitStack()
    pdt = a2.enter_context(nc.psum_tensor("pdt", [NSC, S], F32))

    @blk.tensor
    def _(t):
        t.wait_ge(sA.h, tok_aevac)
        t.wait_ge(sV.h, tok_zf)
        for ts in range(NTS):
            last = t.matmul(pdt[:, ts * TS:(ts + 1) * TS], wdt_s[:],
                            zbf[:, ts * TS:(ts + 1) * TS], start=True, stop=True)
        sMM.inc(last)
    tok_dtmm = sMM.n

    @SCA
    def _(sc):
        sc.wait_ge(sMM.h, tok_dtmm)
        sc.activation(out=sp1[:], in_=pdt[:], func=AF.Relu, bias=bdt_s[:])
        sc.activation(out=sp2[:], in_=pdt[:], func=AF.Abs, bias=bdt_s[:])
        sc.wait_ge(sA.h, sA.n)
        sc.activation(out=sp2[:], in_=sp2[:], func=AF.Exp, scale=-1.0)
        sc.wait_ge(sA.h, sA.n)
        sc.activation(out=sp2[:], in_=sp2[:], func=AF.Ln, bias=1.0)
    tok_sp = sA.n

    @VEC
    def _(v):
        v.wait_ge(sA.h, tok_sp)
        v.tensor_add(out=dt_s[:], in0=sp1[:], in1=sp2[:])
        v.tensor_mul(out=w_s[:], in0=dt_s[:], in1=zf[:])
    tok_dt = sV.n

    if stage_n == -2:
        dbg_tensors.update(dict(dt_s=(dt_s, [NSC, S], F32)))
        emit_taps(["dt_s"])
        a2.close(); ast0.close(); ast.close(); stPF.close(); st.close()
        return nc, sorted(dbg_tensors.keys() & dbg)

    a2.close()
    a3 = contextlib.ExitStack()
    pb = a3.enter_context(nc.psum_tensor("pb", [NSC, S], F32))
    pc = a3.enter_context(nc.psum_tensor("pc", [NSC, S], F32))

    vbase_k = sV.n

    def _kcum(k):
        return 0 if k == 0 else 4 + 5 * (k - 1)

    def tok_uk(k):
        return vbase_k + _kcum(k) + 2

    def tok_pk(k):
        return vbase_k + _kcum(k) + (4 if k == 0 else 5)

    mbase_k = sMM.n

    def tok_bmm(k):
        return mbase_k + 2 * k + 1

    def tok_cmm(k):
        return mbase_k + 2 * k + 2

    @blk.tensor
    def _(t):
        t.wait_ge(sV.h, tok_dt)
        for k in range(KST):
            if k > 0:
                t.wait_ge(sV.h, tok_uk(k - 1))
            for ts in range(NTS):
                last = t.matmul(pb[:, ts * TS:(ts + 1) * TS],
                                wb_s[:, k * 128:(k + 1) * 128],
                                zbf[:, ts * TS:(ts + 1) * TS], start=True, stop=True)
            sMM.inc(last)
            if k > 0:
                t.wait_ge(sV.h, tok_pk(k - 1))
            for ts in range(NTS):
                last = t.matmul(pc[:, ts * TS:(ts + 1) * TS],
                                wc_s[:, k * 128:(k + 1) * 128],
                                zbf[:, ts * TS:(ts + 1) * TS], start=True, stop=True)
            sMM.inc(last)

    @VEC
    def _(v):
        v.wait_ge(sV.h, tok_nega)
        for k in range(KST):
            v.wait_ge(sMM.h, tok_bmm(k))
            v.tensor_scalar(out=ak[:], in0=dt_s[:], scalar1=nega[:, k:k + 1],
                            scalar2=1.0, op0=ALU.mult, op1=ALU.add)
            v.tensor_mul(out=uk[:], in0=w_s[:], in1=pb[:])
            v.tensor_tensor_scan(out=stk[:], data0=ak[:], data1=uk[:],
                                 initial=0.0, op0=ALU.mult, op1=ALU.add)
            v.wait_ge(sMM.h, tok_cmm(k))
            if k == 0:
                v.tensor_mul(out=yacc[:], in0=stk[:], in1=pc[:])
            else:
                v.tensor_mul(out=ptmp[:], in0=stk[:], in1=pc[:])
                v.tensor_add(out=yacc[:], in0=yacc[:], in1=ptmp[:])
        v.tensor_mul(out=ybf[:], in0=yacc[:], in1=gsil[:])
    tok_ybf = sV.n
    assert sV.n == vbase_k + _kcum(KST) + 1, (sV.n, vbase_k, _kcum(KST))

    if stage_n == -1:
        dbg_tensors.update(dict(ybf=(ybf, [NSC, S], BF16)))
        emit_taps(["ybf"])
        a3.close(); ast0.close(); ast.close(); stPF.close(); st.close()
        return nc, sorted(dbg_tensors.keys() & dbg)

    ast0.close()
    aw = contextlib.ExitStack()
    wmean = aw.enter_context(nc.sbuf_tensor("wmean", [128, 6 * 2048], F32))
    wabs = aw.enter_context(nc.sbuf_tensor("wabs", [128, 2048], F32))

    w0_tiles = []
    for j in range(HC):
        for vg in range(2):
            w0_tiles.append((wgvt[j, vg], 2048))
    for dg in range(2):
        for j in range(HC):
            w0_tiles.append((woutT[dg, j], 1024))

    abase_w0 = sA.n
    vbase_w0 = sV.n

    @SCA
    def _(sc):
        for i, (_src, width) in enumerate(w0_tiles):
            if i % 2:
                continue
            e = i // 2
            sl = 2 * (e % 3)
            sc.wait_ge(rW0.sems[sl].h, 16 * (e // 3 + 1))
            sc.activation(out=wabs[:, 0:width],
                          in_=wmean[:, sl * 2048:sl * 2048 + width],
                          func=AF.Abs, accum_out=accW[:, i:i + 1])
    tok_w0acc = sA.n

    @VEC
    def _(v):
        for i, (_src, width) in enumerate(w0_tiles):
            if i % 2 == 0:
                continue
            o = i // 2
            sl = 2 * (o % 3) + 1
            v.wait_ge(rW0.sems[sl].h, 16 * (o // 3 + 1))
            v.tensor_reduce(out=accW[:, i:i + 1],
                            in_=wmean[:, sl * 2048:sl * 2048 + width],
                            axis=AX.X, op=ALU.add, apply_absolute_value=True)
    tok_w0accv = sV.n

    @blk.sync
    def _(s):
        s.wait_ge(sMM.h, tok_zg)
        for i, (src, width) in enumerate(w0_tiles):
            if i % 2:
                continue
            e = i // 2
            sl = 2 * (e % 3)
            if e >= 3:
                s.wait_ge(sA.h, abase_w0 + (e - 3) + 1)
            rW0.inc(sl, s.dma_start(out=wmean[:, sl * 2048:sl * 2048 + width],
                                    in_=src))

    @blk.gpsimd
    def _(g):
        g.wait_ge(sMM.h, tok_zg)
        for i, (src, width) in enumerate(w0_tiles):
            if i % 2 == 0:
                continue
            o = i // 2
            sl = 2 * (o % 3) + 1
            if o >= 3:
                g.wait_ge(sV.h, vbase_w0 + (o - 3) + 1)
            rW0.inc(sl, g.dma_start(out=wmean[:, sl * 2048:sl * 2048 + width],
                                    in_=src))

    a3.close()
    a4 = contextlib.ExitStack()
    pw0 = a4.enter_context(nc.psum_tensor("pw0", [1, 2], F32))
    pw1 = a4.enter_context(nc.psum_tensor("pw1", [128, 2], F32))
    redW = a4.enter_context(nc.sbuf_tensor("redW", [128, 2], F32))
    rowW = a4.enter_context(nc.sbuf_tensor("rowW", [1, 2], F32))

    @VEC
    def _(v):
        v.wait_ge(sA.h, tok_w0acc)
        v.tensor_reduce(out=redW[:, 0:1], in_=accW[:, 0:128], axis=AX.X, op=ALU.add)
        v.tensor_reduce(out=redW[:, 1:2], in_=accW[:, 128:256], axis=AX.X,
                               op=ALU.add)
    tok_redW = sV.n

    @blk.tensor
    def _(t):
        t.wait_ge(sV.h, tok_redW)
        t.matmul(pw0[:, 0:1], onesc[:], redW[:, 0:1], start=True, stop=True)
        sMM.inc(t.matmul(pw0[:, 1:2], onesc[:], redW[:, 1:2], start=True, stop=True))
    tok_w0mm = sMM.n

    @VEC
    def _(v):
        v.wait_ge(sMM.h, tok_w0mm)
        v.tensor_scalar(out=rowW[:, 0:1], in0=pw0[:, 0:1], scalar1=1.0 / (2 * H * D),
                        scalar2=EPS, op0=ALU.mult, op1=ALU.max)
        v.tensor_scalar(out=rowW[:, 1:2], in0=pw0[:, 1:2], scalar1=1.0 / (D * H),
                        scalar2=EPS, op0=ALU.mult, op1=ALU.max)
        v.tensor_scalar_mul(out=mwc1[:], in0=rowW[:, 0:1], scalar1=1.0 / QB)
        v.tensor_scalar_mul(out=mwc2[:], in0=rowW[:, 1:2], scalar1=1.0 / QB)
        v.reciprocal(out=rowW[:], in_=rowW[:])
    tok_w0row = sV.n

    @blk.tensor
    def _(t):
        t.wait_ge(sV.h, tok_w0row)
        sMM.inc(t.matmul(pw1[:], ones1[:], rowW[:], start=True, stop=True))
    tok_w0bc = sMM.n

    @SCA
    def _(sc):
        sc.wait_ge(sMM.h, tok_w0bc)
        sc.copy(out=sgv128[:], in_=pw1[:, 0:1])
        sc.copy(out=sout128[:], in_=pw1[:, 1:2])
    tok_w0done = sA.n
    a4.close()

    dbg_tensors.update(dict(
        zf=(zf, [NSC, S], F32), gsil=(gsil, [NSC, S], F32),
        dt_s=(dt_s, [NSC, S], F32), yacc=(yacc, [NSC, S], F32),
        ybf=(ybf, [NSC, S], BF16), sgv128=(sgv128, [128, 1], F32),
        mwc1=(mwc1, [1, 1], F32), mwc2=(mwc2, [1, 1], F32),
    ))
    if stage_n == 0:
        emit_taps(["zf", "gsil", "dt_s", "yacc", "ybf", "sgv128", "mwc1", "mwc2"])
        aw.close(); ast0.close() if False else None; ast.close(); stPF.close()
        st.close()
        return nc, sorted(dbg_tensors.keys() & dbg)

    aw.close()
    ast.close()

    q1 = stPF.enter_context(nc.sbuf_tensor("q1", [128, NTS * DC * TS], BF16))
    c1b = stPF.enter_context(nc.sbuf_tensor("c1b", [128, S], F32))
    pst = contextlib.ExitStack()
    psb = pst.enter_context
    wo2_s = psb(nc.sbuf_tensor("wo2_s", [NSC, D], BF16))
    xr = psb(nc.sbuf_tensor("xr", [128, 2 * TS], F32))
    res1 = psb(nc.sbuf_tensor("res1", [128, DC * TS], F32))
    hbuf = psb(nc.sbuf_tensor("hbuf", [128, DC * TS], F32))
    paby = psb(nc.sbuf_tensor("paby", [128, DC * TS // 2], F32))
    sgn_p = psb(nc.sbuf_tensor("sgn_p", [128, DC * TS // 2], BF16))
    sqb = psb(nc.sbuf_tensor("sqb", [128, 2 * TS], BF16))
    m1t = psb(nc.sbuf_tensor("m1t", [128, TS], F32))
    trt = psb(nc.sbuf_tensor("trt", [64, TS], F32))
    mu_b = psb(nc.sbuf_tensor("mu_b", [128, TS], F32))
    rstd_b = psb(nc.sbuf_tensor("rstd_b", [128, TS], F32))
    ppst = contextlib.ExitStack()
    pwo = ppst.enter_context(nc.psum_tensor("pwo", [128, 2 * TS], F32))
    pstat = ppst.enter_context(nc.psum_tensor("pstat", [1, 2 * TS], F32))
    pbc = ppst.enter_context(nc.psum_tensor("pbc", [128, TS], F32))

    @blk.gpsimd
    def _(g):
        g.wait_ge(sA.h, tok_w0done)
        sLG.inc(g.dma_start(out=wo2_s[:], in_=woT[:]), 16)
    tok_wo2 = sLG.n

    pA_v, pA_a, pA_m = sV.n, sA.n, sMM.n
    prev_q1ts = None
    prev_xadd_end = None
    for ts in range(NTS):
        abase_evac = sA.n
        @blk.tensor
        def _(t, ts=ts, abase_evac=abase_evac):
            t.wait_ge(sLG.h, tok_wo2)
            t.wait_ge(sV.h, tok_ybf)
            for d in range(DC):
                if d >= 2:
                    t.wait_ge(sA.h, abase_evac + (d - 1))
                sMM.inc(t.matmul(pwo[:, (d % 2) * TS:(d % 2 + 1) * TS],
                                 wo2_s[:, d * 128:(d + 1) * 128],
                                 ybf[:, ts * TS:(ts + 1) * TS], start=True, stop=True))
        tok_womm = sMM.n

        @SCA
        def _(sc, tok_womm=tok_womm, prev_q1ts=prev_q1ts):
            if prev_q1ts is not None:
                sc.wait_ge(sV.h, prev_q1ts)
            for d in range(DC):
                sc.wait_ge(sMM.h, tok_womm - DC + d + 1)
                sc.copy(out=res1[:, d * TS:(d + 1) * TS],
                        in_=pwo[:, (d % 2) * TS:(d % 2 + 1) * TS])
        tok_evac = sA.n

        vbase_x = sV.n

        xr_toks = {}

        @blk.sync
        def _(s, ts=ts, vbase_x=vbase_x, prev_xadd_end=prev_xadd_end):
            if ts == 0:
                s.wait_ge(sV.h, pA_v)
                s.wait_ge(sA.h, pA_a)
                s.wait_ge(sMM.h, pA_m)
            for d in range(DC):
                if d >= 2:
                    s.wait_ge(sV.h, vbase_x + (d - 1))
                elif prev_xadd_end is not None:
                    s.wait_ge(sV.h, prev_xadd_end)
                xr_toks[d] = rPX.inc(d, s.dma_start(
                    out=xr[:, (d % 2) * TS:(d % 2 + 1) * TS],
                    in_=xT[d * 128:(d + 1) * 128, ts * TS:(ts + 1) * TS]))

        @VEC
        def _(v, tok_evac=tok_evac):
            for d in range(DC):
                v.wait_ge(*xr_toks[d])
                v.wait_ge(sA.h, tok_evac - DC + d + 1)
                v.tensor_add(out=res1[:, d * TS:(d + 1) * TS],
                                    in0=res1[:, d * TS:(d + 1) * TS],
                                    in1=xr[:, (d % 2) * TS:(d % 2 + 1) * TS])
        tok_res1 = sV.n
        prev_xadd_end = tok_res1

        abase_sq = sA.n
        mbase_sq = sMM.n
        @SCA
        def _(sc, tok_res1=tok_res1, mbase_sq=mbase_sq):
            sc.wait_ge(sV.h, tok_res1)
            for d in range(DC):
                if d >= 2:
                    sc.wait_ge(sMM.h, mbase_sq + DC + d - 1)
                sc.activation(out=sqb[:, (d % 2) * TS:(d % 2 + 1) * TS],
                                     in_=res1[:, d * TS:(d + 1) * TS], func=AF.Square)
        tok_sq = sA.n

        @blk.tensor
        def _(t, tok_res1=tok_res1, abase_sq=abase_sq):
            t.wait_ge(sV.h, tok_res1)
            for d in range(DC):
                sMM.inc(t.matmul(pstat[:, 0:TS], onesc[:], res1[:, d * TS:(d + 1) * TS],
                                 start=(d == 0), stop=(d == DC - 1)))
            for d in range(DC):
                t.wait_ge(sA.h, abase_sq + d + 1)
                sMM.inc(t.matmul(pstat[:, TS:2 * TS], onescb[:],
                                 sqb[:, (d % 2) * TS:(d % 2 + 1) * TS],
                                 start=(d == 0), stop=(d == DC - 1)))
        tok_st12 = sMM.n

        @VEC
        def _(v, tok_st12=tok_st12):
            v.wait_ge(sMM.h, tok_st12)
            v.tensor_scalar_mul(out=scr1[:], in0=pstat[:, 0:TS], scalar1=1.0 / D)
            v.tensor_scalar_mul(out=scr2[:], in0=pstat[:, TS:2 * TS], scalar1=1.0 / D)
            v.tensor_mul(out=scr4[:], in0=scr1[:], in1=scr1[:])
            v.tensor_sub(out=scr2[:], in0=scr2[:], in1=scr4[:])
        tok_var = sV.n

        @SCA
        def _(sc, tok_var=tok_var):
            sc.wait_ge(sV.h, tok_var)
            sc.activation(out=scr2[:], in_=scr2[:], func=AF.Sqrt, bias=epsc[:])
        tok_sqv = sA.n

        @VEC
        def _(v, tok_sqv=tok_sqv):
            v.wait_ge(sA.h, tok_sqv)
            v.reciprocal(out=scr2[:], in_=scr2[:])
        tok_rstd = sV.n

        @blk.tensor
        def _(t, tok_rstd=tok_rstd):
            t.wait_ge(sV.h, tok_rstd)
            sMM.inc(t.matmul(pbc[:], ones1[:], scr1[:], start=True, stop=True))
        tok_bc1 = sMM.n

        @SCA
        def _(sc, tok_bc1=tok_bc1):
            sc.wait_ge(sMM.h, tok_bc1)
            sc.copy(out=mu_b[:], in_=pbc[:])
        tok_mub = sA.n

        @blk.tensor
        def _(t, tok_mub=tok_mub):
            t.wait_ge(sA.h, tok_mub)
            sMM.inc(t.matmul(pbc[:], ones1[:], scr2[:], start=True, stop=True))
        tok_bc2 = sMM.n

        @SCA
        def _(sc, tok_bc2=tok_bc2):
            sc.wait_ge(sMM.h, tok_bc2)
            sc.copy(out=rstd_b[:], in_=pbc[:])
        tok_rstdb = sA.n

        res1_3 = res1.ap().rearrange("p (c w) -> p c w", c=DC)
        hbuf_3 = hbuf.ap().rearrange("p (c w) -> p c w", c=DC)

        @VEC
        def _(v, tok_rstdb=tok_rstdb):
            v.wait_ge(sA.h, tok_rstdb)
            v.tensor_tensor(out=res1_3, in0=res1_3, in1=rep_bcast(mu_b, DC, TS),
                            op=ALU.subtract)
            v.tensor_tensor(out=res1_3, in0=res1_3,
                                   in1=rep_bcast(rstd_b, DC, TS), op=ALU.mult)
        tok_ln1 = sV.n

        @SCA
        def _(sc, ts=ts, tok_ln1=tok_ln1):
            sc.wait_ge(sV.h, tok_ln1)
            sLn1.inc(sc.dma_start(
                out=ln1_sp[:, :, ts * TS:(ts + 1) * TS],
                in_=res1.ap().rearrange("p (c w) -> p c w", c=DC)), 16)
        tok_ln1sp = sLn1.n

        abase_sq2 = sA.n
        mbase_sq2 = sMM.n
        @SCA
        def _(sc, tok_ln1=tok_ln1, mbase_sq2=mbase_sq2):
            for d in range(DC):
                if d >= 2:
                    sc.wait_ge(sMM.h, mbase_sq2 + d - 1)
                sc.activation(out=sqb[:, (d % 2) * TS:(d % 2 + 1) * TS],
                                     in_=res1[:, d * TS:(d + 1) * TS], func=AF.Square)
        tok_sq2 = sA.n

        @blk.tensor
        def _(t, abase_sq2=abase_sq2):
            for d in range(DC):
                t.wait_ge(sA.h, abase_sq2 + d + 1)
                sMM.inc(t.matmul(pstat[:, TS:2 * TS], onescb[:],
                                 sqb[:, (d % 2) * TS:(d % 2 + 1) * TS],
                                 start=(d == 0), stop=(d == DC - 1)))
        tok_ms = sMM.n

        @VEC
        def _(v, tok_ms=tok_ms):
            v.wait_ge(sMM.h, tok_ms)
            v.tensor_scalar_mul(out=scr2[:], in0=pstat[:, TS:2 * TS],
                                       scalar1=1.0 / D)
        tok_msr = sV.n

        @SCA
        def _(sc, tok_msr=tok_msr):
            sc.wait_ge(sV.h, tok_msr)
            sc.activation(out=scr2[:], in_=scr2[:], func=AF.Sqrt, bias=epsc[:])
        tok_sqr2 = sA.n

        @VEC
        def _(v, tok_sqr2=tok_sqr2):
            v.wait_ge(sA.h, tok_sqr2)
            v.reciprocal(out=scr2[:], in_=scr2[:])
        tok_rms = sV.n

        @blk.tensor
        def _(t, tok_rms=tok_rms):
            t.wait_ge(sV.h, tok_rms)
            sMM.inc(t.matmul(pbc[:], ones1[:], scr2[:], start=True, stop=True))
        tok_bc3 = sMM.n

        @SCA
        def _(sc, tok_bc3=tok_bc3):
            sc.wait_ge(sMM.h, tok_bc3)
            sc.copy(out=rstd_b[:], in_=pbc[:])
        tok_rmsb = sA.n

        @VEC
        def _(v, tok_rmsb=tok_rmsb):
            v.wait_ge(sA.h, tok_rmsb)
            for d in range(DC):
                v.tensor_scalar(out=hbuf[:, d * TS:(d + 1) * TS],
                                in0=res1[:, d * TS:(d + 1) * TS],
                                scalar1=wn_s[:, d:d + 1], scalar2=None, op0=ALU.mult)
            v.tensor_tensor(out=hbuf_3, in0=hbuf_3,
                                   in1=rep_bcast(rstd_b, DC, TS), op=ALU.mult)
        tok_h = sV.n

        abase_habs = sA.n
        vbase_m1 = sV.n

        @SCA
        def _(sc, tok_h=tok_h, vbase_m1=vbase_m1):
            sc.wait_ge(sV.h, tok_h)
            for d in range(DC):
                if d >= 2:
                    sc.wait_ge(sV.h, vbase_m1 + d - 1)
                sc.activation(out=paby[:, (d % 2) * TS:(d % 2 + 1) * TS],
                                     in_=hbuf[:, d * TS:(d + 1) * TS], func=AF.Abs)
        tok_habs = sA.n

        @VEC
        def _(v, abase_habs=abase_habs):
            for d in range(DC):
                v.wait_ge(sA.h, abase_habs + d + 1)
                if d == 0:
                    v.tensor_copy(out=m1t[:], in_=paby[:, 0:TS])
                else:
                    v.tensor_max(out=m1t[:], in0=m1t[:],
                                        in1=paby[:, (d % 2) * TS:(d % 2 + 1) * TS])
            for width in (64, 32):
                v.tensor_copy(out=trt[0:width, :], in_=m1t[width:2 * width, :])
                v.tensor_max(out=m1t[0:width, :], in0=m1t[0:width, :],
                             in1=trt[0:width, :])
            for width in (16, 8, 4, 2, 1):
                mask = [width + i if i < width else 0 for i in range(32)]
                v.stream_shuffle(out=trt[0:32, :], in_=m1t[0:32, :], mask=mask)
                v.tensor_max(out=m1t[0:width, :], in0=m1t[0:width, :],
                             in1=trt[0:width, :])
            v.tensor_scalar(out=scr1[:], in0=m1t[0:1, :], scalar1=EPS, scalar2=None,
                            op0=ALU.max)
            v.reciprocal(out=scr2[:], in_=scr1[:])
            v.tensor_scalar_mul(out=scr2[:], in0=scr2[:], scalar1=QB)
            v.wait_ge(sA.h, tok_w0done)
            v.tensor_scalar(out=scr1[:], in0=scr1[:], scalar1=mwc1[0:1, 0:1],
                                   scalar2=None, op0=ALU.mult)
        tok_s1 = sV.n

        @blk.tensor
        def _(t, tok_s1=tok_s1):
            t.wait_ge(sV.h, tok_s1)
            sMM.inc(t.matmul(pbc[:], ones1[:], scr2[:], start=True, stop=True))
        tok_bs1 = sMM.n

        @SCA
        def _(sc, tok_bs1=tok_bs1):
            sc.wait_ge(sMM.h, tok_bs1)
            sc.copy(out=mu_b[:], in_=pbc[:])
        tok_s1b = sA.n

        @blk.tensor
        def _(t, tok_s1b=tok_s1b):
            t.wait_ge(sA.h, tok_s1b)
            sMM.inc(t.matmul(pbc[:], ones1[:], scr1[:], start=True, stop=True))
        tok_bc1r = sMM.n

        @SCA
        def _(sc, ts=ts, tok_bc1r=tok_bc1r):
            sc.wait_ge(sMM.h, tok_bc1r)
            sc.copy(out=c1b[:, ts * TS:(ts + 1) * TS], in_=pbc[:])
        tok_c1b = sA.n

        @VEC
        def _(v, tok_s1b=tok_s1b):
            v.wait_ge(sA.h, tok_s1b)
            v.tensor_tensor(out=hbuf_3, in0=hbuf_3,
                                   in1=rep_bcast(mu_b, DC, TS), op=ALU.mult)
        tok_y1 = sV.n

        HW = DC * TS // 2
        for hh in range(2):
            @SCA
            def _(sc, hh=hh, tok_y1=tok_y1):
                sc.wait_ge(sV.h, tok_y1 if hh == 0 else sV.n)
                sc.activation(out=sgn_p[:], in_=hbuf[:, hh * HW:(hh + 1) * HW],
                              func=AF.Sign)
                sc.activation(out=paby[:], in_=hbuf[:, hh * HW:(hh + 1) * HW],
                                     func=AF.Abs)
            tok_absgn = sA.n

            @VEC
            def _(v, ts=ts, hh=hh, tok_absgn=tok_absgn, tok_ln1sp=tok_ln1sp):
                v.wait_ge(sA.h, tok_absgn)
                if hh == 0:
                    v.wait_ge(sLn1.h, tok_ln1sp)
                emit_peel(v, paby[:], res1[:, 0:HW])
                v.tensor_mul(out=res1[:, 0:HW], in0=paby[:], in1=sgn_p[:])
                v.tensor_tensor(
                    out=q1[:, ts * DC * TS + hh * HW:ts * DC * TS + (hh + 1) * HW],
                    in0=hbuf[:, hh * HW:(hh + 1) * HW], in1=res1[:, 0:HW],
                    op=ALU.subtract)
            tok_q1ts = sV.n
        prev_q1ts = tok_q1ts

    tok_q1 = sV.n
    tok_ln1_all = sLn1.n
    dbg_tensors.update(dict(q1=(q1, [128, NTS * DC * TS], BF16), c1b=(c1b, [128, S], F32)))
    if stage_n == 1:
        emit_taps(["q1", "c1b", "ybf", "sgv128", "mwc1", "mwc2"])
        ppst.close(); pst.close(); stPF.close()
        st.close()
        return nc, sorted(dbg_tensors.keys() & dbg)

    ppst.close()
    pst.close()

    f1L = contextlib.ExitStack()
    Mx = f1L.enter_context(nc.sbuf_tensor("Mx", [128, S], F32))
    scr3 = f1L.enter_context(nc.sbuf_tensor("scr3", [1, S], F32))
    f1 = contextlib.ExitStack()
    fsb = f1.enter_context
    wld = fsb(nc.sbuf_tensor("wld", [128, 3 * 2048], F32))
    qwv = fsb(nc.sbuf_tensor("qwv", [128, 2 * 2048], BF16))
    qwg = fsb(nc.sbuf_tensor("qwg", [128, 2 * 2048], BF16))
    wqt = fsb(nc.sbuf_tensor("wqt", [128, 2048], F32))
    wqs = fsb(nc.sbuf_tensor("wqs", [128, 2048], BF16))
    u_r = fsb(nc.sbuf_tensor("u_r", [128, 2 * S], F32))
    gc_t = fsb(nc.sbuf_tensor("gc_t", [128, TS], F32))
    sg_t = fsb(nc.sbuf_tensor("sg_t", [128, TS], F32))
    wo_r = fsb(nc.sbuf_tensor("wo_r", [128, 2 * 1024], F32))
    qo_t = fsb(nc.sbuf_tensor("qo_t", [128, 2 * 1024], BF16))
    woqt = fsb(nc.sbuf_tensor("woqt", [128, 1024], F32))
    woqs = fsb(nc.sbuf_tensor("woqs", [128, 1024], BF16))
    f1p = contextlib.ExitStack()
    pvg = f1p.enter_context(nc.psum_tensor("pvg", [128, 8 * TS], F32))

    @VEC
    def _(v):
        v.memset(Mx[:], 0.0)
    tok_mx0 = sV.n

    a0, v0, m0 = sA.n, sV.n, sMM.n
    TOK = {}
    a, v_, m = a0, v0, m0
    _wld_cnt = [0, 0, 0]
    for j in range(HC):
        for h, key in ((0, ("ldv", j)), (1, ("ldg", j))):
            slot = (2 * j + h) % 3
            _wld_cnt[slot] += 16
            TOK[key] = (rWLD.sems[slot].h, _wld_cnt[slot])
    _usp_cnt = [0, 0]
    for j in range(HC + 1):
        if j < HC:
            a += 2; TOK[("qv", j)] = a
            a += 2; TOK[("qg", j)] = a
        if j >= 1:
            jj = j - 1
            for ts in range(NTS):
                a += 1; TOK[("silu", jj, ts)] = a
                a += 1; TOK[("au", jj, ts)] = a
            _usp_cnt[jj % 2] += 16
            TOK[("usp", jj)] = (rUSP.sems[jj % 2].h, _usp_cnt[jj % 2])
        if j < HC:
            for h in (0, 1):
                a += 2; TOK[("woq", 2 * j + h)] = a
    for j in range(HC + 1):
        if j < HC:
            v_ += 1; TOK[("ternv", j)] = v_
            v_ += 1; TOK[("terng", j)] = v_
        if j >= 1:
            jj = j - 1
            for ts in range(NTS):
                v_ += 1; TOK[("gc", jj, ts)] = v_
                v_ += 1; TOK[("sg2", jj, ts)] = v_
                v_ += 1; TOK[("u", jj, ts)] = v_
                v_ += 1; TOK[("mx", jj, ts)] = v_
        if j < HC:
            v_ += 1; TOK[("wot", 2 * j)] = v_
            v_ += 1; TOK[("wot", 2 * j + 1)] = v_
    for j in range(HC):
        for ts in range(NTS):
            m += 1
            TOK[("mm", j, ts)] = m
    end_a, end_v, end_m = a, v_, m

    pP_v, pP_a = tok_q1, sA.n

    @blk.sync
    def _(s):
        for j in range(HC):
            if j < 2:
                s.wait_ge(sV.h, pP_v)
                s.wait_ge(sA.h, pP_a)
                s.wait_ge(sLn1.h, tok_ln1_all)
            else:
                s.wait_ge(sV.h, TOK[("terng", j - 2)])
            rWLD.inc(2 * j, s.dma_start(
                out=wld[:, (2 * j % 3) * 2048:(2 * j % 3 + 1) * 2048], in_=wgvt[j, 0]))
            if j >= 1:
                s.wait_ge(sV.h, TOK[("ternv", j - 1)])
            rWLD.inc(2 * j + 1, s.dma_start(
                out=wld[:, ((2 * j + 1) % 3) * 2048:((2 * j + 1) % 3 + 1) * 2048],
                in_=wgvt[j, 1]))

    @SCA
    def _(sc):
        sc.wait_ge(sA.h, tok_w0done)
        for j in range(HC + 1):
            if j < HC:
                sc.wait_ge(*TOK[("ldv", j)])
                if j >= 1:
                    sc.wait_ge(sV.h, TOK[("terng", j - 1)])
                sc.activation(out=wqt[:], in_=wld[:, (2 * j % 3) * 2048:(2 * j % 3 + 1) * 2048],
                              func=AF.Abs, scale=sgv128[:])
                sc.activation(out=wqs[:], in_=wld[:, (2 * j % 3) * 2048:(2 * j % 3 + 1) * 2048],
                                     func=AF.Sign, scale=sgv128[:])
                sc.wait_ge(*TOK[("ldg", j)])
                sc.wait_ge(sV.h, TOK[("ternv", j)])
                sc.activation(out=wqt[:], in_=wld[:, ((2 * j + 1) % 3) * 2048:((2 * j + 1) % 3 + 1) * 2048],
                              func=AF.Abs, scale=sgv128[:])
                sc.activation(out=wqs[:], in_=wld[:, ((2 * j + 1) % 3) * 2048:((2 * j + 1) % 3 + 1) * 2048],
                                     func=AF.Sign, scale=sgv128[:])
            if j >= 1:
                jj = j - 1
                r = jj % 2
                for ts in range(NTS):
                    sc.wait_ge(sV.h, TOK[("gc", jj, ts)])
                    sc.activation(out=sg_t[:], in_=gc_t[:], func=AF.Silu)
                    sc.wait_ge(sV.h, TOK[("u", jj, ts)])
                    sc.activation(out=gc_t[:],
                                         in_=u_r[:, r * S + ts * TS:r * S + (ts + 1) * TS],
                                         func=AF.Abs)
            for h in ((0, 1) if j < HC else ()):
                i = 2 * j + h
                wl = i % 2
                if i >= 2:
                    sc.wait_ge(sV.h, TOK[("wot", i - 2)])
                    dgp, jwp = divmod(i - 2, HC)
                    rQoS.inc(i - 2, sc.dma_start(
                        out=qo_sp[dgp, jwp // 4, :, (jwp % 4) * 1024:(jwp % 4 + 1) * 1024],
                        in_=qo_t[:, wl * 1024:(wl + 1) * 1024]))
                sc.wait_ge(rWO.sems[wl].h, 16 * (i // 2 + 1))
                if i >= 1:
                    sc.wait_ge(sV.h, TOK[("wot", i - 1)])
                sc.activation(out=woqt[:],
                              in_=wo_r[:, wl * 1024:(wl + 1) * 1024],
                              func=AF.Abs, scale=sout128[:])
                sc.activation(out=woqs[:],
                              in_=wo_r[:, wl * 1024:(wl + 1) * 1024],
                              func=AF.Sign, scale=sout128[:])
            if j == HC:
                for i in (2 * HC - 2, 2 * HC - 1):
                    wl = i % 2
                    sc.wait_ge(sV.h, TOK[("wot", i)])
                    dgp, jwp = divmod(i, HC)
                    rQoS.inc(i, sc.dma_start(
                        out=qo_sp[dgp, jwp // 4, :, (jwp % 4) * 1024:(jwp % 4 + 1) * 1024],
                        in_=qo_t[:, wl * 1024:(wl + 1) * 1024]))
    assert sA.n == end_a

    @VEC
    def _(v):
        for j in range(HC + 1):
            if j < HC:
                v.wait_ge(sA.h, TOK[("qv", j)])
                if j >= 2:
                    v.wait_ge(sMM.h, TOK[("mm", j - 2, NTS - 1)])
                v.scalar_tensor_tensor(
                    out=qwv[:, (j % 2) * 2048:(j % 2 + 1) * 2048], in0=wqt[:],
                    scalar=0.5, in1=wqs[:], op0=ALU.is_ge, op1=ALU.mult)
                v.wait_ge(sA.h, TOK[("qg", j)])
                v.scalar_tensor_tensor(
                    out=qwg[:, (j % 2) * 2048:(j % 2 + 1) * 2048], in0=wqt[:],
                    scalar=0.5, in1=wqs[:], op0=ALU.is_ge, op1=ALU.mult)
            if j >= 1:
                jj = j - 1
                r = jj % 2
                for ts in range(NTS):
                    q = (jj * NTS + ts) % 4
                    v.wait_ge(sMM.h, TOK[("mm", jj, ts)])
                    v.tensor_mul(out=gc_t[:],
                                 in0=pvg[:, (2 * q + 1) * TS:(2 * q + 2) * TS],
                                 in1=c1b[:, ts * TS:(ts + 1) * TS])
                    v.wait_ge(sA.h, TOK[("silu", jj, ts)])
                    v.tensor_mul(out=sg_t[:], in0=sg_t[:],
                                 in1=c1b[:, ts * TS:(ts + 1) * TS])
                    if jj >= 2:
                        v.wait_ge(*TOK[("usp", jj - 2)])
                        v.wait_ge(sA.h, TOK[("au", jj - 2, ts)])
                    v.tensor_mul(out=u_r[:, r * S + ts * TS:r * S + (ts + 1) * TS],
                                 in0=pvg[:, 2 * q * TS:(2 * q + 1) * TS],
                                 in1=sg_t[:])
                    v.wait_ge(sA.h, TOK[("au", jj, ts)])
                    v.tensor_max(out=Mx[:, ts * TS:(ts + 1) * TS],
                                 in0=Mx[:, ts * TS:(ts + 1) * TS], in1=gc_t[:])
            for h in ((0, 1) if j < HC else ()):
                i = 2 * j + h
                wl = i % 2
                v.wait_ge(sA.h, TOK[("woq", i)])
                if i >= 2:
                    v.wait_ge(rQoS.sems[wl].h, 16 * (i // 2))
                v.scalar_tensor_tensor(out=qo_t[:, wl * 1024:(wl + 1) * 1024],
                                       in0=woqt[:], scalar=0.5, in1=woqs[:],
                                       op0=ALU.is_ge, op1=ALU.mult)
    assert sV.n == end_v

    @blk.tensor
    def _(t):
        for j in range(HC):
            r = j % 2
            t.wait_ge(sV.h, TOK[("terng", j)])
            for ts in range(NTS):
                q = (j * NTS + ts) % 4
                if j >= 1:
                    t.wait_ge(sV.h, TOK[("u", j - 1, ts)])
                for k in range(DC):
                    t.matmul(pvg[:, 2 * q * TS:(2 * q + 1) * TS],
                             qwv[:, r * 2048 + k * 128:r * 2048 + (k + 1) * 128],
                             q1[:, (ts * DC + k) * TS:(ts * DC + k + 1) * TS],
                             start=(k == 0), stop=(k == DC - 1))
                for k in range(DC):
                    last = t.matmul(pvg[:, (2 * q + 1) * TS:(2 * q + 2) * TS],
                                    qwg[:, r * 2048 + k * 128:r * 2048 + (k + 1) * 128],
                                    q1[:, (ts * DC + k) * TS:(ts * DC + k + 1) * TS],
                                    start=(k == 0), stop=(k == DC - 1))
                sMM.inc(last)
    assert sMM.n == end_m

    @blk.gpsimd
    def _(g):
        for j in range(HC + 1):
            for h in ((0, 1) if j < HC else ()):
                i = 2 * j + h
                if i < 2:
                    g.wait_ge(sV.h, pP_v)
                    g.wait_ge(sA.h, pP_a)
                    g.wait_ge(sLn1.h, tok_ln1_all)
                else:
                    g.wait_ge(sA.h, TOK[("woq", i - 2)])
                dg, jw = divmod(i, HC)
                rWO.inc(i, g.dma_start(
                    out=wo_r[:, (i % 2) * 1024:(i % 2 + 1) * 1024],
                    in_=woutT[dg, jw]))
            if j >= 1:
                jj = j - 1
                g.wait_ge(sV.h, TOK[("mx", jj, NTS - 1)])
                rUSP.inc(jj, g.dma_start(out=u_sp[jj],
                                         in_=u_r[:, (jj % 2) * S:(jj % 2 + 1) * S]))

    tok_woq = end_a
    tok_qodone = end_v
    tok_qosp = [s.tok() for s in rQoS.sems]

    f1p.close()
    f1.close()
    f15 = contextlib.ExitStack()
    trt2 = f15.enter_context(nc.sbuf_tensor("trt2", [64, S], F32))
    f1p2 = contextlib.ExitStack()
    pbc2 = f1p2.enter_context(nc.psum_tensor("pbc2", [128, S], F32))

    @VEC
    def _(v):
        v.wait_ge(sV.h, end_v)
        for width in (64, 32):
            v.tensor_copy(out=trt2[0:width, :], in_=Mx[width:2 * width, :])
            v.tensor_max(out=Mx[0:width, :], in0=Mx[0:width, :], in1=trt2[0:width, :])
        for width in (16, 8, 4, 2, 1):
            mask = [width + i if i < width else 0 for i in range(32)]
            v.stream_shuffle(out=trt2[0:32, :], in_=Mx[0:32, :], mask=mask)
            v.tensor_max(out=Mx[0:width, :], in0=Mx[0:width, :], in1=trt2[0:width, :])
        v.tensor_scalar(out=scr3[:], in0=Mx[0:1, :], scalar1=EPS, scalar2=None,
                        op0=ALU.max)
        v.reciprocal(out=Mx[0:1, :], in_=scr3[:])
        v.tensor_scalar_mul(out=Mx[0:1, :], in0=Mx[0:1, :], scalar1=QB)
        v.wait_ge(sA.h, tok_w0done)
        v.tensor_scalar(out=scr3[:], in0=scr3[:], scalar1=mwc2[0:1, 0:1],
                               scalar2=None, op0=ALU.mult)
    tok_s2row = sV.n

    @blk.tensor
    def _(t):
        t.wait_ge(sV.h, tok_s2row)
        for ts in range(NTS):
            last = t.matmul(pbc2[:, ts * TS:(ts + 1) * TS], ones1[:],
                            Mx[0:1, ts * TS:(ts + 1) * TS], start=True, stop=True)
        sMM.inc(last)
    tok_bs2 = sMM.n

    @SCA
    def _(sc):
        sc.wait_ge(sMM.h, tok_bs2)
        sc.copy(out=s2b[:], in_=pbc2[:])
    tok_s2b = sA.n

    @blk.tensor
    def _(t):
        t.wait_ge(sA.h, tok_s2b)
        for ts in range(NTS):
            last = t.matmul(pbc2[:, ts * TS:(ts + 1) * TS], ones1[:],
                            scr3[0:1, ts * TS:(ts + 1) * TS], start=True, stop=True)
        sMM.inc(last)
    tok_bc2m = sMM.n

    @SCA
    def _(sc):
        sc.wait_ge(sMM.h, tok_bc2m)
        sc.copy(out=c2b[:], in_=pbc2[:])
    tok_c2b = sA.n

    dbg_tensors.update(dict(Mx=(Mx, [128, S], F32), s2b=(s2b, [128, S], F32),
                            c2b=(c2b, [128, S], F32)))
    if stage_n == 2:
        emit_taps(["Mx", "s2b", "c2b", "q1", "c1b"])
        f1p2.close(); f15.close(); f1L.close(); stPF.close()
        st.close()
        return nc, sorted(dbg_tensors.keys() & dbg)

    f1p2.close()
    f15.close()
    f1L.close()
    stPF.close()

    f2 = contextlib.ExitStack()
    f2b = f2.enter_context
    qo_r = f2b(nc.sbuf_tensor("qo_r", [128, 2 * 4096], BF16))
    q2_r = f2b(nc.sbuf_tensor("q2_r", [128, (HC // 4) * 2048], BF16))
    o_ts = f2b(nc.sbuf_tensor("o_ts", [128, DC * TS], F32))
    lbuf = f2b(nc.sbuf_tensor("lbuf", [128, DC * TS], F32))
    sqb2 = f2b(nc.sbuf_tensor("sqb2", [128, 2 * TS], BF16))
    mu2b = f2b(nc.sbuf_tensor("mu2b", [128, TS], F32))
    rstd2b = f2b(nc.sbuf_tensor("rstd2b", [128, TS], F32))
    ub2 = f2b(nc.sbuf_tensor("ub2", [128, 4 * TS], F32))
    zt2 = f2b(nc.sbuf_tensor("zt2", [128, 4 * TS], F32))
    mq2 = f2b(nc.sbuf_tensor("mq2", [128, 4 * TS], F32))
    q2c = f2b(nc.sbuf_tensor("q2c", [128, 4 * TS], BF16))
    f2p = contextlib.ExitStack()
    pout = f2p.enter_context(nc.psum_tensor("pout", [128, 8 * TS], F32))

    o_3 = o_ts.ap().rearrange("p (c w) -> p c w", c=DC)

    HYB_S = [-64.0, -32.0, -16.0, -8.0, -4.0, -2.0, -1.0, -0.5]

    BND = ([(sV.h, tok_s2row), (sV.h, tok_qodone), (sA.h, tok_woq),
            (sA.h, tok_c2b), (sLn1.h, tok_ln1_all)]
           + [TOK[("usp", HC - 2)], TOK[("usp", HC - 1)]] + list(tok_qosp))

    vfin = {}
    ub2_toks = {}
    st_tok = {}
    NP = HC // 4

    def emit_q2_slice(ts, mid_cb=None):
        vb, ab = sV.n, sA.n

        def soff(p):
            return 8 if (mid_cb is not None and p >= NP // 2) else 0
        for u in range(HC // 2):
            pp, lane = divmod(u, 2)
            vfin[(ts, u)] = vb + 20 * pp + 19 + lane
        sl = slice(ts * TS, (ts + 1) * TS)

        @blk.gpsimd
        def _(g, ts=ts):
            for u in range(HC // 2):
                lane = u % 2
                if ts == 0:
                    g.wait_ge(*TOK[("usp", 2 * u)])
                    g.wait_ge(*TOK[("usp", 2 * u + 1)])
                    if u < 2:
                        for h_, v_ in BND:
                            g.wait_ge(h_, v_)
                if ts > 0 or u >= 2:
                    pu = (ts, u - 2) if u >= 2 else (ts - 1, u + HC // 2 - 2)
                    g.wait_ge(sV.h, vfin[pu])
                for half in range(2):
                    tk = rUB.inc(u, g.dma_start(
                        out=ub2[:, (2 * lane + half) * TS:(2 * lane + half + 1) * TS],
                        in_=u_sp[2 * u + half, :, ts * TS:(ts + 1) * TS]))
                ub2_toks[(ts, u)] = tk

        @VEC
        def _(v, ts=ts, vb=vb, ab=ab, sl=sl):
            s2bc = s2b.ap()[:, sl].rearrange("p (c w) -> p c w", c=1) \
                                  .broadcast_to((128, 2, TS))
            for p in range(NP):
                for lane in range(2):
                    u = 2 * p + lane
                    v.wait_ge(*ub2_toks[(ts, u)])
                    ubv = ub2.ap()[:, lane * 2 * TS:(lane + 1) * 2 * TS] \
                             .rearrange("p (c w) -> p c w", c=2)
                    v.tensor_tensor(out=ubv, in0=ubv, in1=s2bc, op=ALU.mult)
                for k in range(8):
                    for lane in range(2):
                        c0, c1 = lane * 2 * TS, (lane + 1) * 2 * TS
                        v.wait_ge(sA.h, ab + 18 * p + soff(p) + 2 * k + lane + 1)
                        v.scalar_tensor_tensor(
                            out=zt2[:, c0:c1],
                            in0=mq2[:, c0:c1], scalar=HYB_S[k],
                            in1=(ub2 if k == 0 else zt2)[:, c0:c1],
                            op0=ALU.mult, op1=ALU.add)
                for lane in range(2):
                    c0, c1 = lane * 2 * TS, (lane + 1) * 2 * TS
                    v.scalar_tensor_tensor(
                        out=mq2[:, c0:c1], in0=zt2[:, c0:c1], scalar=-1.0,
                        in1=ub2[:, c0:c1], op0=ALU.mult, op1=ALU.add)
        assert sV.n == vb + 20 * NP

        @SCA
        def _(sc, ts=ts, vb=vb, ab=ab):
            for p in range(NP):
                if mid_cb is not None and p == NP // 2:
                    mid_cb(sc)
                for k in range(8):
                    for lane in range(2):
                        c0, c1 = lane * 2 * TS, (lane + 1) * 2 * TS
                        sc.wait_ge(sV.h, vb + 20 * p
                                   + (1 + lane if k == 0 else 2 * k + 1 + lane))
                        sc.activation(out=mq2[:, c0:c1],
                                      in_=(ub2 if k == 0 else zt2)[:, c0:c1],
                                      func=AF.Sign, bias=pbh[:])
                for lane in range(2):
                    u = 2 * p + lane
                    c0, c1 = lane * 2 * TS, (lane + 1) * 2 * TS
                    sc.wait_ge(sV.h, vfin[(ts, u)])
                    pu = ((ts, u - 2) if u >= 2
                          else (ts - 1, u + HC // 2 - 2) if ts > 0 else None)
                    if pu is not None:
                        sc.wait_ge(*st_tok[pu])
                    sc.activation(out=q2c[:, lane * 1024:(lane + 1) * 1024],
                                  in_=mq2[:, c0:c1], func=AF.Copy, bias=-0.5)
                    st_tok[(ts, u)] = rQ2S.inc(lane, sc.dma_start(
                        out=q2_sp[ts, u // 2, :, (u % 2) * 1024:(u % 2 + 1) * 1024],
                        in_=q2c[:, lane * 1024:(lane + 1) * 1024]))
        assert sA.n == ab + 18 * NP + (8 if mid_cb is not None else 0)

    prev_next_ok = None
    prev_res2 = None
    prev_mm_end = None
    prev_f2_m0 = None

    def emit_f2_loads_mms(ts, dg1_gate):
        nonlocal prev_mm_end, prev_f2_m0
        f2_m0 = sMM.n
        f2_toks = {}
        q2tok = {}

        @blk.sync
        def _(s, ts=ts, f2_m0=f2_m0, prev_mm_end=prev_mm_end,
              prev_f2_m0=prev_f2_m0):
            for h_, v_ in tok_qosp:
                s.wait_ge(h_, v_)
            if ts == 0:
                for h_, v_ in BND:
                    s.wait_ge(h_, v_)
            for cj in range(HC // 4):
                s.wait_ge(*st_tok[(ts, 2 * cj)])
                s.wait_ge(*st_tok[(ts, 2 * cj + 1)])
                if prev_f2_m0 is not None:
                    s.wait_ge(sMM.h, prev_f2_m0 + HC + 4 * cj + 4)
                q2tok[cj] = rQ2R.inc(cj, s.dma_start(
                    out=q2_r[:, cj * 2048:(cj + 1) * 2048],
                    in_=q2_sp[ts, cj]))
            for c in range(2 * (HC // 4)):
                dg, cj = divmod(c, HC // 4)
                if c >= 2:
                    s.wait_ge(sMM.h, f2_m0 + (c - 1) * 4)
                elif prev_mm_end is not None:
                    s.wait_ge(sMM.h, prev_mm_end)
                f2_toks[c] = rF2.inc(c, s.dma_start(
                    out=qo_r[:, (c % 2) * 4096:(c % 2 + 1) * 4096],
                    in_=qo_sp[dg, cj]))

        @blk.tensor
        def _(t, ts=ts, dg1_gate=dg1_gate, prev_next_ok=prev_next_ok):
            if prev_next_ok is not None:
                t.wait_ge(sA.h, prev_next_ok)
            else:
                t.wait_ge(sA.h, tok_c2b)
            for dg in range(2):
                if dg == 1:
                    t.wait_ge(sA.h, dg1_gate)
                for j in range(HC):
                    c = dg * (HC // 4) + j // 4
                    t.wait_ge(*f2_toks[c])
                    if dg == 0:
                        t.wait_ge(*q2tok[j // 4])
                    base = (c % 2) * 4096 + (j % 4) * 1024
                    rhs = q2_r[:, (j // 4) * 2048 + (j % 4) * TS:
                               (j // 4) * 2048 + (j % 4) * TS + TS]
                    for d in range(8):
                        last = t.matmul(pout[:, d * TS:(d + 1) * TS],
                                        qo_r[:, base + d * 128:base + (d + 1) * 128],
                                        rhs, start=(j == 0), stop=(j == HC - 1))
                    sMM.inc(last)
        prev_mm_end = sMM.n
        prev_f2_m0 = f2_m0
        assert sMM.n == f2_m0 + 2 * HC
        return f2_m0

    def mk_evdg0(ts, f2_m0):
        def cb(sc):
            sc.wait_ge(sMM.h, f2_m0 + HC)
            if ts > 0:
                sc.wait_ge(sOut.h, 16 * ts)
            for d in range(8):
                sc.copy(out=o_ts[:, d * TS:(d + 1) * TS],
                        in_=pout[:, d * TS:(d + 1) * TS])
        return cb

    def emit_f2_epilogue(ts, f2_m0):
        nonlocal prev_next_ok, prev_res2

        @SCA
        def _(sc, ts=ts, f2_m0=f2_m0):
            sc.wait_ge(sMM.h, f2_m0 + 2 * HC)
            for d in range(8):
                sc.copy(out=o_ts[:, (8 + d) * TS:(9 + d) * TS],
                        in_=pout[:, d * TS:(d + 1) * TS])
        tok_ev2 = sA.n

        @blk.sync
        def _(s, ts=ts, prev_res2=prev_res2):
            s.wait_ge(sLn1.h, tok_ln1_all)
            if ts == 0:
                for h_, v_ in BND:
                    s.wait_ge(h_, v_)
            if prev_res2 is not None:
                s.wait_ge(sV.h, prev_res2)
            sLb.inc(s.dma_start(
                out=lbuf.ap().rearrange("p (c w) -> p c w", c=DC),
                in_=ln1_sp[:, :, ts * TS:(ts + 1) * TS]), 16)
        tok_lb = sLb.n

        c2sl = c2b.ap()[:, ts * TS:(ts + 1) * TS].rearrange(
            "p (c w) -> p c w", c=1).broadcast_to((128, DC, TS))

        if "f2dbg" in dbg and ts == 0:
            draw = nc.dram_tensor("dbg_raw0", [128, DC * TS], F32, kind="ExternalOutput").ap()
            _ev, _lg0 = tok_ev2, sLG.n

            @blk.gpsimd
            def _(g):
                g.wait_ge(sA.h, _ev)
                sLG.inc(g.dma_start(out=draw, in_=o_ts[:]), 16)
            _dump1 = sLG.n

        @VEC
        def _(v, tok_ev2=tok_ev2, tok_lb=tok_lb, c2sl=c2sl, ts=ts):
            v.wait_ge(sA.h, tok_ev2)
            v.wait_ge(sLb.h, tok_lb)
            if "f2dbg" in dbg and ts == 0:
                v.wait_ge(sLG.h, _dump1)
            v.tensor_tensor(out=o_3, in0=o_3, in1=c2sl, op=ALU.mult)
            v.scalar_tensor_tensor(out=o_ts[:], in0=lbuf[:], scalar=2.0,
                                   in1=o_ts[:], op0=ALU.mult, op1=ALU.add)
        tok_res2 = sV.n
        prev_res2 = tok_res2
        if "f2dbg" in dbg and ts == 0:
            dres = nc.dram_tensor("dbg_res20", [128, DC * TS], F32, kind="ExternalOutput").ap()
            dlb = nc.dram_tensor("dbg_lbuf0", [128, DC * TS], F32, kind="ExternalOutput").ap()
            _r2 = tok_res2

            @blk.gpsimd
            def _(g):
                g.wait_ge(sV.h, _r2)
                sLG.inc(g.dma_start(out=dres, in_=o_ts[:]), 16)
                sLG.inc(g.dma_start(out=dlb, in_=lbuf[:]), 16)
            _dump2 = sLG.n

            @SCA
            def _(sc):
                sc.wait_ge(sLG.h, _dump2)


        abase_sq3 = sA.n
        mbase_sq3 = sMM.n
        @SCA
        def _(sc, tok_res2=tok_res2, mbase_sq3=mbase_sq3):
            sc.wait_ge(sV.h, tok_res2)
            for d in range(DC):
                if d >= 2:
                    sc.wait_ge(sMM.h, mbase_sq3 + DC + d - 1)
                sc.activation(out=sqb2[:, (d % 2) * TS:(d % 2 + 1) * TS],
                                     in_=o_ts[:, d * TS:(d + 1) * TS], func=AF.Square)
        tok_sq3 = sA.n

        @blk.tensor
        def _(t, tok_res2=tok_res2, abase_sq3=abase_sq3):
            t.wait_ge(sV.h, tok_res2)
            for d in range(DC):
                sMM.inc(t.matmul(pout[0:1, 0:TS], onesc[:], o_ts[:, d * TS:(d + 1) * TS],
                                 start=(d == 0), stop=(d == DC - 1)))
            for d in range(DC):
                t.wait_ge(sA.h, abase_sq3 + d + 1)
                sMM.inc(t.matmul(pout[0:1, TS:2 * TS], onescb[:],
                                 sqb2[:, (d % 2) * TS:(d % 2 + 1) * TS],
                                 start=(d == 0), stop=(d == DC - 1)))
        tok_st3 = sMM.n

        @VEC
        def _(v, tok_st3=tok_st3):
            v.wait_ge(sMM.h, tok_st3)
            v.tensor_scalar_mul(out=scr1[:], in0=pout[0:1, 0:TS], scalar1=1.0 / D)
            v.tensor_scalar_mul(out=scr2[:], in0=pout[0:1, TS:2 * TS], scalar1=1.0 / D)
            v.tensor_mul(out=scr4[:], in0=scr1[:], in1=scr1[:])
            v.tensor_sub(out=scr2[:], in0=scr2[:], in1=scr4[:])
        tok_var2 = sV.n

        @SCA
        def _(sc, tok_var2=tok_var2):
            sc.wait_ge(sV.h, tok_var2)
            sc.activation(out=scr2[:], in_=scr2[:], func=AF.Sqrt, bias=epsc[:])
        tok_sqr3 = sA.n

        @VEC
        def _(v, tok_sqr3=tok_sqr3):
            v.wait_ge(sA.h, tok_sqr3)
            v.reciprocal(out=scr2[:], in_=scr2[:])
        tok_rstd2 = sV.n

        @blk.tensor
        def _(t, tok_rstd2=tok_rstd2):
            t.wait_ge(sV.h, tok_rstd2)
            t.matmul(pout[:, 2 * TS:3 * TS], ones1[:], scr1[:], start=True, stop=True)
            sMM.inc(t.matmul(pout[:, 3 * TS:4 * TS], ones1[:], scr2[:], start=True,
                             stop=True))
        tok_bc6 = sMM.n

        @SCA
        def _(sc, tok_bc6=tok_bc6):
            sc.wait_ge(sMM.h, tok_bc6)
            sc.copy(out=mu2b[:], in_=pout[:, 2 * TS:3 * TS])
            sc.copy(out=rstd2b[:], in_=pout[:, 3 * TS:4 * TS])
        tok_bc6c = sA.n
        prev_next_ok = tok_bc6c

        @VEC
        def _(v, tok_bc6c=tok_bc6c):
            v.wait_ge(sA.h, tok_bc6c)
            v.tensor_tensor(out=o_3, in0=o_3, in1=rep_bcast(mu2b, DC, TS),
                            op=ALU.subtract)
            v.tensor_tensor(out=o_3, in0=o_3, in1=rep_bcast(rstd2b, DC, TS),
                                   op=ALU.mult)
        tok_out = sV.n

        @SCA
        def _(sc, ts=ts, tok_out=tok_out):
            sc.wait_ge(sV.h, tok_out)
            sOut.inc(sc.dma_start(out=outT[:, :, ts * TS:(ts + 1) * TS],
                                  in_=o_3), 16)

    emit_q2_slice(0)
    for ts in range(NTS):
        dg1_gate = sA.n + (18 * (NP // 2) + 8 if ts + 1 < NTS else 8)
        f2_m0 = emit_f2_loads_mms(ts, dg1_gate)
        if ts + 1 < NTS:
            emit_q2_slice(ts + 1, mid_cb=mk_evdg0(ts, f2_m0))
        else:
            @SCA
            def _(sc, ts=ts, f2_m0=f2_m0, cb=mk_evdg0(ts, f2_m0)):
                cb(sc)
        emit_f2_epilogue(ts, f2_m0)

    tok_allout = sOut.n

    @blk.sync
    def _(s):
        s.wait_ge(sOut.h, tok_allout)

    f2p.close()
    f2.close()
    st.close()
    return nc, sorted(dbg_tensors.keys() & dbg)


def _prep_shared(Wi, Wg, Wdt, Wb, Wc, Wo, A_log, b_dt, w_norm, w_gv, w_out):
    f = np.ascontiguousarray
    shared = {
        "wiT": f(Wi.T), "wgT": f(Wg.T), "wdtT": f(Wdt.T),
        "wbrep": f(np.broadcast_to(Wb[:, :, None], (KST, NSC, 128))),
        "wcrep": f(np.broadcast_to(Wc[:, :, None], (KST, NSC, 128))),
        "woT": f(Wo.T), "alog": f(A_log), "bdt": f(b_dt.reshape(NSC, 1)),
        "wnorm": f(w_norm.reshape(DC, 128).T),
    }
    W4 = w_gv.reshape(2, HC, 128, DC, 128)
    shared["wgvt"] = f(W4.transpose(1, 0, 4, 3, 2).reshape(HC, 2, 128, DC * 128))
    V5 = w_out.reshape(2, 8, 128, HC, 128)
    shared["woutT"] = f(V5.transpose(0, 3, 4, 1, 2).reshape(2, HC, 128, 8 * 128))
    return shared


def kernel(L0_attn_x, L0_attn_Wi, L0_attn_Wo, L0_attn_A_log, L0_attn_Wb,
           L0_attn_Wc, L0_attn_Wdt, L0_attn_b_dt, L0_attn_Wg,
           L0_ffn_w_norm, L0_ffn_w_gv, L0_ffn_w_out):
    x = np.asarray(L0_attn_x, dtype=np.float32)
    B = x.shape[0]
    shared = _prep_shared(
        np.asarray(L0_attn_Wi, np.float32), np.asarray(L0_attn_Wg, np.float32),
        np.asarray(L0_attn_Wdt, np.float32), np.asarray(L0_attn_Wb, np.float32),
        np.asarray(L0_attn_Wc, np.float32), np.asarray(L0_attn_Wo, np.float32),
        np.asarray(L0_attn_A_log, np.float32), np.asarray(L0_attn_b_dt, np.float32),
        np.asarray(L0_ffn_w_norm, np.float32), np.asarray(L0_ffn_w_gv, np.float32),
        np.asarray(L0_ffn_w_out, np.float32))
    nc, _ = build_program()
    in_maps = []
    for b in range(B):
        m = dict(shared)
        m["xT"] = np.ascontiguousarray(x[b].T)
        in_maps.append(m)
    res = run_bass_kernel_spmd(nc, in_maps, core_ids=list(range(B)))
    out = np.empty((B, S, D), np.float32)
    for b in range(B):
        o = np.asarray(res.results[b]["outT"])
        out[b] = o.transpose(1, 0, 2).reshape(D, S).T
    return out
